# version 47
# baseline (speedup 1.0000x reference)
import numpy as np

import concourse.bass as bass
import concourse.mybir as mybir
import concourse.bass_utils as _bu
from concourse.bass_utils import run_bass_kernel_spmd

if not getattr(_bu, "_ldw_patch", False):
    _orig_run_command = _bu.run_command

    def _run_command_ldw(argv, **kw):
        if isinstance(argv, list):
            argv = ["--enable-ldw-opt=true" if a == "--enable-ldw-opt=false" else a
                    for a in argv]
        return _orig_run_command(argv, **kw)

    _bu.run_command = _run_command_ldw
    _bu._ldw_patch = True

F32 = mybir.dt.float32
BF16 = mybir.dt.bfloat16
AF = mybir.ActivationFunctionType
ALU = mybir.AluOpType
AX = mybir.AxisListType

D = 1024
T = 4096
NCH = 8
HEADS = 8
KEXP = 8
R = 64
KR = 512
NKRCH = 4
TILE = 512
NT = T // TILE
SB = 512
NSB = T // SB
EPS_LN = 1e-5
EPS_RMS = 1e-6

_ENGINES = ("tensor", "vector", "scalar", "sync", "gpsimd")
_DMA_ENGINES = ("sync", "gpsimd")

_DEBUG_TAPS = []


class _Step:
    __slots__ = ("eng", "emit", "deps", "inc_needed", "sem_val", "tag", "semkey")

    def __init__(self, eng, emit, tag="", semkey=None):
        self.eng = eng
        self.emit = emit
        self.deps = []
        self.inc_needed = eng in _DMA_ENGINES or semkey == "sdma"
        self.sem_val = None
        self.tag = tag
        self.semkey = semkey or eng


class _Res:
    __slots__ = ("writers", "readers", "rsw", "name")

    def __init__(self, name=""):
        self.writers = []
        self.readers = []
        self.rsw = False
        self.name = name


class _Plan:
    def __init__(self):
        self.steps = []

    def add(self, eng, emit, reads=(), writes=(), tag="", extra_deps=(), semkey=None):
        st = _Step(eng, emit, tag, semkey)
        deps = []
        for r in reads:
            deps.extend(r.writers)
        for w in writes:
            deps.extend(w.writers)
            deps.extend(w.readers)
        deps.extend(extra_deps)
        seen = set()
        for d in deps:
            if d is st or id(d) in seen:
                continue
            seen.add(id(d))
            st.deps.append(d)
        for r in reads:
            r.readers.append(st)
            r.rsw = True
        for w in writes:
            if w.rsw:
                w.writers = [st]
                w.readers = []
                w.rsw = False
            else:
                w.writers.append(st)
        self.steps.append(st)
        return st

    def finalize(self):
        for st in self.steps:
            for d in st.deps:
                d.inc_needed = True
        counters = {e: 0 for e in _ENGINES}
        counters["sdma"] = 0
        for st in self.steps:
            if st.inc_needed:
                dma = st.eng in _DMA_ENGINES or st.semkey == "sdma"
                counters[st.semkey] += 16 if dma else 1
                st.sem_val = counters[st.semkey]

    def emit_engine(self, name, eng, sems):
        watermark = {}
        for st in self.steps:
            if st.eng != name:
                continue
            for d in st.deps:
                if watermark.get(d.semkey, 0) >= d.sem_val:
                    continue
                eng.wait_ge(sems[d.semkey], d.sem_val)
                watermark[d.semkey] = d.sem_val
            insts = st.emit(eng)
            if st.inc_needed:
                if not isinstance(insts, (list, tuple)):
                    insts = [insts]
                if name in _DMA_ENGINES:
                    for i in insts:
                        i.then_inc(sems[name], 16)
                    eng.wait_ge(sems[name], st.sem_val)
                    watermark[name] = st.sem_val
                elif st.semkey == "sdma":
                    for i in insts:
                        i.then_inc(sems["sdma"], 16)
                else:
                    insts[-1].then_inc(sems[name], 1)


def _pack_cm(m):
    r, c = m.shape
    return np.ascontiguousarray(
        m.reshape(r // 128, 128, c).transpose(1, 0, 2).reshape(128, -1))


def _host_prep(inp):
    g = {k: np.asarray(v, dtype=np.float32) for k, v in inp.items()}
    w = {}
    x = g["L0_attn_x"]
    xs = [_pack_cm(np.ascontiguousarray(x[b].T)) for b in range(8)]
    w["wqkvT"] = _pack_cm(np.ascontiguousarray(g["L0_attn_w_qkv"].T))
    w["woutT"] = _pack_cm(np.ascontiguousarray(g["L0_attn_w_out"].T))
    w["WQT"] = _pack_cm(np.ascontiguousarray(g["L0_ffn_W_Q"].T))
    w["WKT"] = _pack_cm(np.ascontiguousarray(g["L0_ffn_W_K"].T))
    w["WIN"] = _pack_cm(np.ascontiguousarray(g["L0_ffn_W_inv"]))
    w["R1T"] = _pack_cm(np.ascontiguousarray(g["L0_ffn_router_w1"].T))
    w["WOT"] = _pack_cm(np.ascontiguousarray(g["L0_ffn_W_O"].T))
    w["R2T"] = _pack_cm(np.ascontiguousarray(g["L0_ffn_router_w2"].T))
    w["Vf"] = _pack_cm(g["L0_ffn_V_fwd"].transpose(1, 0, 2).reshape(D, KR))
    w["Wf"] = _pack_cm(g["L0_ffn_W_fwd"].transpose(1, 0, 2).reshape(D, KR))
    w["Vix"] = _pack_cm(g["L0_ffn_W_inv_exp"].transpose(1, 0, 2).reshape(D, KR))
    w["Wiy"] = _pack_cm(g["L0_ffn_V_inv"].transpose(1, 0, 2).reshape(D, KR))
    w["Uf"] = _pack_cm(g["L0_ffn_U_fwd"].transpose(0, 2, 1).reshape(KR, D))
    w["Ui"] = _pack_cm(g["L0_ffn_U_inv"].transpose(0, 2, 1).reshape(KR, D))
    w["bfw"] = np.ascontiguousarray(g["L0_ffn_b_fwd"])
    w["biw"] = np.ascontiguousarray(g["L0_ffn_b_inv"])
    w["gamma"] = np.ascontiguousarray(g["L0_attn_norm_weight"].reshape(NCH, 128).T)
    w["r1b"] = np.ascontiguousarray(g["L0_ffn_router_b1"].reshape(NCH, 128).T)
    w["temp"] = np.ascontiguousarray(g["L0_attn_temperature"].reshape(8, 1))
    w["r2b"] = np.ascontiguousarray(g["L0_ffn_router_b2"].reshape(8, 1))
    w["eb"] = np.ascontiguousarray(g["L0_ffn_expert_bias"].reshape(8, 1))
    w["alpha"] = np.ascontiguousarray(g["L0_ffn_alpha_bi"].reshape(1, 1))
    w["ident"] = np.eye(128, dtype=np.float32)
    mask = np.full((128, 128), -30000.0, dtype=np.float32)
    for h in range(HEADS):
        mask[16 * h:16 * h + 16, 16 * h:16 * h + 16] = 0.0
    w["mask"] = mask
    nv = np.zeros((128, 32), dtype=np.float32)
    for blk in range(32):
        nv[:, blk] = 1.0 / (blk * 128 + np.arange(128) + 1.0)
    w["nvinv"] = nv
    esel = np.zeros((8, 512), dtype=np.float32)
    for c in range(4):
        for p in range(128):
            esel[2 * c + p // 64, c * 128 + p] = 1.0
    w["Esel"] = esel
    hsel = np.zeros((8, 128), dtype=np.float32)
    for p in range(128):
        hsel[p // 16, p] = 1.0
    w["hsel"] = hsel
    return xs, w


def build_program(taps=()):
    nc = bass.Bass("TRN2", target_bir_lowering=False)
    P = _Plan()

    dx = nc.dram_tensor("xT", [128, NCH * T], F32, kind="ExternalInput").ap()
    dout = nc.dram_tensor("outT", [128, NCH * T], F32, kind="ExternalOutput").ap()
    dw = {}
    for name, shape in [
        ("wqkvT", [128, NCH * 384]), ("woutT", [128, 1024]),
        ("WQT", [128, NCH * D]), ("WKT", [128, NCH * D]), ("WIN", [128, NCH * D]),
        ("R1T", [128, NCH * D]), ("WOT", [128, NCH * D]), ("R2T", [128, 64]),
        ("Vf", [128, NCH * KR]), ("Wf", [128, NCH * KR]),
        ("Vix", [128, NCH * KR]), ("Wiy", [128, NCH * KR]),
        ("Uf", [128, NKRCH * D]), ("Ui", [128, NKRCH * D]),
        ("bfw", [8, 1024]), ("biw", [8, 1024]),
        ("gamma", [128, 8]), ("r1b", [128, 8]),
        ("temp", [8, 1]), ("r2b", [8, 1]), ("eb", [8, 1]), ("alpha", [1, 1]),
        ("ident", [128, 128]), ("mask", [128, 128]), ("nvinv", [128, 32]),
        ("Esel", [8, 512]), ("hsel", [8, 128]),
    ]:
        dw[name] = nc.dram_tensor(name, shape, F32, kind="ExternalInput").ap()

    tap_drams = {}
    res = {}
    STREAM_W = ("WQT", "WKT", "R1T", "R1TI", "UFUI", "WOT")
    dscr = {}
    for nm in STREAM_W:
        dscr[nm] = nc.dram_tensor("scr_" + nm, [128, NCH * D], BF16).ap()
        res["scr_" + nm] = _Res("scr_" + nm)
    for nm in ("VIX2", "WIY2"):
        dscr[nm] = nc.dram_tensor("scr_" + nm, [128, NCH * KR], BF16).ap()
        res["scr_" + nm] = _Res("scr_" + nm)

    def sb(name, shape, dtype):
        h = nc.alloc_sbuf_tensor(name, shape, dtype)
        res[name] = _Res(name)
        return h.ap()

    WATb = sb("WATb", [128, 1024], BF16)
    vfm = sb("vfm", [128, T], BF16)
    wstr = [sb("wsA", [128, NCH * D], BF16), sb("wsB", [128, NCH * D], BF16),
            sb("wsC", [128, NCH * D], BF16)]
    wstr_res = [res["wsA"], res["wsB"], res["wsC"]]
    R2Tb = sb("R2Tb", [128, 64], BF16)
    Vfb = sb("Vfb", [128, NCH * KR], BF16)
    Wfb = sb("Wfb", [128, NCH * KR], BF16)
    Vixb = sb("Vixb", [128, NCH * KR], BF16)
    Wiyb = sb("Wiyb", [128, NCH * KR], BF16)
    bfb = sb("bfb", [8, 1024], BF16)
    bib = sb("bib", [8, 1024], BF16)
    identb = sb("identb", [128, 128], BF16)
    nvinvS = sb("nvinvS", [128, 32], F32)
    EselB = sb("EselB", [8, 512], BF16)
    gammaS = sb("gammaS", [128, 8], F32)
    r1bS = sb("r1bS", [128, 8], F32)
    tempS = sb("tempS", [8, 1], F32)
    r2bS = sb("r2bS", [8, 1], F32)
    ebS = sb("ebS", [8, 1], F32)
    alphaS = sb("alphaS", [1, 1], F32)
    onesB = sb("onesB", [128, 1], BF16)
    negidB = sb("negidB", [128, 1], BF16)
    invdB = sb("invdB", [128, 1], BF16)
    ones1F = sb("ones1F", [1, 128], F32)
    onesColF = sb("onesColF", [128, 1], F32)
    gSqkvB = sb("gSqkvB", [1, 384], BF16)
    bias2 = sb("bias2", [8, 1], F32)
    ethb = sb("ethb", [128, 1], F32)
    alphab = sb("alphab", [128, 1], F32)
    etmpb = sb("etmpb", [8, 1], BF16)
    negmB = sb("negmB", [1, TILE], BF16)
    negmB1 = sb("negmB1", [1, TILE], BF16)
    maskS = sb("maskS", [128, 128], F32)
    hselB = sb("hselB", [8, 128], BF16)
    rkS = sb("rkS", [128, 1], F32)
    st_a = sb("st_a", [1, TILE], F32)
    st_b = sb("st_b", [1, TILE], F32)
    st_c = sb("st_c", [1, TILE], F32)
    sqA = sb("sqA", [128, 1], F32)
    nrmA = sb("nrmA", [128, 1], F32)
    sclA = sb("sclA", [128, 1], F32)
    carryF = sb("carryF", [128, 4], F32)
    carryI = sb("carryI", [128, 4], F32)
    smx1 = sb("smx1", [128, 1], F32)
    smx2 = sb("smx2", [128, 1], F32)
    smxM = sb("smxM", [128, 4], F32)
    sumM = sb("sumM", [128, 4], F32)
    smxQ = sb("smxQ", [128, 4], F32)
    wtsBF = sb("wtsBF", [128, 32], BF16)
    wtsBI = sb("wtsBI", [128, 32], BF16)
    out_res = _Res("OUT")
    res["OUT"] = out_res

    SBUF_TOP = 229344
    arena_base = nc._sbuf_addr_for_side("left")
    arena_base = (arena_base + 31) // 32 * 32

    class _Arena:
        def __init__(self, base):
            self.off = base

        def alloc(self, name, shape, dtype):
            per_part = int(np.prod(shape[1:])) * mybir.dt.size(dtype)
            off = (self.off + 31) // 32 * 32
            assert off + per_part <= SBUF_TOP, (
                f"arena overflow: {name} at {off}+{per_part} > {SBUF_TOP}")
            h = nc.alloc_sbuf_tensor_at(name, shape, dtype, offset=off)
            self.off = off + per_part
            res[name] = _Res(name)
            return h.ap()

    banks = []
    for i in range(8):
        h = nc.alloc_psum_tensor(f"pb{i}", [128, 512], F32)
        banks.append({"f32": h.ap(), "b16": h.ap().bitcast(BF16),
                      "res": _Res(f"bank{i}"), "i": i})
    bank_rr = [0]
    pinned = set()

    def take_bank(pin=False):
        for _ in range(8):
            b = banks[bank_rr[0] % 8]
            bank_rr[0] += 1
            if b["i"] not in pinned:
                if pin:
                    pinned.add(b["i"])
                return b
        raise RuntimeError("all psum banks pinned")

    def unpin(b):
        pinned.discard(b["i"])

    def dma(queue, out_ap, in_ap, reads=(), writes=(), tag=""):
        return P.add(queue, lambda e, o=out_ap, i=in_ap: e.dma_start(out=o, in_=i),
                     reads=reads, writes=writes, tag=tag,
                     semkey="sdma" if queue == "scalar" else None)

    def tap(name, ap_, rd):
        if name not in taps:
            return
        d = nc.dram_tensor("tap_" + name, list(ap_.shape), ap_.dtype,
                           kind="ExternalOutput").ap()
        tap_drams[name] = d
        dma("gpsimd", d, ap_, reads=[rd], tag="tap_" + name)

    def mm_group(ops, bank, reads, tag=""):
        n = len(ops)

        def em(eng, ops=ops, n=n):
            insts = []
            for i, (o, l, r) in enumerate(ops):
                insts.append(eng.matmul(o, l, r, start=(i == 0), stop=(i == n - 1)))
            return insts
        return P.add("tensor", em, reads=reads, writes=[bank["res"]], tag=tag)

    def ve(emit, reads=(), writes=(), tag=""):
        return P.add("vector", emit, reads=reads, writes=writes, tag=tag)

    def ae(emit, reads=(), writes=(), tag=""):
        return P.add("scalar", emit, reads=reads, writes=writes, tag=tag)

    def te(emit, reads=(), writes=(), tag=""):
        return P.add("tensor", emit, reads=reads, writes=writes, tag=tag)

    def ge(emit, reads=(), writes=(), tag=""):
        return P.add("gpsimd", emit, reads=reads, writes=writes, tag=tag)

    def rch(ap_, c, w=SB):
        return ap_.rearrange("p (c t) -> p c t", c=ap_.shape[1] // w)[:, c, :]

    arena = _Arena(arena_base)
    xt0 = arena.alloc("xt0", [128, NCH * TILE], F32)
    xt1 = arena.alloc("xt1", [128, NCH * TILE], F32)
    qfm = arena.alloc("qfm", [128, T], BF16)
    kfm = arena.alloc("kfm", [128, T], BF16)
    xb16 = arena.alloc("xb16", [128, NCH * TILE], BF16)
    xb16b = arena.alloc("xb16b", [128, NCH * TILE], BF16)
    xb16c = arena.alloc("xb16c", [128, NCH * TILE], BF16)
    xsq16 = arena.alloc("xsq16", [128, NCH * TILE], BF16)
    ssbA = arena.alloc("ssbA", [128, TILE], F32)
    ssbA1 = arena.alloc("ssbA1", [128, TILE], F32)
    qtm = arena.alloc("qtm", [128, 128], BF16)
    ktm = arena.alloc("ktm", [128, 128], BF16)
    qtm2 = arena.alloc("qtm2", [128, 128], BF16)
    ktm2 = arena.alloc("ktm2", [128, 128], BF16)
    simsb = arena.alloc("simsb", [128, 128], F32)
    Ab = arena.alloc("Ab", [128, 128], BF16)
    identf = arena.alloc("identf", [128, 128], F32)
    nsq = arena.alloc("nsq", [128, TILE], BF16)
    sqQa = arena.alloc("sqQa", [128, 1], F32)
    sqKa = arena.alloc("sqKa", [128, 1], F32)
    nred = arena.alloc("nred", [128, 1], F32)
    st_a2 = arena.alloc("st_a2", [1, TILE], F32)
    st_b2 = arena.alloc("st_b2", [1, TILE], F32)
    st_c2 = arena.alloc("st_c2", [1, TILE], F32)
    wqkvTb = arena.alloc("wqkvTb", [128, NCH * 384], BF16)
    woutTb = arena.alloc("woutTb", [128, 1024], BF16)


    def wload(dst_ap, dname, rkey):
        dma("gpsimd", dst_ap, dw[dname], writes=[res[rkey]], tag="ld_" + dname)

    dma("gpsimd", wstr[0], dw["WIN"], writes=[wstr_res[0]], tag="prep_ld_WIN")
    dma("gpsimd", qfm, dw["Vix"], writes=[res["qfm"]], tag="prep_ld_Vix")
    dma("gpsimd", wstr[1], dw["R1T"], writes=[wstr_res[1]], tag="prep_ld_R1T")
    wload(wqkvTb, "wqkvT", "wqkvTb")
    wload(identb, "ident", "identb")
    wload(hselB, "hsel", "hselB")
    wload(woutTb, "woutT", "woutTb")
    dma("gpsimd", maskS, dw["mask"], writes=[res["maskS"]])
    dma("gpsimd", identf, dw["ident"], writes=[res["identf"]])
    dma("gpsimd", gammaS, dw["gamma"], writes=[res["gammaS"]])
    dma("gpsimd", tempS, dw["temp"], writes=[res["tempS"]])

    for ap_, val, rk in ((sqQa, 0.0, "sqQa"), (sqKa, 0.0, "sqKa"),
                         (onesB, 1.0, "onesB"), (negidB, -1.0 / 1024.0, "negidB"),
                         (invdB, 1.0 / 1024.0, "invdB"), (ones1F, 1.0, "ones1F"),
                         (onesColF, 1.0, "onesColF"), (carryF, 0.0, "carryF"),
                         (carryI, 0.0, "carryI")):
        ve(lambda e, ap_=ap_, val=val: e.memset(ap_, val),
           writes=[res[rk]], tag="memset_" + rk)

    xts = [xt0, xt1]
    xts_res = [res["xt0"], res["xt1"]]
    xb16s = [xb16, xb16b, xb16c]
    xb16s_res = [res["xb16"], res["xb16b"], res["xb16c"]]
    ssbAs = [ssbA, ssbA1]
    ssbAs_res = [res["ssbA"], res["ssbA1"]]
    negmBs = [negmB, negmB1]
    negmBs_res = [res["negmB"], res["negmB1"]]
    qkv_dst = [(qfm, res["qfm"]), (kfm, res["kfm"]), (vfm, res["vfm"])]

    A_banks = {}

    def emit_A_head_dma(t):
        xt = xts[t % 2]
        xt_res = xts_res[t % 2]
        xb = xb16s[t % 3]
        xb_res = xb16s_res[t % 3]
        dma("sync", xt.rearrange("p (c t) -> p c t", c=NCH),
            dx.rearrange("p (c t) -> p c t", c=NCH)[:, :, t * TILE:(t + 1) * TILE],
            writes=[xt_res], tag=f"A_x{t}")
        ae(lambda e, xt=xt, xb=xb: e.copy(xb, xt),
           reads=[xt_res], writes=[xb_res], tag=f"A_cast{t}")

    def emit_A_head_mm(t):
        xb = xb16s[t % 3]
        xb_res = xb16s_res[t % 3]
        bnm = take_bank(pin=True)
        mm_group([(bnm["f32"][0:1, 0:TILE], negidB, rch(xb, c, TILE))
                  for c in range(NCH)], bnm,
                 reads=[xb_res, res["negidB"]], tag=f"A_negm{t}")
        ae(lambda e, xb=xb: e.activation(xsq16, xb, AF.Square),
           reads=[xb_res], writes=[res["xsq16"]], tag=f"A_sq{t}")
        mm_group([(bnm["f32"][32:33, 0:TILE], invdB, rch(xsq16, c, TILE))
                  for c in range(NCH)], bnm,
                 reads=[res["xsq16"], res["invdB"]], tag=f"A_msq{t}")
        A_banks[t] = bnm

    def emit_A_head(t):
        emit_A_head_dma(t)
        emit_A_head_mm(t)


    emit_A_head(0)
    emit_A_head(1)

    def prep_vw(nm, src_ap, src_res, stage_ap, stage_res):
        for a in range(NCH):
            bk_ = take_bank()
            mm_group([(bk_["f32"][:, 0:KR],
                       wstr[0][:, c * D + a * 128:c * D + (a + 1) * 128],
                       src_ap.rearrange("p (c t) -> p c t", c=NCH)[:, c, :])
                      for c in range(NCH)], bk_,
                     reads=[wstr_res[0], src_res], tag=f"prep_{nm}_{a}")
            ae(lambda e, b=bk_, a=a, stage_ap=stage_ap:
               e.copy(stage_ap[:, a * KR:(a + 1) * KR], b["f32"][:, 0:KR]),
               reads=[bk_["res"]], writes=[stage_res], tag=f"prep_{nm}ev_{a}")

    dma("gpsimd", xb16c, dw["Wiy"], writes=[res["xb16c"]], tag="prep_ld_Wiy")
    prep_vw("VIX2", qfm, res["qfm"], kfm, res["kfm"])
    xt1b = xt1.bitcast(BF16)
    for a in range(NCH):
        for hf in range(2):
            bk_ = take_bank()
            mm_group([(bk_["f32"][:, 0:512],
                       wstr[0][:, c * D + a * 128:c * D + (a + 1) * 128],
                       wstr[1][:, c * D + hf * 512:c * D + (hf + 1) * 512])
                      for c in range(NCH)], bk_,
                     reads=[wstr_res[0], wstr_res[1]], tag=f"prep_R1TI_{a}_{hf}")
            ae(lambda e, b=bk_, a=a, hf=hf:
               e.copy(xt1b[:, a * D + hf * 512:a * D + (hf + 1) * 512],
                      b["f32"][:, 0:512]),
               reads=[bk_["res"]], writes=[res["xt1"]], tag=f"prep_R1TIev_{a}_{hf}")
    prep_vw("WIY2", xb16c, res["xb16c"], qfm, res["qfm"])
    dma("gpsimd", dscr["R1T"], wstr[1], reads=[wstr_res[1]],
        writes=[res["scr_R1T"]], tag="precast_st_R1T")
    dma("gpsimd", dscr["R1TI"], xt1b, reads=[res["xt1"]],
        writes=[res["scr_R1TI"]], tag="prep_st_R1TI")
    dma("gpsimd", dscr["VIX2"], kfm, reads=[res["kfm"]], writes=[res["scr_VIX2"]],
        tag="prep_st_VIX2")
    dma("gpsimd", dscr["WIY2"], qfm, reads=[res["qfm"]], writes=[res["scr_WIY2"]],
        tag="prep_st_WIY2")

    for c in range(NCH):
        ve(lambda e, c=c: e.tensor_scalar_mul(wqkvTb[:, c * 384:(c + 1) * 384],
                                              wqkvTb[:, c * 384:(c + 1) * 384],
                                              gammaS[:, c:c + 1]),
           reads=[res["gammaS"]], writes=[res["wqkvTb"]], tag=f"gt_scale{c}")

    bk = take_bank()
    mm_group([(bk["f32"][0:1, 0:384], onesB, wqkvTb[:, c * 384:(c + 1) * 384])
              for c in range(NCH)], bk,
             reads=[res["wqkvTb"], res["onesB"]], tag="gSqkv_mm")
    ae(lambda e, b=bk: e.copy(gSqkvB, b["f32"][0:1, 0:384]),
       reads=[bk["res"]], writes=[res["gSqkvB"]], tag="gSqkv_cp")

    ae(lambda e: e.activation(tempS, tempS, AF.Exp), writes=[res["tempS"]], tag="exp_temp")
    ae(lambda e: e.copy(etmpb, tempS), reads=[res["tempS"]], writes=[res["etmpb"]])
    bk = take_bank()
    mm_group([(bk["f32"][0:128, 0:1], hselB, etmpb)], bk,
             reads=[res["hselB"], res["etmpb"]], tag="ethb_mm")
    ve(lambda e, b=bk: e.tensor_copy(ethb, b["f32"][0:128, 0:1]),
       reads=[bk["res"]], writes=[res["ethb"]])


    def emit_A_tail(t):
        bnm = A_banks.pop(t)
        bms = bnm
        sA, sB, sC = ((st_a, st_b, st_c) if t % 2 == 0 else (st_a2, st_b2, st_c2))
        rA, rB, rC = (("st_a", "st_b", "st_c") if t % 2 == 0
                      else ("st_a2", "st_b2", "st_c2"))
        ae(lambda e, b=bnm, sA=sA: e.activation(sA, b["f32"][0:1, 0:TILE], AF.Square),
           reads=[bnm["res"]], writes=[res[rA]], tag=f"A_m2{t}")
        ve(lambda e, b=bms, sA=sA, sB=sB: e.tensor_sub(sB, b["f32"][32:33, 0:TILE], sA),
           reads=[bms["res"], res[rA]], writes=[res[rB]], tag=f"A_var{t}")
        ve(lambda e, sA=sA, sB=sB: e.tensor_scalar(sA, sB, 1.0 + EPS_RMS,
                                                   EPS_LN * EPS_RMS,
                                                   ALU.mult, ALU.add),
           reads=[res[rB]], writes=[res[rA]])
        ae(lambda e, sA=sA: e.activation(sA, sA, AF.Ln), writes=[res[rA]])
        ae(lambda e, sA=sA, sC=sC: e.activation(sC, sA, AF.Exp, scale=-0.5),
           reads=[res[rA]], writes=[res[rC]], tag=f"A_s{t}")
        ae(lambda e, b=bnm, t=t: e.copy(negmBs[t % 2], b["f32"][0:1, 0:TILE]),
           reads=[bnm["res"]], writes=[negmBs_res[t % 2]], tag=f"A_negmB{t}")
        bsc = take_bank()
        mm_group([(bsc["f32"][:, 0:TILE], ones1F, sC)], bsc,
                 reads=[res[rC], res["ones1F"]], tag=f"A_sbc{t}")
        ve(lambda e, b=bsc, t=t: e.tensor_copy(ssbAs[t % 2], b["f32"][:, 0:TILE]),
           reads=[bsc["res"]], writes=[ssbAs_res[t % 2]], tag=f"A_ssb{t}")
        unpin(bnm)

    bG = take_bank(pin=True)
    qtms = [qtm, qtm2]
    ktms = [ktm, ktm2]
    qtms_res = [res["qtm"], res["qtm2"]]
    ktms_res = [res["ktm"], res["ktm2"]]

    def emit_A_gram(t):
        for b_ in range(TILE // 128):
            i = t * (TILE // 128) + b_
            bt1 = take_bank()
            te(lambda e, b=bt1, i=i: e.transpose(b["b16"][:, 0:128],
                                                 qfm[:, i * 128:(i + 1) * 128], identb),
               reads=[res["qfm"], res["identb"]], writes=[bt1["res"]], tag=f"G_tq{i}")
            ve(lambda e, b=bt1, i=i: e.tensor_copy(qtms[i % 2], b["b16"][:, 0:128]),
               reads=[bt1["res"]], writes=[qtms_res[i % 2]])
            bt2 = take_bank()
            te(lambda e, b=bt2, i=i: e.transpose(b["b16"][:, 0:128],
                                                 kfm[:, i * 128:(i + 1) * 128], identb),
               reads=[res["kfm"], res["identb"]], writes=[bt2["res"]], tag=f"G_tk{i}")
            ve(lambda e, b=bt2, i=i: e.tensor_copy(ktms[i % 2], b["b16"][:, 0:128]),
               reads=[bt2["res"]], writes=[ktms_res[i % 2]])
            te(lambda e, b=bG, i=i: e.matmul(b["f32"][:, 0:128], qtms[i % 2], ktms[i % 2],
                                             start=(i == 0), stop=(i == 31)),
               reads=[qtms_res[i % 2], ktms_res[i % 2]], writes=[bG["res"]],
               tag=f"G_mm{i}")

    def emit_A_qkv(t):
        xb = xb16s[t % 3]
        xb_res = xb16s_res[t % 3]
        for oc in range(3):
            bq = take_bank()
            ops = [(bq["f32"][:, 0:TILE],
                    wqkvTb[:, c * 384 + oc * 128: c * 384 + (oc + 1) * 128],
                    rch(xb, c, TILE)) for c in range(NCH)]
            ops.append((bq["f32"][:, 0:TILE], gSqkvB[:, oc * 128:(oc + 1) * 128],
                        negmBs[t % 2]))
            mm_group(ops, bq, reads=[res["wqkvTb"], xb_res, res["gSqkvB"],
                                     negmBs_res[t % 2]], tag=f"A_qkv{t}_{oc}")
            dst, dres = qkv_dst[oc]
            ve(lambda e, b=bq, dst=dst, t=t:
               e.tensor_mul(dst[:, t * TILE:(t + 1) * TILE], b["f32"][:, 0:TILE],
                            ssbAs[t % 2]),
               reads=[bq["res"], ssbAs_res[t % 2]], writes=[dres], tag=f"A_ev{t}_{oc}")
        for (fm2, fres2, acc, accr) in ((qfm, res["qfm"], sqQa, "sqQa"),
                                        (kfm, res["kfm"], sqKa, "sqKa")):
            ae(lambda e, fm2=fm2, t=t: e.activation(nsq, fm2[:, t * TILE:(t + 1) * TILE],
                                                    AF.Square),
               reads=[fres2], writes=[res["nsq"]], tag=f"A_nsq{t}")
            ve(lambda e: e.reduce_sum(nred, nsq, AX.X),
               reads=[res["nsq"]], writes=[res["nred"]], tag=f"A_nred{t}")
            ve(lambda e, acc=acc: e.tensor_add(acc, acc, nred),
               reads=[res["nred"]], writes=[res[accr]], tag=f"A_nacc{t}")

    emit_A_tail(0)
    for t in range(NT):
        if t + 2 < NT:
            emit_A_head(t + 2)
        if t + 1 < NT:
            emit_A_tail(t + 1)
        emit_A_qkv(t)
        emit_A_gram(t)

    tap("qfm", qfm, res["qfm"])
    tap("kfm", kfm, res["kfm"])
    tap("vfm", vfm, res["vfm"])

    for (acc, accr, which) in ((sqQa, "sqQa", "q"), (sqKa, "sqKa", "k")):
        ae(lambda e, acc=acc: e.activation(nrmA, acc, AF.Sqrt),
           reads=[res[accr]], writes=[res["nrmA"]])
        ve(lambda e: e.tensor_scalar_max(nrmA, nrmA, 1e-12), writes=[res["nrmA"]])
        ve(lambda e: e.reciprocal(nrmA, nrmA), writes=[res["nrmA"]])
        if which == "q":
            ve(lambda e: e.tensor_mul(sclA, nrmA, ethb),
               reads=[res["nrmA"], res["ethb"]], writes=[res["sclA"]], tag="N_qscale")
        else:
            ve(lambda e: e.tensor_copy(rkS, nrmA),
               reads=[res["nrmA"]], writes=[res["rkS"]], tag="N_kscale")

    brt = take_bank()
    te(lambda e, b=brt: e.transpose(b["f32"][0:1, 0:128], rkS, identf),
       reads=[res["rkS"], res["identf"]], writes=[brt["res"]], tag="S_rkT")
    ae(lambda e, b=brt: e.copy(st_a[:, 0:128], b["f32"][0:1, 0:128]),
       reads=[brt["res"]], writes=[res["st_a"]], tag="S_rkrow")
    bcb = take_bank()
    mm_group([(bcb["f32"][:, 0:128], ones1F, st_a[:, 0:128])], bcb,
             reads=[res["st_a"], res["ones1F"]], tag="S_colmm")
    ae(lambda e, b=bcb: e.copy(ssbA[:, 0:128], b["f32"][:, 0:128]),
       reads=[bcb["res"]], writes=[res["ssbA"]], tag="S_colcp")
    ve(lambda e, b=bG: e.scalar_tensor_tensor(simsb, b["f32"][:, 0:128], sclA,
                                              ssbA[:, 0:128],
                                              ALU.mult, ALU.mult),
       reads=[bG["res"], res["sclA"], res["ssbA"]], writes=[res["simsb"]],
       tag="S_scale")
    unpin(bG)
    ve(lambda e: e.tensor_add(simsb, simsb, maskS),
       reads=[res["maskS"]], writes=[res["simsb"]], tag="S_mask")
    ve(lambda e: e.reduce_max(smx1, simsb, AX.X, negate=True),
       reads=[res["simsb"]], writes=[res["smx1"]])
    ae(lambda e: e.activation(simsb, simsb, AF.Exp, bias=smx1),
       reads=[res["smx1"]], writes=[res["simsb"]], tag="S_exp")
    ve(lambda e: e.reduce_sum(smx2, simsb, AX.X),
       reads=[res["simsb"]], writes=[res["smx2"]])
    ve(lambda e: e.reciprocal(smx2, smx2), writes=[res["smx2"]])
    ve(lambda e: e.tensor_scalar_mul(Ab, simsb, smx2),
       reads=[res["simsb"], res["smx2"]], writes=[res["Ab"]], tag="S_A")
    tap("Ab", Ab, res["Ab"])
    for half in range(2):
        bw = take_bank()
        mm_group([(bw["f32"][:, 0:512], Ab, woutTb[:, half * 512:(half + 1) * 512])],
                 bw, reads=[res["Ab"], res["woutTb"]], tag=f"WAT{half}")
        ve(lambda e, b=bw, half=half:
           e.tensor_copy(WATb[:, half * 512:(half + 1) * 512], b["f32"][:, 0:512]),
           reads=[bw["res"]], writes=[res["WATb"]], tag=f"WATev{half}")
    tap("WATb", WATb, res["WATb"])

    for nm, dst, rkey in [("R2T", R2Tb, "R2Tb"),
                          ("Vf", Vfb, "Vfb"), ("Wf", Wfb, "Wfb"),
                          ("bfw", bfb, "bfb"), ("biw", bib, "bib"),
                          ("Esel", EselB, "EselB")]:
        wload(dst, nm, rkey)
    dma("gpsimd", Vixb, dscr["VIX2"], reads=[res["scr_VIX2"]],
        writes=[res["Vixb"]], tag="ld_VIX2")
    dma("gpsimd", Wiyb, dscr["WIY2"], reads=[res["scr_WIY2"]],
        writes=[res["Wiyb"]], tag="ld_WIY2")
    dma("gpsimd", r1bS, dw["r1b"], writes=[res["r1bS"]])
    dma("gpsimd", r2bS, dw["r2b"], writes=[res["r2bS"]])
    dma("gpsimd", ebS, dw["eb"], writes=[res["ebS"]])
    dma("gpsimd", alphaS, dw["alpha"], writes=[res["alphaS"]])
    dma("gpsimd", nvinvS, dw["nvinv"], writes=[res["nvinvS"]])

    ve(lambda e: e.tensor_add(bias2, r2bS, ebS),
       reads=[res["r2bS"], res["ebS"]], writes=[res["bias2"]], tag="bias2")
    bk = take_bank()
    mm_group([(bk["f32"][0:128, 0:1], ones1F, alphaS)], bk,
             reads=[res["ones1F"], res["alphaS"]], tag="alpha_mm")
    ve(lambda e, b=bk: e.tensor_copy(alphab, b["f32"][0:128, 0:1]),
       reads=[bk["res"]], writes=[res["alphab"]])
    ve(lambda e: e.tensor_scalar_mul(bib, bib, alphab[0:8, 0:1]),
       reads=[res["alphab"]], writes=[res["bib"]], tag="abi")

    for nm in ("WQT", "WKT", "WOT"):
        buf = wstr[0] if nm != "WKT" else wstr[1]
        bres = wstr_res[0] if nm != "WKT" else wstr_res[1]
        dma("gpsimd", buf, dw[nm], writes=[bres], tag=f"precast_ld_{nm}")
        dma("gpsimd", dscr[nm], buf, reads=[bres], writes=[res["scr_" + nm]],
            tag=f"precast_st_{nm}")
    buf, bres = wstr[1], wstr_res[1]
    dma("gpsimd", buf[:, 0:NKRCH * D], dw["Uf"], writes=[bres], tag="precast_ld_Uf")
    dma("gpsimd", buf[:, NKRCH * D:2 * NKRCH * D], dw["Ui"], writes=[bres],
        tag="precast_ld_Ui")
    ve(lambda e, buf=buf: e.tensor_scalar_mul(
        buf[:, NKRCH * D:2 * NKRCH * D], buf[:, NKRCH * D:2 * NKRCH * D], alphab),
       reads=[res["alphab"]], writes=[bres], tag="precast_aUi")
    dma("gpsimd", dscr["UFUI"], buf, reads=[bres], writes=[res["scr_UFUI"]],
        tag="precast_st_UFUI")

    def _steps_of(*names):
        out, seen = [], set()
        for nm_ in names:
            r_ = res[nm_]
            for st_ in r_.writers + r_.readers:
                if id(st_) not in seen:
                    seen.add(id(st_))
                    out.append(st_)
        return out

    fence0 = _steps_of("xt0")
    fence1 = _steps_of("xt1")

    arena = _Arena(arena_base)
    ybufs = [arena.alloc("ybuf0", [128, NCH * SB], F32),
             arena.alloc("ybuf1", [128, NCH * SB], F32)]
    ybufs_res = [res["ybuf0"], res["ybuf1"]]
    yb16 = arena.alloc("yb16", [128, NCH * SB], BF16)
    lntmp = arena.alloc("lntmp", [128, NCH * SB], BF16)
    lntmpb = lntmp
    Qfm = arena.alloc("Qfm", [128, NCH * SB], BF16)
    Kfm = arena.alloc("Kfm", [128, NCH * SB], BF16)
    hsb = [arena.alloc("hsb0", [128, SB], BF16), arena.alloc("hsb1", [128, SB], BF16)]
    hsb_res = [res["hsb0"], res["hsb1"]]
    lgsbF = arena.alloc("lgsbF", [8, SB], BF16)
    lgsbI = arena.alloc("lgsbI", [8, SB], BF16)
    wfmF = arena.alloc("wfmF", [8, SB], BF16)
    wfmI = arena.alloc("wfmI", [8, SB], BF16)
    cumS = arena.alloc("cumS", [128, NKRCH * SB], F32)
    P2F = arena.alloc("P2F", [128, NKRCH * SB], BF16)
    P2I = arena.alloc("P2I", [128, NKRCH * SB], BF16)
    ff16 = arena.alloc("ff16", [128, NCH * SB], BF16)

    def wuse(s, nm):
        idx = list(STREAM_W).index(nm)
        return wstr[idx % 3], wstr_res[idx % 3]

    def wstream(s, nm):
        buf, bres = wuse(s, nm)
        dma("gpsimd", buf, dscr[nm], reads=[res["scr_" + nm]], writes=[bres],
            tag=f"B_wld{s}_{nm}")

    def emit_load(s, extra=()):
        yb = ybufs[s % 2]
        P.add("sync",
              lambda e, yb=yb, s=s: e.dma_start(
                  out=yb.rearrange("p (c t) -> p c t", c=NCH),
                  in_=dx.rearrange("p (c t) -> p c t", c=NCH)[:, :, s * SB:(s + 1) * SB]),
              writes=[ybufs_res[s % 2]], extra_deps=extra, tag=f"B_x{s}")

    def emit_proj(s):
        yb = ybufs[s % 2]
        yb_res = ybufs_res[s % 2]
        for oc in range(NCH):
            bp = take_bank()
            mm_group([(bp["f32"][:, 0:SB], WATb[:, oc * 128:(oc + 1) * 128],
                       vfm[:, s * SB:(s + 1) * SB])], bp,
                     reads=[res["WATb"], res["vfm"]], tag=f"B_proj{s}_{oc}")
            ve(lambda e, b=bp, oc=oc, yb=yb:
               e.tensor_add(yb[:, oc * SB:(oc + 1) * SB],
                            yb[:, oc * SB:(oc + 1) * SB], b["f32"][:, 0:SB]),
               reads=[bp["res"]], writes=[yb_res], tag=f"B_res{s}_{oc}")

    B_banks = {}

    def emit_stats_head(s):
        yb = ybufs[s % 2]
        yb_res = ybufs_res[s % 2]
        ae(lambda e, yb=yb: e.copy(yb16, yb),
           reads=[yb_res], writes=[res["yb16"]], tag=f"B_cast{s}")
        bnm = take_bank(pin=True)
        mm_group([(bnm["f32"][0:1, 0:SB], negidB, rch(yb16, c)) for c in range(NCH)],
                 bnm, reads=[res["yb16"], res["negidB"]], tag=f"B_negm{s}")
        ae(lambda e: e.activation(lntmp, yb16, AF.Square),
           reads=[res["yb16"]], writes=[res["lntmp"]], tag=f"B_sq{s}")
        bms = take_bank(pin=True)
        mm_group([(bms["f32"][0:1, 0:SB], invdB, rch(lntmp, c)) for c in range(NCH)],
                 bms, reads=[res["lntmp"], res["invdB"]], tag=f"B_msq{s}")
        B_banks[s] = (bnm, bms)

    def emit_stats_tail(s):
        yb = ybufs[s % 2]
        yb_res = ybufs_res[s % 2]
        bnm, bms = B_banks.pop(s)
        ae(lambda e, b=bnm: e.activation(st_a, b["f32"][0:1, 0:SB], AF.Square),
           reads=[bnm["res"]], writes=[res["st_a"]], tag=f"B_m2{s}")
        ve(lambda e, b=bms: e.tensor_sub(st_b, b["f32"][0:1, 0:SB], st_a),
           reads=[bms["res"], res["st_a"]], writes=[res["st_b"]])
        ve(lambda e: e.tensor_scalar_add(st_a, st_b, EPS_LN),
           reads=[res["st_b"]], writes=[res["st_a"]])
        ae(lambda e: e.activation(st_a, st_a, AF.Ln), writes=[res["st_a"]])
        ae(lambda e: e.activation(st_c, st_a, AF.Exp, scale=-0.5),
           reads=[res["st_a"]], writes=[res["st_c"]], tag=f"B_s{s}")
        ae(lambda e, b=bnm: e.copy(st_b, b["f32"][0:1, 0:SB]),
           reads=[bnm["res"]], writes=[res["st_b"]], tag=f"B_negm_sb{s}")
        unpin(bnm)
        unpin(bms)

    def emit_stats_bcast(s):
        yb = ybufs[s % 2]
        yb_res = ybufs_res[s % 2]
        bnb = take_bank()
        mm_group([(bnb["f32"][:, 0:SB], ones1F, st_b)], bnb,
                 reads=[res["st_b"], res["ones1F"]], tag=f"B_nmb{s}")
        bsb = take_bank()
        mm_group([(bsb["f32"][:, 0:SB], ones1F, st_c)], bsb,
                 reads=[res["st_c"], res["ones1F"]], tag=f"B_sbc{s}")
        ve(lambda e, b=bnb, yb=yb:
           e.tensor_add(lntmp.rearrange("p (c t) -> p c t", c=NCH),
                        yb.rearrange("p (c t) -> p c t", c=NCH),
                        b["f32"][:, 0:SB].unsqueeze(1).broadcast_to([128, NCH, SB]),
                        ),
           reads=[bnb["res"], yb_res], writes=[res["lntmp"]], tag=f"B_ln1_{s}")
        ve(lambda e, b=bsb:
           e.tensor_mul(yb16.rearrange("p (c t) -> p c t", c=NCH),
                        lntmp.rearrange("p (c t) -> p c t", c=NCH),
                        b["f32"][:, 0:SB].unsqueeze(1).broadcast_to([128, NCH, SB])),
           reads=[bsb["res"], res["lntmp"]], writes=[res["yb16"]], tag=f"B_ln2_{s}")

    def emit_qk(s):
        for (nm, dst, dkey) in (("WQT", Qfm, "Qfm"), ("WKT", Kfm, "Kfm")):
            wt, wres = wuse(s, nm)
            for oc in range(NCH):
                bq = take_bank()
                mm_group([(bq["f32"][:, 0:SB],
                           wt[:, c * D + oc * 128: c * D + (oc + 1) * 128],
                           rch(yb16, c)) for c in range(NCH)], bq,
                         reads=[wres, res["yb16"]], tag=f"B_{nm}{s}_{oc}")
                ae(lambda e, b=bq, dst=dst, oc=oc:
                   e.copy(dst[:, oc * SB:(oc + 1) * SB], b["f32"][:, 0:SB]),
                   reads=[bq["res"]], writes=[res[dkey]], tag=f"B_{nm}ev{s}_{oc}")

    def emit_h(s, bs, wname, lgsb, lgkey):
        wt, wres = wuse(s, wname)
        blg = take_bank(pin=True)
        for jc in range(NCH):
            bh = take_bank()
            mm_group([(bh["f32"][:, 0:SB], wt[:, c * D + jc * 128: c * D + (jc + 1) * 128],
                       rch(Qfm, c)) for c in range(NCH)], bh,
                     reads=[wres, res["Qfm"]], tag=f"B{bs}_h{s}_{jc}")
            hb = hsb[jc % 2]
            hbr = hsb_res[jc % 2]
            ae(lambda e, b=bh, hb=hb, jc=jc:
               e.activation(hb, b["f32"][:, 0:SB], AF.Gelu, bias=r1bS[:, jc:jc + 1]),
               reads=[bh["res"], res["r1bS"]], writes=[hbr], tag=f"B{bs}_hev{s}_{jc}")
            te(lambda e, b=blg, hb=hb, jc=jc:
               e.matmul(b["f32"][0:8, 0:SB], R2Tb[:, jc * 8:(jc + 1) * 8], hb,
                        start=(jc == 0), stop=(jc == NCH - 1)),
               reads=[hbr, res["R2Tb"]], writes=[blg["res"]], tag=f"B{bs}_lg{s}_{jc}")
        ve(lambda e, b=blg, lgsb=lgsb: e.tensor_scalar_add(lgsb, b["f32"][0:8, 0:SB], bias2),
           reads=[blg["res"], res["bias2"]], writes=[res[lgkey]], tag=f"B{bs}_lgev{s}")
        unpin(blg)

    def emit_smx_pre(s, bs, lgsb, lgkey, wts, wtsk):
        btA = take_bank(pin=True)
        for bi in range(SB // 128):
            te(lambda e, b=btA, bi=bi, lgsb=lgsb:
               e.transpose(b["b16"][:, bi * 8:(bi + 1) * 8],
                           lgsb[:, bi * 128:(bi + 1) * 128], identb[0:8, 0:8]),
               reads=[res[lgkey], res["identb"]], writes=[btA["res"]],
               tag=f"B{bs}_lt{s}_{bi}")
        ve(lambda e, b=btA:
           e.reduce_max(smxM, b["b16"][:, 0:32].rearrange("p (c e) -> p c e", e=8),
                        AX.X, negate=True),
           reads=[btA["res"]], writes=[res["smxM"]], tag=f"B{bs}_smax{s}")
        ve(lambda e, b=btA, wts=wts:
           e.tensor_add(wts.rearrange("p (c e) -> p c e", e=8),
                        b["b16"][:, 0:32].rearrange("p (c e) -> p c e", e=8),
                        smxM.unsqueeze(2).broadcast_to([128, 4, 8])),
           reads=[btA["res"], res["smxM"]], writes=[res[wtsk]],
           tag=f"B{bs}_ssub{s}")
        unpin(btA)

    def emit_smx_mid(s, bs, wts, wtsk):
        ae(lambda e, wts=wts: e.activation(wts, wts, AF.Exp, bias=0.0),
           writes=[res[wtsk]], tag=f"B{bs}_wexp{s}")
        for c in range(SB // 128):
            ve(lambda e, c=c, wts=wts: e.reduce_sum(sumM[:, c:c + 1],
                                                    wts[:, c * 8:(c + 1) * 8], AX.X),
               reads=[res[wtsk]], writes=[res["sumM"]], tag=f"B{bs}_ssum{s}_{c}")
        ve(lambda e: e.reciprocal(sumM, sumM), writes=[res["sumM"]],
           tag=f"B{bs}_srec{s}")
        ve(lambda e, s=s: e.tensor_mul(smxQ, nvinvS[:, s * 4:(s + 1) * 4], sumM),
           reads=[res["nvinvS"], res["sumM"]], writes=[res["smxQ"]],
           tag=f"B{bs}_snv{s}")
        ve(lambda e, wts=wts: e.tensor_mul(wts.rearrange("p (c e) -> p c e", e=8),
                                           wts.rearrange("p (c e) -> p c e", e=8),
                                           smxQ.unsqueeze(2).broadcast_to([128, 4, 8])),
           reads=[res["smxQ"]], writes=[res[wtsk]], tag=f"B{bs}_wts{s}")

    def emit_smx_back(s, bs, wfm, wfmk, wts, wtsk):
        btB = take_bank()
        for bi in range(SB // 128):
            te(lambda e, b=btB, bi=bi, wts=wts:
               e.transpose(b["b16"][0:8, bi * 128:(bi + 1) * 128],
                           wts[:, bi * 8:(bi + 1) * 8], identb),
               reads=[res[wtsk], res["identb"]], writes=[btB["res"]],
               tag=f"B{bs}_wt{s}_{bi}")
        ae(lambda e, b=btB, wfm=wfm: e.copy(wfm, b["b16"][0:8, 0:SB]),
           reads=[btB["res"]], writes=[res[wfmk]], tag=f"B{bs}_wfm{s}")

    def emit_ywxv(s, bs, Vp, Vpk, Wp, Wpk, carry, carryk, hook=None):
        for kc in range(NKRCH):
            byw = take_bank()
            mm_group([(byw["f32"][:, 0:SB],
                       Wp[:, c * KR + kc * 128: c * KR + (kc + 1) * 128],
                       rch(Kfm, c)) for c in range(NCH)], byw,
                     reads=[res[Wpk], res["Kfm"]], tag=f"B{bs}_yw{s}_{kc}")
            ve(lambda e, b=byw, kc=kc, carry=carry, s=s:
               e.tensor_tensor_scan(rch(cumS, kc), onesColF.broadcast_to([128, SB]),
                                    b["f32"][:, 0:SB],
                                    carry[:, kc:kc + 1] if s > 0 else 0.0,
                                    ALU.mult, ALU.add),
               reads=[byw["res"], res["onesColF"]] + ([res[carryk]] if s > 0 else []),
               writes=[res["cumS"]], tag=f"B{bs}_scan{s}_{kc}")
            ve(lambda e, kc=kc, carry=carry:
               e.tensor_copy(carry[:, kc:kc + 1], rch(cumS, kc)[:, SB - 1:SB]),
               reads=[res["cumS"]], writes=[res[carryk]], tag=f"B{bs}_carry{s}_{kc}")
            bxv = take_bank()
            mm_group([(bxv["f32"][:, 0:SB],
                       Vp[:, c * KR + kc * 128: c * KR + (kc + 1) * 128],
                       rch(Qfm, c)) for c in range(NCH)], bxv,
                     reads=[res[Vpk], res["Qfm"]], tag=f"B{bs}_xv{s}_{kc}")
            if kc == 0 and hook is not None:
                hook()
            ve(lambda e, b=bxv, kc=kc: e.tensor_tensor(rch(cumS, kc), rch(cumS, kc),
                                                       b["f32"][:, 0:SB], ALU.mult),
               reads=[bxv["res"]], writes=[res["cumS"]], tag=f"B{bs}_p1{s}_{kc}")
            bwr = take_bank()
            mm_group([(bwr["f32"][:, 0:SB], EselB[:, kc * 128:(kc + 1) * 128], wfmF if bs == "F" else wfmI)],
                     bwr, reads=[res["EselB"], res["wfmF" if bs == "F" else "wfmI"]],
                     tag=f"B{bs}_wrep{s}_{kc}")
            P2, P2k = (P2F, "P2F") if bs == "F" else (P2I, "P2I")
            ve(lambda e, b=bwr, kc=kc, P2=P2:
               e.tensor_mul(rch(P2, kc), rch(cumS, kc), b["f32"][:, 0:SB]),
               reads=[res["cumS"], bwr["res"]], writes=[res[P2k]],
               tag=f"B{bs}_p2{s}_{kc}")

    def emit_out(s):
        wt, wres = wuse(s, "UFUI")
        ufo = 0
        uio = NKRCH * D
        for dc in range(NCH):
            bo = take_bank()
            ops = []
            out = bo["f32"][:, 0:SB]
            for kc in range(NKRCH):
                ops.append((out, wt[:, ufo + kc * D + dc * 128: ufo + kc * D + (dc + 1) * 128],
                            rch(P2F, kc)))
            ops.append((out, bfb[0:8, dc * 128:(dc + 1) * 128], wfmF))
            for kc in range(NKRCH):
                ops.append((out, wt[:, uio + kc * D + dc * 128: uio + kc * D + (dc + 1) * 128],
                            rch(P2I, kc)))
            ops.append((out, bib[0:8, dc * 128:(dc + 1) * 128], wfmI))
            mm_group(ops, bo,
                     reads=[wres, res["bfb"], res["bib"],
                            res["P2F"], res["P2I"], res["wfmF"], res["wfmI"]],
                     tag=f"B_out{s}_{dc}")
            ae(lambda e, b=bo, dc=dc: e.copy(ff16[:, dc * SB:(dc + 1) * SB],
                                             b["f32"][:, 0:SB]),
               reads=[bo["res"]], writes=[res["ff16"]], tag=f"B_outev{s}_{dc}")

    def emit_wo(s):
        yb = ybufs[s % 2]
        yb_res = ybufs_res[s % 2]
        wt, wres = wuse(s, "WOT")
        for oc in range(NCH):
            bw = take_bank()
            mm_group([(bw["f32"][:, 0:SB], wt[:, c * D + oc * 128: c * D + (oc + 1) * 128],
                       rch(ff16, c)) for c in range(NCH)], bw,
                     reads=[wres, res["ff16"]], tag=f"B_wo{s}_{oc}")
            ve(lambda e, b=bw, oc=oc, yb=yb:
               e.tensor_add(yb[:, oc * SB:(oc + 1) * SB],
                            yb[:, oc * SB:(oc + 1) * SB], b["f32"][:, 0:SB]),
               reads=[bw["res"]], writes=[yb_res], tag=f"B_wores{s}_{oc}")
        dma("sync", dout.rearrange("p (c t) -> p c t", c=NCH)[:, :, s * SB:(s + 1) * SB],
            yb.rearrange("p (c t) -> p c t", c=NCH),
            reads=[yb_res], writes=[out_res], tag=f"B_store{s}")

    emit_load(0, extra=fence0)
    wstream(0, "WQT")
    wstream(0, "WKT")
    wstream(0, "R1T")
    emit_proj(0)
    emit_stats_head(0)
    emit_stats_tail(0)
    emit_stats_bcast(0)
    for s in range(NSB):
        emit_qk(s)
        wstream(s, "R1TI")
        if s + 1 < NSB:
            emit_load(s + 1, extra=(fence1 if s + 1 == 1 else ()))
            emit_proj(s + 1)
        emit_h(s, "F", "R1T", lgsbF, "lgsbF")
        emit_smx_pre(s, "F", lgsbF, "lgsbF", wtsBF, "wtsBF")
        wstream(s, "UFUI")
        wstream(s, "WOT")
        emit_h(s, "I", "R1TI", lgsbI, "lgsbI")
        emit_smx_pre(s, "I", lgsbI, "lgsbI", wtsBI, "wtsBI")
        emit_smx_mid(s, "F", wtsBF, "wtsBF")
        if s + 1 < NSB:
            wstream(s + 1, "WQT")
        emit_ywxv(s, "F", Vfb, "Vfb", Wfb, "Wfb", carryF, "carryF",
                  hook=lambda s=s: (emit_smx_back(s, "F", wfmF, "wfmF",
                                                  wtsBF, "wtsBF"),
                                    emit_smx_mid(s, "I", wtsBI, "wtsBI")))
        emit_ywxv(s, "I", Vixb, "Vixb", Wiyb, "Wiyb", carryI, "carryI",
                  hook=lambda s=s: emit_smx_back(s, "I", wfmI, "wfmI",
                                                 wtsBI, "wtsBI"))
        if s + 1 < NSB:
            emit_stats_head(s + 1)
            emit_stats_tail(s + 1)
            emit_stats_bcast(s + 1)
        emit_out(s)
        if s + 1 < NSB:
            wstream(s + 1, "WKT")
        emit_wo(s)
        if s + 1 < NSB:
            wstream(s + 1, "R1T")
        if s == 0:
            tap("y0", ybufs[0], ybufs_res[0])
            tap("Q0", Qfm, res["Qfm"])
            tap("cum0", cumS, res["cumS"])

    P.add("sync", lambda e: e.nop(), reads=[out_res], tag="final_wait")

    P.finalize()

    with (nc.semaphore("sem_t") as s_t, nc.semaphore("sem_v") as s_v,
          nc.semaphore("sem_a") as s_a, nc.semaphore("sem_s") as s_s,
          nc.semaphore("sem_g") as s_g, nc.semaphore("sem_d") as s_d,
          nc.Block() as block):
        sems = {"tensor": s_t, "vector": s_v, "scalar": s_a,
                "sync": s_s, "gpsimd": s_g, "sdma": s_d}

        @block.tensor
        def _(eng):
            P.emit_engine("tensor", eng, sems)

        @block.vector
        def _(eng):
            P.emit_engine("vector", eng, sems)

        @block.scalar
        def _(eng):
            P.emit_engine("scalar", eng, sems)

        @block.sync
        def _(eng):
            P.emit_engine("sync", eng, sems)

        @block.gpsimd
        def _(eng):
            P.emit_engine("gpsimd", eng, sems)

    return nc, tap_drams


def kernel(**inputs):
    xs, w = _host_prep(inputs)
    taps = tuple(_DEBUG_TAPS)
    nc, tap_drams = build_program(taps)
    in_maps = []
    for b in range(8):
        m = {"xT": xs[b]}
        m.update(w)
        in_maps.append(m)
    r = run_bass_kernel_spmd(nc, in_maps, core_ids=list(range(8)))
    outs = []
    for b in range(8):
        o = np.asarray(r.results[b]["outT"], dtype=np.float32)
        outs.append(o.reshape(128, NCH, T).transpose(1, 0, 2).reshape(D, T).T)
    if taps:
        kernel.last_taps = [{k: np.asarray(r.results[b]["tap_" + k]) for k in taps
                             if ("tap_" + k) in r.results[b]} for b in range(8)]
    return np.ascontiguousarray(np.stack(outs, axis=0))


# version 48
# speedup vs baseline: 1.0061x; 1.0061x over previous
import numpy as np

import concourse.bass as bass
import concourse.mybir as mybir
import concourse.bass_utils as _bu
from concourse.bass_utils import run_bass_kernel_spmd

if not getattr(_bu, "_ldw_patch", False):
    _orig_run_command = _bu.run_command

    def _run_command_ldw(argv, **kw):
        if isinstance(argv, list):
            argv = ["--enable-ldw-opt=true" if a == "--enable-ldw-opt=false" else a
                    for a in argv]
        return _orig_run_command(argv, **kw)

    _bu.run_command = _run_command_ldw
    _bu._ldw_patch = True

F32 = mybir.dt.float32
BF16 = mybir.dt.bfloat16
AF = mybir.ActivationFunctionType
ALU = mybir.AluOpType
AX = mybir.AxisListType

D = 1024
T = 4096
NCH = 8
HEADS = 8
KEXP = 8
R = 64
KR = 512
NKRCH = 4
TILE = 512
NT = T // TILE
SB = 512
NSB = T // SB
EPS_LN = 1e-5
EPS_RMS = 1e-6

_ENGINES = ("tensor", "vector", "scalar", "sync", "gpsimd")
_DMA_ENGINES = ("sync", "gpsimd")

_DEBUG_TAPS = []


class _Step:
    __slots__ = ("eng", "emit", "deps", "inc_needed", "sem_val", "tag", "semkey")

    def __init__(self, eng, emit, tag="", semkey=None):
        self.eng = eng
        self.emit = emit
        self.deps = []
        self.inc_needed = eng in _DMA_ENGINES or semkey == "sdma"
        self.sem_val = None
        self.tag = tag
        self.semkey = semkey or eng


class _Res:
    __slots__ = ("writers", "readers", "rsw", "name")

    def __init__(self, name=""):
        self.writers = []
        self.readers = []
        self.rsw = False
        self.name = name


class _Plan:
    def __init__(self):
        self.steps = []

    def add(self, eng, emit, reads=(), writes=(), tag="", extra_deps=(), semkey=None):
        st = _Step(eng, emit, tag, semkey)
        deps = []
        for r in reads:
            deps.extend(r.writers)
        for w in writes:
            deps.extend(w.writers)
            deps.extend(w.readers)
        deps.extend(extra_deps)
        seen = set()
        for d in deps:
            if d is st or id(d) in seen:
                continue
            seen.add(id(d))
            st.deps.append(d)
        for r in reads:
            r.readers.append(st)
            r.rsw = True
        for w in writes:
            if w.rsw:
                w.writers = [st]
                w.readers = []
                w.rsw = False
            else:
                w.writers.append(st)
        self.steps.append(st)
        return st

    def finalize(self):
        for st in self.steps:
            for d in st.deps:
                d.inc_needed = True
        counters = {e: 0 for e in _ENGINES}
        counters["sdma"] = 0
        for st in self.steps:
            if st.inc_needed:
                dma = st.eng in _DMA_ENGINES or st.semkey == "sdma"
                counters[st.semkey] += 16 if dma else 1
                st.sem_val = counters[st.semkey]

    def emit_engine(self, name, eng, sems):
        watermark = {}
        for st in self.steps:
            if st.eng != name:
                continue
            for d in st.deps:
                if watermark.get(d.semkey, 0) >= d.sem_val:
                    continue
                eng.wait_ge(sems[d.semkey], d.sem_val)
                watermark[d.semkey] = d.sem_val
            insts = st.emit(eng)
            if st.inc_needed:
                if not isinstance(insts, (list, tuple)):
                    insts = [insts]
                if name in _DMA_ENGINES:
                    for i in insts:
                        i.then_inc(sems[name], 16)
                    eng.wait_ge(sems[name], st.sem_val)
                    watermark[name] = st.sem_val
                elif st.semkey == "sdma":
                    for i in insts:
                        i.then_inc(sems["sdma"], 16)
                else:
                    insts[-1].then_inc(sems[name], 1)


def _pack_cm(m):
    r, c = m.shape
    return np.ascontiguousarray(
        m.reshape(r // 128, 128, c).transpose(1, 0, 2).reshape(128, -1))


def _host_prep(inp):
    g = {k: np.asarray(v, dtype=np.float32) for k, v in inp.items()}
    w = {}
    x = g["L0_attn_x"]
    xs = [_pack_cm(np.ascontiguousarray(x[b].T)) for b in range(8)]
    w["wqkvT"] = _pack_cm(np.ascontiguousarray(g["L0_attn_w_qkv"].T))
    w["woutT"] = _pack_cm(np.ascontiguousarray(g["L0_attn_w_out"].T))
    w["WQT"] = _pack_cm(np.ascontiguousarray(g["L0_ffn_W_Q"].T))
    w["WKT"] = _pack_cm(np.ascontiguousarray(g["L0_ffn_W_K"].T))
    w["WIN"] = _pack_cm(np.ascontiguousarray(g["L0_ffn_W_inv"]))
    w["R1T"] = _pack_cm(np.ascontiguousarray(g["L0_ffn_router_w1"].T))
    w["WOT"] = _pack_cm(np.ascontiguousarray(g["L0_ffn_W_O"].T))
    w["R2T"] = _pack_cm(np.ascontiguousarray(g["L0_ffn_router_w2"].T))
    w["Vf"] = _pack_cm(g["L0_ffn_V_fwd"].transpose(1, 0, 2).reshape(D, KR))
    w["Wf"] = _pack_cm(g["L0_ffn_W_fwd"].transpose(1, 0, 2).reshape(D, KR))
    w["Vix"] = _pack_cm(g["L0_ffn_W_inv_exp"].transpose(1, 0, 2).reshape(D, KR))
    w["Wiy"] = _pack_cm(g["L0_ffn_V_inv"].transpose(1, 0, 2).reshape(D, KR))
    w["Uf"] = _pack_cm(g["L0_ffn_U_fwd"].transpose(0, 2, 1).reshape(KR, D))
    w["Ui"] = _pack_cm(g["L0_ffn_U_inv"].transpose(0, 2, 1).reshape(KR, D))
    w["bfw"] = np.ascontiguousarray(g["L0_ffn_b_fwd"])
    w["biw"] = np.ascontiguousarray(g["L0_ffn_b_inv"])
    w["gamma"] = np.ascontiguousarray(g["L0_attn_norm_weight"].reshape(NCH, 128).T)
    w["r1b"] = np.ascontiguousarray(g["L0_ffn_router_b1"].reshape(NCH, 128).T)
    w["temp"] = np.ascontiguousarray(g["L0_attn_temperature"].reshape(8, 1))
    w["r2b"] = np.ascontiguousarray(g["L0_ffn_router_b2"].reshape(8, 1))
    w["eb"] = np.ascontiguousarray(g["L0_ffn_expert_bias"].reshape(8, 1))
    w["alpha"] = np.ascontiguousarray(g["L0_ffn_alpha_bi"].reshape(1, 1))
    w["ident"] = np.eye(128, dtype=np.float32)
    mask = np.full((128, 128), -30000.0, dtype=np.float32)
    for h in range(HEADS):
        mask[16 * h:16 * h + 16, 16 * h:16 * h + 16] = 0.0
    w["mask"] = mask
    nv = np.zeros((128, 32), dtype=np.float32)
    for blk in range(32):
        nv[:, blk] = 1.0 / (blk * 128 + np.arange(128) + 1.0)
    w["nvinv"] = nv
    esel = np.zeros((8, 512), dtype=np.float32)
    for c in range(4):
        for p in range(128):
            esel[2 * c + p // 64, c * 128 + p] = 1.0
    w["Esel"] = esel
    hsel = np.zeros((8, 128), dtype=np.float32)
    for p in range(128):
        hsel[p // 16, p] = 1.0
    w["hsel"] = hsel
    return xs, w


def build_program(taps=()):
    nc = bass.Bass("TRN2", target_bir_lowering=False)
    P = _Plan()

    dx = nc.dram_tensor("xT", [128, NCH * T], F32, kind="ExternalInput").ap()
    dout = nc.dram_tensor("outT", [128, NCH * T], F32, kind="ExternalOutput").ap()
    dw = {}
    for name, shape in [
        ("wqkvT", [128, NCH * 384]), ("woutT", [128, 1024]),
        ("WQT", [128, NCH * D]), ("WKT", [128, NCH * D]), ("WIN", [128, NCH * D]),
        ("R1T", [128, NCH * D]), ("WOT", [128, NCH * D]), ("R2T", [128, 64]),
        ("Vf", [128, NCH * KR]), ("Wf", [128, NCH * KR]),
        ("Vix", [128, NCH * KR]), ("Wiy", [128, NCH * KR]),
        ("Uf", [128, NKRCH * D]), ("Ui", [128, NKRCH * D]),
        ("bfw", [8, 1024]), ("biw", [8, 1024]),
        ("gamma", [128, 8]), ("r1b", [128, 8]),
        ("temp", [8, 1]), ("r2b", [8, 1]), ("eb", [8, 1]), ("alpha", [1, 1]),
        ("ident", [128, 128]), ("mask", [128, 128]), ("nvinv", [128, 32]),
        ("Esel", [8, 512]), ("hsel", [8, 128]),
    ]:
        dw[name] = nc.dram_tensor(name, shape, F32, kind="ExternalInput").ap()

    tap_drams = {}
    res = {}
    STREAM_W = ("WQT", "WKT", "R1T", "R1TI", "UFUI", "WOT")
    dscr = {}
    for nm in STREAM_W:
        dscr[nm] = nc.dram_tensor("scr_" + nm, [128, NCH * D], BF16).ap()
        res["scr_" + nm] = _Res("scr_" + nm)
    for nm in ("VIX2", "WIY2"):
        dscr[nm] = nc.dram_tensor("scr_" + nm, [128, NCH * KR], BF16).ap()
        res["scr_" + nm] = _Res("scr_" + nm)

    def sb(name, shape, dtype):
        h = nc.alloc_sbuf_tensor(name, shape, dtype)
        res[name] = _Res(name)
        return h.ap()

    WATb = sb("WATb", [128, 1024], BF16)
    vfm = sb("vfm", [128, T], BF16)
    wstr = [sb("wsA", [128, NCH * D], BF16), sb("wsB", [128, NCH * D], BF16),
            sb("wsC", [128, NCH * D], BF16)]
    wstr_res = [res["wsA"], res["wsB"], res["wsC"]]
    R2Tb = sb("R2Tb", [128, 64], BF16)
    Vfb = sb("Vfb", [128, NCH * KR], BF16)
    Wfb = sb("Wfb", [128, NCH * KR], BF16)
    Vixb = sb("Vixb", [128, NCH * KR], BF16)
    Wiyb = sb("Wiyb", [128, NCH * KR], BF16)
    bfb = sb("bfb", [8, 1024], BF16)
    bib = sb("bib", [8, 1024], BF16)
    identb = sb("identb", [128, 128], BF16)
    nvinvS = sb("nvinvS", [128, 32], F32)
    EselB = sb("EselB", [8, 512], BF16)
    gammaS = sb("gammaS", [128, 8], F32)
    r1bS = sb("r1bS", [128, 8], F32)
    tempS = sb("tempS", [8, 1], F32)
    r2bS = sb("r2bS", [8, 1], F32)
    ebS = sb("ebS", [8, 1], F32)
    alphaS = sb("alphaS", [1, 1], F32)
    onesB = sb("onesB", [128, 1], BF16)
    negidB = sb("negidB", [128, 1], BF16)
    invdB = sb("invdB", [128, 1], BF16)
    ones1F = sb("ones1F", [1, 128], F32)
    onesColF = sb("onesColF", [128, 1], F32)
    gSqkvB = sb("gSqkvB", [1, 384], BF16)
    bias2 = sb("bias2", [8, 1], F32)
    ethb = sb("ethb", [128, 1], F32)
    alphab = sb("alphab", [128, 1], F32)
    etmpb = sb("etmpb", [8, 1], BF16)
    negmB = sb("negmB", [1, TILE], BF16)
    negmB1 = sb("negmB1", [1, TILE], BF16)
    maskS = sb("maskS", [128, 128], F32)
    hselB = sb("hselB", [8, 128], BF16)
    rkS = sb("rkS", [128, 1], F32)
    st_a = sb("st_a", [1, TILE], F32)
    st_b = sb("st_b", [1, TILE], F32)
    st_c = sb("st_c", [1, TILE], F32)
    sqA = sb("sqA", [128, 1], F32)
    nrmA = sb("nrmA", [128, 1], F32)
    sclA = sb("sclA", [128, 1], F32)
    carryF = sb("carryF", [128, 4], F32)
    carryI = sb("carryI", [128, 4], F32)
    smx1 = sb("smx1", [128, 1], F32)
    smx2 = sb("smx2", [128, 1], F32)
    smxM = sb("smxM", [128, 4], F32)
    sumM = sb("sumM", [128, 4], F32)
    smxQ = sb("smxQ", [128, 4], F32)
    wtsBF = sb("wtsBF", [128, 32], BF16)
    wtsBI = sb("wtsBI", [128, 32], BF16)
    out_res = _Res("OUT")
    res["OUT"] = out_res

    SBUF_TOP = 229344
    arena_base = nc._sbuf_addr_for_side("left")
    arena_base = (arena_base + 31) // 32 * 32

    class _Arena:
        def __init__(self, base):
            self.off = base

        def alloc(self, name, shape, dtype):
            per_part = int(np.prod(shape[1:])) * mybir.dt.size(dtype)
            off = (self.off + 31) // 32 * 32
            assert off + per_part <= SBUF_TOP, (
                f"arena overflow: {name} at {off}+{per_part} > {SBUF_TOP}")
            h = nc.alloc_sbuf_tensor_at(name, shape, dtype, offset=off)
            self.off = off + per_part
            res[name] = _Res(name)
            return h.ap()

    banks = []
    for i in range(8):
        h = nc.alloc_psum_tensor(f"pb{i}", [128, 512], F32)
        banks.append({"f32": h.ap(), "b16": h.ap().bitcast(BF16),
                      "res": _Res(f"bank{i}"), "i": i})
    bank_rr = [0]
    pinned = set()

    def take_bank(pin=False):
        for _ in range(8):
            b = banks[bank_rr[0] % 8]
            bank_rr[0] += 1
            if b["i"] not in pinned:
                if pin:
                    pinned.add(b["i"])
                return b
        raise RuntimeError("all psum banks pinned")

    def unpin(b):
        pinned.discard(b["i"])

    def dma(queue, out_ap, in_ap, reads=(), writes=(), tag=""):
        return P.add(queue, lambda e, o=out_ap, i=in_ap: e.dma_start(out=o, in_=i),
                     reads=reads, writes=writes, tag=tag,
                     semkey="sdma" if queue == "scalar" else None)

    def tap(name, ap_, rd):
        if name not in taps:
            return
        d = nc.dram_tensor("tap_" + name, list(ap_.shape), ap_.dtype,
                           kind="ExternalOutput").ap()
        tap_drams[name] = d
        dma("gpsimd", d, ap_, reads=[rd], tag="tap_" + name)

    def mm_group(ops, bank, reads, tag=""):
        n = len(ops)

        def em(eng, ops=ops, n=n):
            insts = []
            for i, (o, l, r) in enumerate(ops):
                insts.append(eng.matmul(o, l, r, start=(i == 0), stop=(i == n - 1)))
            return insts
        return P.add("tensor", em, reads=reads, writes=[bank["res"]], tag=tag)

    def ve(emit, reads=(), writes=(), tag=""):
        return P.add("vector", emit, reads=reads, writes=writes, tag=tag)

    def ae(emit, reads=(), writes=(), tag=""):
        return P.add("scalar", emit, reads=reads, writes=writes, tag=tag)

    def te(emit, reads=(), writes=(), tag=""):
        return P.add("tensor", emit, reads=reads, writes=writes, tag=tag)

    def ge(emit, reads=(), writes=(), tag=""):
        return P.add("gpsimd", emit, reads=reads, writes=writes, tag=tag)

    def rch(ap_, c, w=SB):
        return ap_.rearrange("p (c t) -> p c t", c=ap_.shape[1] // w)[:, c, :]

    arena = _Arena(arena_base)
    xt0 = arena.alloc("xt0", [128, NCH * TILE], F32)
    xt1 = arena.alloc("xt1", [128, NCH * TILE], F32)
    qfm = arena.alloc("qfm", [128, T], BF16)
    kfm = arena.alloc("kfm", [128, T], BF16)
    xb16 = arena.alloc("xb16", [128, NCH * TILE], BF16)
    xb16b = arena.alloc("xb16b", [128, NCH * TILE], BF16)
    xb16c = arena.alloc("xb16c", [128, NCH * TILE], BF16)
    xsq16 = arena.alloc("xsq16", [128, NCH * TILE], BF16)
    ssbA = arena.alloc("ssbA", [128, TILE], F32)
    ssbA1 = arena.alloc("ssbA1", [128, TILE], F32)
    qtm = arena.alloc("qtm", [128, 128], BF16)
    ktm = arena.alloc("ktm", [128, 128], BF16)
    qtm2 = arena.alloc("qtm2", [128, 128], BF16)
    ktm2 = arena.alloc("ktm2", [128, 128], BF16)
    simsb = arena.alloc("simsb", [128, 128], F32)
    Ab = arena.alloc("Ab", [128, 128], BF16)
    identf = arena.alloc("identf", [128, 128], F32)
    nsq = arena.alloc("nsq", [128, TILE], BF16)
    sqQa = arena.alloc("sqQa", [128, 1], F32)
    sqKa = arena.alloc("sqKa", [128, 1], F32)
    nred = arena.alloc("nred", [128, 1], F32)
    st_a2 = arena.alloc("st_a2", [1, TILE], F32)
    st_b2 = arena.alloc("st_b2", [1, TILE], F32)
    st_c2 = arena.alloc("st_c2", [1, TILE], F32)
    wqkvTb = arena.alloc("wqkvTb", [128, NCH * 384], BF16)
    woutTb = arena.alloc("woutTb", [128, 1024], BF16)


    def wload(dst_ap, dname, rkey):
        dma("gpsimd", dst_ap, dw[dname], writes=[res[rkey]], tag="ld_" + dname)

    dma("gpsimd", wstr[0], dw["WIN"], writes=[wstr_res[0]], tag="prep_ld_WIN")
    dma("gpsimd", qfm, dw["Vix"], writes=[res["qfm"]], tag="prep_ld_Vix")
    dma("gpsimd", wstr[1], dw["R1T"], writes=[wstr_res[1]], tag="prep_ld_R1T")
    wload(wqkvTb, "wqkvT", "wqkvTb")
    wload(identb, "ident", "identb")
    wload(hselB, "hsel", "hselB")
    wload(woutTb, "woutT", "woutTb")
    dma("gpsimd", maskS, dw["mask"], writes=[res["maskS"]])
    dma("gpsimd", identf, dw["ident"], writes=[res["identf"]])
    dma("gpsimd", gammaS, dw["gamma"], writes=[res["gammaS"]])
    dma("gpsimd", tempS, dw["temp"], writes=[res["tempS"]])

    for ap_, val, rk in ((sqQa, 0.0, "sqQa"), (sqKa, 0.0, "sqKa"),
                         (onesB, 1.0, "onesB"), (negidB, -1.0 / 1024.0, "negidB"),
                         (invdB, 1.0 / 1024.0, "invdB"), (ones1F, 1.0, "ones1F"),
                         (onesColF, 1.0, "onesColF"), (carryF, 0.0, "carryF"),
                         (carryI, 0.0, "carryI")):
        ve(lambda e, ap_=ap_, val=val: e.memset(ap_, val),
           writes=[res[rk]], tag="memset_" + rk)

    xts = [xt0, xt1]
    xts_res = [res["xt0"], res["xt1"]]
    xb16s = [xb16, xb16b, xb16c]
    xb16s_res = [res["xb16"], res["xb16b"], res["xb16c"]]
    ssbAs = [ssbA, ssbA1]
    ssbAs_res = [res["ssbA"], res["ssbA1"]]
    negmBs = [negmB, negmB1]
    negmBs_res = [res["negmB"], res["negmB1"]]
    qkv_dst = [(qfm, res["qfm"]), (kfm, res["kfm"]), (vfm, res["vfm"])]

    A_banks = {}

    def emit_A_head_dma(t):
        xt = xts[t % 2]
        xt_res = xts_res[t % 2]
        xb = xb16s[t % 3]
        xb_res = xb16s_res[t % 3]
        dma("sync", xt.rearrange("p (c t) -> p c t", c=NCH),
            dx.rearrange("p (c t) -> p c t", c=NCH)[:, :, t * TILE:(t + 1) * TILE],
            writes=[xt_res], tag=f"A_x{t}")
        ae(lambda e, xt=xt, xb=xb: e.copy(xb, xt),
           reads=[xt_res], writes=[xb_res], tag=f"A_cast{t}")

    def emit_A_head_mm(t):
        xb = xb16s[t % 3]
        xb_res = xb16s_res[t % 3]
        bnm = take_bank(pin=True)
        mm_group([(bnm["f32"][0:1, 0:TILE], negidB, rch(xb, c, TILE))
                  for c in range(NCH)], bnm,
                 reads=[xb_res, res["negidB"]], tag=f"A_negm{t}")
        ae(lambda e, xb=xb: e.activation(xsq16, xb, AF.Square),
           reads=[xb_res], writes=[res["xsq16"]], tag=f"A_sq{t}")
        mm_group([(bnm["f32"][32:33, 0:TILE], invdB, rch(xsq16, c, TILE))
                  for c in range(NCH)], bnm,
                 reads=[res["xsq16"], res["invdB"]], tag=f"A_msq{t}")
        A_banks[t] = bnm

    def emit_A_head(t):
        emit_A_head_dma(t)
        emit_A_head_mm(t)


    emit_A_head(0)
    emit_A_head(1)

    def prep_vw(nm, src_ap, src_res, stage_ap, stage_res):
        for a in range(NCH):
            bk_ = take_bank()
            mm_group([(bk_["f32"][:, 0:KR],
                       wstr[0][:, c * D + a * 128:c * D + (a + 1) * 128],
                       src_ap.rearrange("p (c t) -> p c t", c=NCH)[:, c, :])
                      for c in range(NCH)], bk_,
                     reads=[wstr_res[0], src_res], tag=f"prep_{nm}_{a}")
            ae(lambda e, b=bk_, a=a, stage_ap=stage_ap:
               e.copy(stage_ap[:, a * KR:(a + 1) * KR], b["f32"][:, 0:KR]),
               reads=[bk_["res"]], writes=[stage_res], tag=f"prep_{nm}ev_{a}")

    dma("gpsimd", xb16c, dw["Wiy"], writes=[res["xb16c"]], tag="prep_ld_Wiy")
    prep_vw("VIX2", qfm, res["qfm"], kfm, res["kfm"])
    xt1b = xt1.bitcast(BF16)
    for a in range(NCH):
        for hf in range(2):
            bk_ = take_bank()
            mm_group([(bk_["f32"][:, 0:512],
                       wstr[0][:, c * D + a * 128:c * D + (a + 1) * 128],
                       wstr[1][:, c * D + hf * 512:c * D + (hf + 1) * 512])
                      for c in range(NCH)], bk_,
                     reads=[wstr_res[0], wstr_res[1]], tag=f"prep_R1TI_{a}_{hf}")
            ae(lambda e, b=bk_, a=a, hf=hf:
               e.copy(xt1b[:, a * D + hf * 512:a * D + (hf + 1) * 512],
                      b["f32"][:, 0:512]),
               reads=[bk_["res"]], writes=[res["xt1"]], tag=f"prep_R1TIev_{a}_{hf}")
    prep_vw("WIY2", xb16c, res["xb16c"], qfm, res["qfm"])
    dma("gpsimd", dscr["R1T"], wstr[1], reads=[wstr_res[1]],
        writes=[res["scr_R1T"]], tag="precast_st_R1T")
    dma("gpsimd", dscr["R1TI"], xt1b, reads=[res["xt1"]],
        writes=[res["scr_R1TI"]], tag="prep_st_R1TI")
    dma("gpsimd", dscr["VIX2"], kfm, reads=[res["kfm"]], writes=[res["scr_VIX2"]],
        tag="prep_st_VIX2")
    dma("gpsimd", dscr["WIY2"], qfm, reads=[res["qfm"]], writes=[res["scr_WIY2"]],
        tag="prep_st_WIY2")

    for c in range(NCH):
        ve(lambda e, c=c: e.tensor_scalar_mul(wqkvTb[:, c * 384:(c + 1) * 384],
                                              wqkvTb[:, c * 384:(c + 1) * 384],
                                              gammaS[:, c:c + 1]),
           reads=[res["gammaS"]], writes=[res["wqkvTb"]], tag=f"gt_scale{c}")

    bk = take_bank()
    mm_group([(bk["f32"][0:1, 0:384], onesB, wqkvTb[:, c * 384:(c + 1) * 384])
              for c in range(NCH)], bk,
             reads=[res["wqkvTb"], res["onesB"]], tag="gSqkv_mm")
    ae(lambda e, b=bk: e.copy(gSqkvB, b["f32"][0:1, 0:384]),
       reads=[bk["res"]], writes=[res["gSqkvB"]], tag="gSqkv_cp")

    ae(lambda e: e.activation(tempS, tempS, AF.Exp), writes=[res["tempS"]], tag="exp_temp")
    ae(lambda e: e.copy(etmpb, tempS), reads=[res["tempS"]], writes=[res["etmpb"]])
    bk = take_bank()
    mm_group([(bk["f32"][0:128, 0:1], hselB, etmpb)], bk,
             reads=[res["hselB"], res["etmpb"]], tag="ethb_mm")
    ve(lambda e, b=bk: e.tensor_copy(ethb, b["f32"][0:128, 0:1]),
       reads=[bk["res"]], writes=[res["ethb"]])


    def emit_A_tail(t):
        bnm = A_banks.pop(t)
        bms = bnm
        sA, sB, sC = ((st_a, st_b, st_c) if t % 2 == 0 else (st_a2, st_b2, st_c2))
        rA, rB, rC = (("st_a", "st_b", "st_c") if t % 2 == 0
                      else ("st_a2", "st_b2", "st_c2"))
        ae(lambda e, b=bnm, sA=sA: e.activation(sA, b["f32"][0:1, 0:TILE], AF.Square),
           reads=[bnm["res"]], writes=[res[rA]], tag=f"A_m2{t}")
        ve(lambda e, b=bms, sA=sA, sB=sB: e.tensor_sub(sB, b["f32"][32:33, 0:TILE], sA),
           reads=[bms["res"], res[rA]], writes=[res[rB]], tag=f"A_var{t}")
        ve(lambda e, sA=sA, sB=sB: e.tensor_scalar(sA, sB, 1.0 + EPS_RMS,
                                                   EPS_LN * EPS_RMS,
                                                   ALU.mult, ALU.add),
           reads=[res[rB]], writes=[res[rA]])
        ae(lambda e, sA=sA: e.activation(sA, sA, AF.Ln), writes=[res[rA]])
        ae(lambda e, sA=sA, sC=sC: e.activation(sC, sA, AF.Exp, scale=-0.5),
           reads=[res[rA]], writes=[res[rC]], tag=f"A_s{t}")
        ae(lambda e, b=bnm, t=t: e.copy(negmBs[t % 2], b["f32"][0:1, 0:TILE]),
           reads=[bnm["res"]], writes=[negmBs_res[t % 2]], tag=f"A_negmB{t}")
        bsc = take_bank()
        mm_group([(bsc["f32"][:, 0:TILE], ones1F, sC)], bsc,
                 reads=[res[rC], res["ones1F"]], tag=f"A_sbc{t}")
        ve(lambda e, b=bsc, t=t: e.tensor_copy(ssbAs[t % 2], b["f32"][:, 0:TILE]),
           reads=[bsc["res"]], writes=[ssbAs_res[t % 2]], tag=f"A_ssb{t}")
        unpin(bnm)

    bG = take_bank(pin=True)
    qtms = [qtm, qtm2]
    ktms = [ktm, ktm2]
    qtms_res = [res["qtm"], res["qtm2"]]
    ktms_res = [res["ktm"], res["ktm2"]]

    def emit_A_gram(t):
        for b_ in range(TILE // 128):
            i = t * (TILE // 128) + b_
            bt1 = take_bank()
            te(lambda e, b=bt1, i=i: e.transpose(b["b16"][:, 0:128],
                                                 qfm[:, i * 128:(i + 1) * 128], identb),
               reads=[res["qfm"], res["identb"]], writes=[bt1["res"]], tag=f"G_tq{i}")
            ve(lambda e, b=bt1, i=i: e.tensor_copy(qtms[i % 2], b["b16"][:, 0:128]),
               reads=[bt1["res"]], writes=[qtms_res[i % 2]])
            bt2 = take_bank()
            te(lambda e, b=bt2, i=i: e.transpose(b["b16"][:, 0:128],
                                                 kfm[:, i * 128:(i + 1) * 128], identb),
               reads=[res["kfm"], res["identb"]], writes=[bt2["res"]], tag=f"G_tk{i}")
            ve(lambda e, b=bt2, i=i: e.tensor_copy(ktms[i % 2], b["b16"][:, 0:128]),
               reads=[bt2["res"]], writes=[ktms_res[i % 2]])
            te(lambda e, b=bG, i=i: e.matmul(b["f32"][:, 0:128], qtms[i % 2], ktms[i % 2],
                                             start=(i == 0), stop=(i == 31)),
               reads=[qtms_res[i % 2], ktms_res[i % 2]], writes=[bG["res"]],
               tag=f"G_mm{i}")

    def emit_A_qkv(t):
        xb = xb16s[t % 3]
        xb_res = xb16s_res[t % 3]
        for oc in range(3):
            bq = take_bank()
            ops = [(bq["f32"][:, 0:TILE],
                    wqkvTb[:, c * 384 + oc * 128: c * 384 + (oc + 1) * 128],
                    rch(xb, c, TILE)) for c in range(NCH)]
            ops.append((bq["f32"][:, 0:TILE], gSqkvB[:, oc * 128:(oc + 1) * 128],
                        negmBs[t % 2]))
            mm_group(ops, bq, reads=[res["wqkvTb"], xb_res, res["gSqkvB"],
                                     negmBs_res[t % 2]], tag=f"A_qkv{t}_{oc}")
            dst, dres = qkv_dst[oc]
            ve(lambda e, b=bq, dst=dst, t=t:
               e.tensor_mul(dst[:, t * TILE:(t + 1) * TILE], b["f32"][:, 0:TILE],
                            ssbAs[t % 2]),
               reads=[bq["res"], ssbAs_res[t % 2]], writes=[dres], tag=f"A_ev{t}_{oc}")
        for (fm2, fres2, acc, accr) in ((qfm, res["qfm"], sqQa, "sqQa"),
                                        (kfm, res["kfm"], sqKa, "sqKa")):
            ae(lambda e, fm2=fm2, t=t: e.activation(nsq, fm2[:, t * TILE:(t + 1) * TILE],
                                                    AF.Square),
               reads=[fres2], writes=[res["nsq"]], tag=f"A_nsq{t}")
            ve(lambda e: e.reduce_sum(nred, nsq, AX.X),
               reads=[res["nsq"]], writes=[res["nred"]], tag=f"A_nred{t}")
            ve(lambda e, acc=acc: e.tensor_add(acc, acc, nred),
               reads=[res["nred"]], writes=[res[accr]], tag=f"A_nacc{t}")

    emit_A_tail(0)
    for t in range(NT):
        if t + 2 < NT:
            emit_A_head(t + 2)
        if t + 1 < NT:
            emit_A_tail(t + 1)
        emit_A_qkv(t)
        emit_A_gram(t)

    tap("qfm", qfm, res["qfm"])
    tap("kfm", kfm, res["kfm"])
    tap("vfm", vfm, res["vfm"])

    for (acc, accr, which) in ((sqQa, "sqQa", "q"), (sqKa, "sqKa", "k")):
        ae(lambda e, acc=acc: e.activation(nrmA, acc, AF.Sqrt),
           reads=[res[accr]], writes=[res["nrmA"]])
        ve(lambda e: e.tensor_scalar_max(nrmA, nrmA, 1e-12), writes=[res["nrmA"]])
        ve(lambda e: e.reciprocal(nrmA, nrmA), writes=[res["nrmA"]])
        if which == "q":
            ve(lambda e: e.tensor_mul(sclA, nrmA, ethb),
               reads=[res["nrmA"], res["ethb"]], writes=[res["sclA"]], tag="N_qscale")
        else:
            ve(lambda e: e.tensor_copy(rkS, nrmA),
               reads=[res["nrmA"]], writes=[res["rkS"]], tag="N_kscale")

    brt = take_bank()
    te(lambda e, b=brt: e.transpose(b["f32"][0:1, 0:128], rkS, identf),
       reads=[res["rkS"], res["identf"]], writes=[brt["res"]], tag="S_rkT")
    ae(lambda e, b=brt: e.copy(st_a[:, 0:128], b["f32"][0:1, 0:128]),
       reads=[brt["res"]], writes=[res["st_a"]], tag="S_rkrow")
    bcb = take_bank()
    mm_group([(bcb["f32"][:, 0:128], ones1F, st_a[:, 0:128])], bcb,
             reads=[res["st_a"], res["ones1F"]], tag="S_colmm")
    ae(lambda e, b=bcb: e.copy(ssbA[:, 0:128], b["f32"][:, 0:128]),
       reads=[bcb["res"]], writes=[res["ssbA"]], tag="S_colcp")
    ve(lambda e, b=bG: e.scalar_tensor_tensor(simsb, b["f32"][:, 0:128], sclA,
                                              ssbA[:, 0:128],
                                              ALU.mult, ALU.mult),
       reads=[bG["res"], res["sclA"], res["ssbA"]], writes=[res["simsb"]],
       tag="S_scale")
    unpin(bG)
    ve(lambda e: e.tensor_add(simsb, simsb, maskS),
       reads=[res["maskS"]], writes=[res["simsb"]], tag="S_mask")
    ve(lambda e: e.reduce_max(smx1, simsb, AX.X, negate=True),
       reads=[res["simsb"]], writes=[res["smx1"]])
    ae(lambda e: e.activation(simsb, simsb, AF.Exp, bias=smx1),
       reads=[res["smx1"]], writes=[res["simsb"]], tag="S_exp")
    ve(lambda e: e.reduce_sum(smx2, simsb, AX.X),
       reads=[res["simsb"]], writes=[res["smx2"]])
    ve(lambda e: e.reciprocal(smx2, smx2), writes=[res["smx2"]])
    ve(lambda e: e.tensor_scalar_mul(Ab, simsb, smx2),
       reads=[res["simsb"], res["smx2"]], writes=[res["Ab"]], tag="S_A")
    tap("Ab", Ab, res["Ab"])
    for half in range(2):
        bw = take_bank()
        mm_group([(bw["f32"][:, 0:512], Ab, woutTb[:, half * 512:(half + 1) * 512])],
                 bw, reads=[res["Ab"], res["woutTb"]], tag=f"WAT{half}")
        ve(lambda e, b=bw, half=half:
           e.tensor_copy(WATb[:, half * 512:(half + 1) * 512], b["f32"][:, 0:512]),
           reads=[bw["res"]], writes=[res["WATb"]], tag=f"WATev{half}")
    tap("WATb", WATb, res["WATb"])

    for nm, dst, rkey in [("R2T", R2Tb, "R2Tb"),
                          ("Vf", Vfb, "Vfb"), ("Wf", Wfb, "Wfb"),
                          ("bfw", bfb, "bfb"), ("biw", bib, "bib"),
                          ("Esel", EselB, "EselB")]:
        wload(dst, nm, rkey)
    dma("gpsimd", Vixb, dscr["VIX2"], reads=[res["scr_VIX2"]],
        writes=[res["Vixb"]], tag="ld_VIX2")
    dma("gpsimd", Wiyb, dscr["WIY2"], reads=[res["scr_WIY2"]],
        writes=[res["Wiyb"]], tag="ld_WIY2")
    dma("gpsimd", r1bS, dw["r1b"], writes=[res["r1bS"]])
    dma("gpsimd", r2bS, dw["r2b"], writes=[res["r2bS"]])
    dma("gpsimd", ebS, dw["eb"], writes=[res["ebS"]])
    dma("gpsimd", alphaS, dw["alpha"], writes=[res["alphaS"]])
    dma("gpsimd", nvinvS, dw["nvinv"], writes=[res["nvinvS"]])

    ve(lambda e: e.tensor_add(bias2, r2bS, ebS),
       reads=[res["r2bS"], res["ebS"]], writes=[res["bias2"]], tag="bias2")
    bk = take_bank()
    mm_group([(bk["f32"][0:128, 0:1], ones1F, alphaS)], bk,
             reads=[res["ones1F"], res["alphaS"]], tag="alpha_mm")
    ve(lambda e, b=bk: e.tensor_copy(alphab, b["f32"][0:128, 0:1]),
       reads=[bk["res"]], writes=[res["alphab"]])
    ve(lambda e: e.tensor_scalar_mul(bib, bib, alphab[0:8, 0:1]),
       reads=[res["alphab"]], writes=[res["bib"]], tag="abi")

    for nm in ("WQT", "WKT", "WOT"):
        buf = wstr[0] if nm != "WKT" else wstr[1]
        bres = wstr_res[0] if nm != "WKT" else wstr_res[1]
        dma("gpsimd", buf, dw[nm], writes=[bres], tag=f"precast_ld_{nm}")
        dma("gpsimd", dscr[nm], buf, reads=[bres], writes=[res["scr_" + nm]],
            tag=f"precast_st_{nm}")
    buf, bres = wstr[1], wstr_res[1]
    dma("gpsimd", buf[:, 0:NKRCH * D], dw["Uf"], writes=[bres], tag="precast_ld_Uf")
    dma("gpsimd", buf[:, NKRCH * D:2 * NKRCH * D], dw["Ui"], writes=[bres],
        tag="precast_ld_Ui")
    ve(lambda e, buf=buf: e.tensor_scalar_mul(
        buf[:, NKRCH * D:2 * NKRCH * D], buf[:, NKRCH * D:2 * NKRCH * D], alphab),
       reads=[res["alphab"]], writes=[bres], tag="precast_aUi")
    dma("gpsimd", dscr["UFUI"], buf, reads=[bres], writes=[res["scr_UFUI"]],
        tag="precast_st_UFUI")

    def _steps_of(*names):
        out, seen = [], set()
        for nm_ in names:
            r_ = res[nm_]
            for st_ in r_.writers + r_.readers:
                if id(st_) not in seen:
                    seen.add(id(st_))
                    out.append(st_)
        return out

    fence0 = _steps_of("xt0")
    fence1 = _steps_of("xt1")

    arena = _Arena(arena_base)
    ybufs = [arena.alloc("ybuf0", [128, NCH * SB], F32),
             arena.alloc("ybuf1", [128, NCH * SB], F32)]
    ybufs_res = [res["ybuf0"], res["ybuf1"]]
    yb16 = arena.alloc("yb16", [128, NCH * SB], BF16)
    lntmp = arena.alloc("lntmp", [128, NCH * SB], BF16)
    lntmpb = lntmp
    Qfm = arena.alloc("Qfm", [128, NCH * SB], BF16)
    Kfm = arena.alloc("Kfm", [128, NCH * SB], BF16)
    hsb = [arena.alloc("hsb0", [128, SB], BF16), arena.alloc("hsb1", [128, SB], BF16)]
    hsb_res = [res["hsb0"], res["hsb1"]]
    lgsbF = arena.alloc("lgsbF", [8, SB], BF16)
    lgsbI = arena.alloc("lgsbI", [8, SB], BF16)
    wfmF = arena.alloc("wfmF", [8, SB], BF16)
    wfmI = arena.alloc("wfmI", [8, SB], BF16)
    cumS = arena.alloc("cumS", [128, NKRCH * SB], F32)
    P2F = arena.alloc("P2F", [128, NKRCH * SB], BF16)
    P2I = arena.alloc("P2I", [128, NKRCH * SB], BF16)
    ff16 = arena.alloc("ff16", [128, NCH * SB], BF16)

    def wuse(s, nm):
        idx = list(STREAM_W).index(nm)
        return wstr[idx % 3], wstr_res[idx % 3]

    def wstream(s, nm):
        buf, bres = wuse(s, nm)
        dma("gpsimd", buf, dscr[nm], reads=[res["scr_" + nm]], writes=[bres],
            tag=f"B_wld{s}_{nm}")

    def emit_load(s, extra=()):
        yb = ybufs[s % 2]
        P.add("sync",
              lambda e, yb=yb, s=s: e.dma_start(
                  out=yb.rearrange("p (c t) -> p c t", c=NCH),
                  in_=dx.rearrange("p (c t) -> p c t", c=NCH)[:, :, s * SB:(s + 1) * SB]),
              writes=[ybufs_res[s % 2]], extra_deps=extra, tag=f"B_x{s}")

    def emit_proj(s):
        yb = ybufs[s % 2]
        yb_res = ybufs_res[s % 2]
        for oc in range(NCH):
            bp = take_bank()
            mm_group([(bp["f32"][:, 0:SB], WATb[:, oc * 128:(oc + 1) * 128],
                       vfm[:, s * SB:(s + 1) * SB])], bp,
                     reads=[res["WATb"], res["vfm"]], tag=f"B_proj{s}_{oc}")
            ve(lambda e, b=bp, oc=oc, yb=yb:
               e.tensor_add(yb[:, oc * SB:(oc + 1) * SB],
                            yb[:, oc * SB:(oc + 1) * SB], b["f32"][:, 0:SB]),
               reads=[bp["res"]], writes=[yb_res], tag=f"B_res{s}_{oc}")

    B_banks = {}

    def emit_stats_head(s):
        yb = ybufs[s % 2]
        yb_res = ybufs_res[s % 2]
        ae(lambda e, yb=yb: e.copy(yb16, yb),
           reads=[yb_res], writes=[res["yb16"]], tag=f"B_cast{s}")
        bnm = take_bank(pin=True)
        mm_group([(bnm["f32"][0:1, 0:SB], negidB, rch(yb16, c)) for c in range(NCH)],
                 bnm, reads=[res["yb16"], res["negidB"]], tag=f"B_negm{s}")
        ae(lambda e: e.activation(lntmp, yb16, AF.Square),
           reads=[res["yb16"]], writes=[res["lntmp"]], tag=f"B_sq{s}")
        bms = take_bank(pin=True)
        mm_group([(bms["f32"][0:1, 0:SB], invdB, rch(lntmp, c)) for c in range(NCH)],
                 bms, reads=[res["lntmp"], res["invdB"]], tag=f"B_msq{s}")
        B_banks[s] = (bnm, bms)

    def emit_stats_tail(s):
        yb = ybufs[s % 2]
        yb_res = ybufs_res[s % 2]
        bnm, bms = B_banks.pop(s)
        ae(lambda e, b=bnm: e.activation(st_a, b["f32"][0:1, 0:SB], AF.Square),
           reads=[bnm["res"]], writes=[res["st_a"]], tag=f"B_m2{s}")
        ve(lambda e, b=bms: e.tensor_sub(st_b, b["f32"][0:1, 0:SB], st_a),
           reads=[bms["res"], res["st_a"]], writes=[res["st_b"]])
        ve(lambda e: e.tensor_scalar_add(st_a, st_b, EPS_LN),
           reads=[res["st_b"]], writes=[res["st_a"]])
        ae(lambda e: e.activation(st_a, st_a, AF.Ln), writes=[res["st_a"]])
        ae(lambda e: e.activation(st_c, st_a, AF.Exp, scale=-0.5),
           reads=[res["st_a"]], writes=[res["st_c"]], tag=f"B_s{s}")
        ae(lambda e, b=bnm: e.copy(st_b, b["f32"][0:1, 0:SB]),
           reads=[bnm["res"]], writes=[res["st_b"]], tag=f"B_negm_sb{s}")
        unpin(bnm)
        unpin(bms)

    def emit_stats_bcast(s):
        yb = ybufs[s % 2]
        yb_res = ybufs_res[s % 2]
        bnb = take_bank()
        mm_group([(bnb["f32"][:, 0:SB], ones1F, st_b)], bnb,
                 reads=[res["st_b"], res["ones1F"]], tag=f"B_nmb{s}")
        bsb = take_bank()
        mm_group([(bsb["f32"][:, 0:SB], ones1F, st_c)], bsb,
                 reads=[res["st_c"], res["ones1F"]], tag=f"B_sbc{s}")
        ve(lambda e, b=bnb, yb=yb:
           e.tensor_add(lntmp.rearrange("p (c t) -> p c t", c=NCH),
                        yb.rearrange("p (c t) -> p c t", c=NCH),
                        b["f32"][:, 0:SB].unsqueeze(1).broadcast_to([128, NCH, SB]),
                        ),
           reads=[bnb["res"], yb_res], writes=[res["lntmp"]], tag=f"B_ln1_{s}")
        ve(lambda e, b=bsb:
           e.tensor_mul(yb16.rearrange("p (c t) -> p c t", c=NCH),
                        lntmp.rearrange("p (c t) -> p c t", c=NCH),
                        b["f32"][:, 0:SB].unsqueeze(1).broadcast_to([128, NCH, SB])),
           reads=[bsb["res"], res["lntmp"]], writes=[res["yb16"]], tag=f"B_ln2_{s}")

    def emit_qk(s):
        for (nm, dst, dkey) in (("WQT", Qfm, "Qfm"), ("WKT", Kfm, "Kfm")):
            wt, wres = wuse(s, nm)
            for oc in range(NCH):
                bq = take_bank()
                mm_group([(bq["f32"][:, 0:SB],
                           wt[:, c * D + oc * 128: c * D + (oc + 1) * 128],
                           rch(yb16, c)) for c in range(NCH)], bq,
                         reads=[wres, res["yb16"]], tag=f"B_{nm}{s}_{oc}")
                ae(lambda e, b=bq, dst=dst, oc=oc:
                   e.copy(dst[:, oc * SB:(oc + 1) * SB], b["f32"][:, 0:SB]),
                   reads=[bq["res"]], writes=[res[dkey]], tag=f"B_{nm}ev{s}_{oc}")

    def emit_h(s, bs, wname, lgsb, lgkey):
        wt, wres = wuse(s, wname)
        blg = take_bank(pin=True)
        for jc in range(NCH):
            bh = take_bank()
            mm_group([(bh["f32"][:, 0:SB], wt[:, c * D + jc * 128: c * D + (jc + 1) * 128],
                       rch(Qfm, c)) for c in range(NCH)], bh,
                     reads=[wres, res["Qfm"]], tag=f"B{bs}_h{s}_{jc}")
            hb = hsb[jc % 2]
            hbr = hsb_res[jc % 2]
            ae(lambda e, b=bh, hb=hb, jc=jc:
               e.activation(hb, b["f32"][:, 0:SB], AF.Gelu, bias=r1bS[:, jc:jc + 1]),
               reads=[bh["res"], res["r1bS"]], writes=[hbr], tag=f"B{bs}_hev{s}_{jc}")
            te(lambda e, b=blg, hb=hb, jc=jc:
               e.matmul(b["f32"][0:8, 0:SB], R2Tb[:, jc * 8:(jc + 1) * 8], hb,
                        start=(jc == 0), stop=(jc == NCH - 1)),
               reads=[hbr, res["R2Tb"]], writes=[blg["res"]], tag=f"B{bs}_lg{s}_{jc}")
        ve(lambda e, b=blg, lgsb=lgsb: e.tensor_scalar_add(lgsb, b["f32"][0:8, 0:SB], bias2),
           reads=[blg["res"], res["bias2"]], writes=[res[lgkey]], tag=f"B{bs}_lgev{s}")
        unpin(blg)

    def emit_smx_pre(s, bs, lgsb, lgkey, wts, wtsk):
        btA = take_bank(pin=True)
        for bi in range(SB // 128):
            te(lambda e, b=btA, bi=bi, lgsb=lgsb:
               e.transpose(b["b16"][:, bi * 8:(bi + 1) * 8],
                           lgsb[:, bi * 128:(bi + 1) * 128], identb[0:8, 0:8]),
               reads=[res[lgkey], res["identb"]], writes=[btA["res"]],
               tag=f"B{bs}_lt{s}_{bi}")
        ve(lambda e, b=btA:
           e.reduce_max(smxM, b["b16"][:, 0:32].rearrange("p (c e) -> p c e", e=8),
                        AX.X, negate=True),
           reads=[btA["res"]], writes=[res["smxM"]], tag=f"B{bs}_smax{s}")
        ve(lambda e, b=btA, wts=wts:
           e.tensor_add(wts.rearrange("p (c e) -> p c e", e=8),
                        b["b16"][:, 0:32].rearrange("p (c e) -> p c e", e=8),
                        smxM.unsqueeze(2).broadcast_to([128, 4, 8])),
           reads=[btA["res"], res["smxM"]], writes=[res[wtsk]],
           tag=f"B{bs}_ssub{s}")
        unpin(btA)

    def emit_smx_mid(s, bs, wts, wtsk):
        ae(lambda e, wts=wts: e.activation(wts, wts, AF.Exp, bias=0.0),
           writes=[res[wtsk]], tag=f"B{bs}_wexp{s}")
        for c in range(SB // 128):
            ve(lambda e, c=c, wts=wts: e.reduce_sum(sumM[:, c:c + 1],
                                                    wts[:, c * 8:(c + 1) * 8], AX.X),
               reads=[res[wtsk]], writes=[res["sumM"]], tag=f"B{bs}_ssum{s}_{c}")
        ve(lambda e: e.reciprocal(sumM, sumM), writes=[res["sumM"]],
           tag=f"B{bs}_srec{s}")
        ve(lambda e, s=s: e.tensor_mul(smxQ, nvinvS[:, s * 4:(s + 1) * 4], sumM),
           reads=[res["nvinvS"], res["sumM"]], writes=[res["smxQ"]],
           tag=f"B{bs}_snv{s}")
        ve(lambda e, wts=wts: e.tensor_mul(wts.rearrange("p (c e) -> p c e", e=8),
                                           wts.rearrange("p (c e) -> p c e", e=8),
                                           smxQ.unsqueeze(2).broadcast_to([128, 4, 8])),
           reads=[res["smxQ"]], writes=[res[wtsk]], tag=f"B{bs}_wts{s}")

    def emit_smx_back(s, bs, wfm, wfmk, wts, wtsk):
        btB = take_bank()
        for bi in range(SB // 128):
            te(lambda e, b=btB, bi=bi, wts=wts:
               e.transpose(b["b16"][0:8, bi * 128:(bi + 1) * 128],
                           wts[:, bi * 8:(bi + 1) * 8], identb),
               reads=[res[wtsk], res["identb"]], writes=[btB["res"]],
               tag=f"B{bs}_wt{s}_{bi}")
        ae(lambda e, b=btB, wfm=wfm: e.copy(wfm, b["b16"][0:8, 0:SB]),
           reads=[btB["res"]], writes=[res[wfmk]], tag=f"B{bs}_wfm{s}")

    def emit_ywxv(s, bs, Vp, Vpk, Wp, Wpk, carry, carryk, hook=None):
        for kc in range(NKRCH):
            byw = take_bank()
            mm_group([(byw["f32"][:, 0:SB],
                       Wp[:, c * KR + kc * 128: c * KR + (kc + 1) * 128],
                       rch(Kfm, c)) for c in range(NCH)], byw,
                     reads=[res[Wpk], res["Kfm"]], tag=f"B{bs}_yw{s}_{kc}")
            ve(lambda e, b=byw, kc=kc, carry=carry, s=s:
               e.tensor_tensor_scan(rch(cumS, kc), onesColF.broadcast_to([128, SB]),
                                    b["f32"][:, 0:SB],
                                    carry[:, kc:kc + 1] if s > 0 else 0.0,
                                    ALU.mult, ALU.add),
               reads=[byw["res"], res["onesColF"]] + ([res[carryk]] if s > 0 else []),
               writes=[res["cumS"]], tag=f"B{bs}_scan{s}_{kc}")
            ve(lambda e, kc=kc, carry=carry:
               e.tensor_copy(carry[:, kc:kc + 1], rch(cumS, kc)[:, SB - 1:SB]),
               reads=[res["cumS"]], writes=[res[carryk]], tag=f"B{bs}_carry{s}_{kc}")
            bxv = take_bank()
            mm_group([(bxv["f32"][:, 0:SB],
                       Vp[:, c * KR + kc * 128: c * KR + (kc + 1) * 128],
                       rch(Qfm, c)) for c in range(NCH)], bxv,
                     reads=[res[Vpk], res["Qfm"]], tag=f"B{bs}_xv{s}_{kc}")
            if kc == 0 and hook is not None:
                hook()
            ve(lambda e, b=bxv, kc=kc: e.tensor_tensor(rch(cumS, kc), rch(cumS, kc),
                                                       b["f32"][:, 0:SB], ALU.mult),
               reads=[bxv["res"]], writes=[res["cumS"]], tag=f"B{bs}_p1{s}_{kc}")
            bwr = take_bank()
            mm_group([(bwr["f32"][:, 0:SB], EselB[:, kc * 128:(kc + 1) * 128], wfmF if bs == "F" else wfmI)],
                     bwr, reads=[res["EselB"], res["wfmF" if bs == "F" else "wfmI"]],
                     tag=f"B{bs}_wrep{s}_{kc}")
            P2, P2k = (P2F, "P2F") if bs == "F" else (P2I, "P2I")
            ve(lambda e, b=bwr, kc=kc, P2=P2:
               e.tensor_mul(rch(P2, kc), rch(cumS, kc), b["f32"][:, 0:SB]),
               reads=[res["cumS"], bwr["res"]], writes=[res[P2k]],
               tag=f"B{bs}_p2{s}_{kc}")

    def emit_out(s):
        wt, wres = wuse(s, "UFUI")
        ufo = 0
        uio = NKRCH * D
        for dc in range(NCH):
            bo = take_bank()
            ops = []
            out = bo["f32"][:, 0:SB]
            for kc in range(NKRCH):
                ops.append((out, wt[:, ufo + kc * D + dc * 128: ufo + kc * D + (dc + 1) * 128],
                            rch(P2F, kc)))
            ops.append((out, bfb[0:8, dc * 128:(dc + 1) * 128], wfmF))
            for kc in range(NKRCH):
                ops.append((out, wt[:, uio + kc * D + dc * 128: uio + kc * D + (dc + 1) * 128],
                            rch(P2I, kc)))
            ops.append((out, bib[0:8, dc * 128:(dc + 1) * 128], wfmI))
            mm_group(ops, bo,
                     reads=[wres, res["bfb"], res["bib"],
                            res["P2F"], res["P2I"], res["wfmF"], res["wfmI"]],
                     tag=f"B_out{s}_{dc}")
            ae(lambda e, b=bo, dc=dc: e.copy(ff16[:, dc * SB:(dc + 1) * SB],
                                             b["f32"][:, 0:SB]),
               reads=[bo["res"]], writes=[res["ff16"]], tag=f"B_outev{s}_{dc}")

    def emit_wo(s):
        yb = ybufs[s % 2]
        yb_res = ybufs_res[s % 2]
        wt, wres = wuse(s, "WOT")
        for oc in range(NCH):
            bw = take_bank()
            mm_group([(bw["f32"][:, 0:SB], wt[:, c * D + oc * 128: c * D + (oc + 1) * 128],
                       rch(ff16, c)) for c in range(NCH)], bw,
                     reads=[wres, res["ff16"]], tag=f"B_wo{s}_{oc}")
            ve(lambda e, b=bw, oc=oc, yb=yb:
               e.tensor_add(yb[:, oc * SB:(oc + 1) * SB],
                            yb[:, oc * SB:(oc + 1) * SB], b["f32"][:, 0:SB]),
               reads=[bw["res"]], writes=[yb_res], tag=f"B_wores{s}_{oc}")
        dma("sync", dout.rearrange("p (c t) -> p c t", c=NCH)[:, :, s * SB:(s + 1) * SB],
            yb.rearrange("p (c t) -> p c t", c=NCH),
            reads=[yb_res], writes=[out_res], tag=f"B_store{s}")

    emit_load(0, extra=fence0)
    wstream(0, "WQT")
    wstream(0, "WKT")
    wstream(0, "R1T")
    emit_proj(0)
    emit_stats_head(0)
    emit_stats_tail(0)
    emit_stats_bcast(0)
    for s in range(NSB):
        emit_qk(s)
        wstream(s, "R1TI")
        if s + 1 < NSB:
            emit_load(s + 1, extra=(fence1 if s + 1 == 1 else ()))
            emit_proj(s + 1)
        emit_h(s, "F", "R1T", lgsbF, "lgsbF")
        emit_smx_pre(s, "F", lgsbF, "lgsbF", wtsBF, "wtsBF")
        wstream(s, "UFUI")
        wstream(s, "WOT")
        emit_h(s, "I", "R1TI", lgsbI, "lgsbI")
        emit_smx_pre(s, "I", lgsbI, "lgsbI", wtsBI, "wtsBI")
        emit_smx_mid(s, "F", wtsBF, "wtsBF")
        if s + 1 < NSB:
            wstream(s + 1, "WQT")
        emit_ywxv(s, "F", Vfb, "Vfb", Wfb, "Wfb", carryF, "carryF",
                  hook=lambda s=s: (emit_smx_back(s, "F", wfmF, "wfmF",
                                                  wtsBF, "wtsBF"),
                                    emit_smx_mid(s, "I", wtsBI, "wtsBI")))
        emit_ywxv(s, "I", Vixb, "Vixb", Wiyb, "Wiyb", carryI, "carryI",
                  hook=lambda s=s: emit_smx_back(s, "I", wfmI, "wfmI",
                                                 wtsBI, "wtsBI"))
        if s + 1 < NSB:
            emit_stats_head(s + 1)
            emit_stats_tail(s + 1)
        emit_out(s)
        if s + 1 < NSB:
            wstream(s + 1, "WKT")
            emit_stats_bcast(s + 1)
        emit_wo(s)
        if s + 1 < NSB:
            wstream(s + 1, "R1T")
        if s == 0:
            tap("y0", ybufs[0], ybufs_res[0])
            tap("Q0", Qfm, res["Qfm"])
            tap("cum0", cumS, res["cumS"])

    P.add("sync", lambda e: e.nop(), reads=[out_res], tag="final_wait")

    P.finalize()

    with (nc.semaphore("sem_t") as s_t, nc.semaphore("sem_v") as s_v,
          nc.semaphore("sem_a") as s_a, nc.semaphore("sem_s") as s_s,
          nc.semaphore("sem_g") as s_g, nc.semaphore("sem_d") as s_d,
          nc.Block() as block):
        sems = {"tensor": s_t, "vector": s_v, "scalar": s_a,
                "sync": s_s, "gpsimd": s_g, "sdma": s_d}

        @block.tensor
        def _(eng):
            P.emit_engine("tensor", eng, sems)

        @block.vector
        def _(eng):
            P.emit_engine("vector", eng, sems)

        @block.scalar
        def _(eng):
            P.emit_engine("scalar", eng, sems)

        @block.sync
        def _(eng):
            P.emit_engine("sync", eng, sems)

        @block.gpsimd
        def _(eng):
            P.emit_engine("gpsimd", eng, sems)

    return nc, tap_drams


def kernel(**inputs):
    xs, w = _host_prep(inputs)
    taps = tuple(_DEBUG_TAPS)
    nc, tap_drams = build_program(taps)
    in_maps = []
    for b in range(8):
        m = {"xT": xs[b]}
        m.update(w)
        in_maps.append(m)
    r = run_bass_kernel_spmd(nc, in_maps, core_ids=list(range(8)))
    outs = []
    for b in range(8):
        o = np.asarray(r.results[b]["outT"], dtype=np.float32)
        outs.append(o.reshape(128, NCH, T).transpose(1, 0, 2).reshape(D, T).T)
    if taps:
        kernel.last_taps = [{k: np.asarray(r.results[b]["tap_" + k]) for k in taps
                             if ("tap_" + k) in r.results[b]} for b in range(8)]
    return np.ascontiguousarray(np.stack(outs, axis=0))


# version 49
# speedup vs baseline: 1.0184x; 1.0122x over previous
import numpy as np

import concourse.bass as bass
import concourse.mybir as mybir
import concourse.bass_utils as _bu
from concourse.bass_utils import run_bass_kernel_spmd

if not getattr(_bu, "_ldw_patch", False):
    _orig_run_command = _bu.run_command

    def _run_command_ldw(argv, **kw):
        if isinstance(argv, list):
            argv = ["--enable-ldw-opt=true" if a == "--enable-ldw-opt=false" else a
                    for a in argv]
        return _orig_run_command(argv, **kw)

    _bu.run_command = _run_command_ldw
    _bu._ldw_patch = True

F32 = mybir.dt.float32
BF16 = mybir.dt.bfloat16
AF = mybir.ActivationFunctionType
ALU = mybir.AluOpType
AX = mybir.AxisListType

D = 1024
T = 4096
NCH = 8
HEADS = 8
KEXP = 8
R = 64
KR = 512
NKRCH = 4
TILE = 512
NT = T // TILE
SB = 512
NSB = T // SB
EPS_LN = 1e-5
EPS_RMS = 1e-6

_ENGINES = ("tensor", "vector", "scalar", "sync", "gpsimd")
_DMA_ENGINES = ("sync", "gpsimd")

_DEBUG_TAPS = []


class _Step:
    __slots__ = ("eng", "emit", "deps", "inc_needed", "sem_val", "tag", "semkey")

    def __init__(self, eng, emit, tag="", semkey=None):
        self.eng = eng
        self.emit = emit
        self.deps = []
        self.inc_needed = eng in _DMA_ENGINES or semkey == "sdma"
        self.sem_val = None
        self.tag = tag
        self.semkey = semkey or eng


class _Res:
    __slots__ = ("writers", "readers", "rsw", "name")

    def __init__(self, name=""):
        self.writers = []
        self.readers = []
        self.rsw = False
        self.name = name


class _Plan:
    def __init__(self):
        self.steps = []

    def add(self, eng, emit, reads=(), writes=(), tag="", extra_deps=(), semkey=None):
        st = _Step(eng, emit, tag, semkey)
        deps = []
        for r in reads:
            deps.extend(r.writers)
        for w in writes:
            deps.extend(w.writers)
            deps.extend(w.readers)
        deps.extend(extra_deps)
        seen = set()
        for d in deps:
            if d is st or id(d) in seen:
                continue
            seen.add(id(d))
            st.deps.append(d)
        for r in reads:
            r.readers.append(st)
            r.rsw = True
        for w in writes:
            if w.rsw:
                w.writers = [st]
                w.readers = []
                w.rsw = False
            else:
                w.writers.append(st)
        self.steps.append(st)
        return st

    def finalize(self):
        for st in self.steps:
            for d in st.deps:
                d.inc_needed = True
        counters = {e: 0 for e in _ENGINES}
        counters["sdma"] = 0
        for st in self.steps:
            if st.inc_needed:
                dma = st.eng in _DMA_ENGINES or st.semkey == "sdma"
                counters[st.semkey] += 16 if dma else 1
                st.sem_val = counters[st.semkey]

    def emit_engine(self, name, eng, sems):
        watermark = {}
        for st in self.steps:
            if st.eng != name:
                continue
            for d in st.deps:
                if watermark.get(d.semkey, 0) >= d.sem_val:
                    continue
                eng.wait_ge(sems[d.semkey], d.sem_val)
                watermark[d.semkey] = d.sem_val
            insts = st.emit(eng)
            if st.inc_needed:
                if not isinstance(insts, (list, tuple)):
                    insts = [insts]
                if name in _DMA_ENGINES:
                    for i in insts:
                        i.then_inc(sems[name], 16)
                    eng.wait_ge(sems[name], st.sem_val)
                    watermark[name] = st.sem_val
                elif st.semkey == "sdma":
                    for i in insts:
                        i.then_inc(sems["sdma"], 16)
                else:
                    insts[-1].then_inc(sems[name], 1)


def _pack_cm(m):
    r, c = m.shape
    return np.ascontiguousarray(
        m.reshape(r // 128, 128, c).transpose(1, 0, 2).reshape(128, -1))


def _host_prep(inp):
    g = {k: np.asarray(v, dtype=np.float32) for k, v in inp.items()}
    w = {}
    x = g["L0_attn_x"]
    xs = [_pack_cm(np.ascontiguousarray(x[b].T)) for b in range(8)]
    w["wqkvT"] = _pack_cm(np.ascontiguousarray(g["L0_attn_w_qkv"].T))
    w["woutT"] = _pack_cm(np.ascontiguousarray(g["L0_attn_w_out"].T))
    w["WQT"] = _pack_cm(np.ascontiguousarray(g["L0_ffn_W_Q"].T))
    w["WKT"] = _pack_cm(np.ascontiguousarray(g["L0_ffn_W_K"].T))
    w["WIN"] = _pack_cm(np.ascontiguousarray(g["L0_ffn_W_inv"]))
    w["R1T"] = _pack_cm(np.ascontiguousarray(g["L0_ffn_router_w1"].T))
    w["WOT"] = _pack_cm(np.ascontiguousarray(g["L0_ffn_W_O"].T))
    w["R2T"] = _pack_cm(np.ascontiguousarray(g["L0_ffn_router_w2"].T))
    w["Vf"] = _pack_cm(g["L0_ffn_V_fwd"].transpose(1, 0, 2).reshape(D, KR))
    w["Wf"] = _pack_cm(g["L0_ffn_W_fwd"].transpose(1, 0, 2).reshape(D, KR))
    w["Vix"] = _pack_cm(g["L0_ffn_W_inv_exp"].transpose(1, 0, 2).reshape(D, KR))
    w["Wiy"] = _pack_cm(g["L0_ffn_V_inv"].transpose(1, 0, 2).reshape(D, KR))
    w["Uf"] = _pack_cm(g["L0_ffn_U_fwd"].transpose(0, 2, 1).reshape(KR, D))
    w["Ui"] = _pack_cm(g["L0_ffn_U_inv"].transpose(0, 2, 1).reshape(KR, D))
    w["bfw"] = np.ascontiguousarray(g["L0_ffn_b_fwd"])
    w["biw"] = np.ascontiguousarray(g["L0_ffn_b_inv"])
    w["gamma"] = np.ascontiguousarray(g["L0_attn_norm_weight"].reshape(NCH, 128).T)
    w["r1b"] = np.ascontiguousarray(g["L0_ffn_router_b1"].reshape(NCH, 128).T)
    w["temp"] = np.ascontiguousarray(g["L0_attn_temperature"].reshape(8, 1))
    w["r2b"] = np.ascontiguousarray(g["L0_ffn_router_b2"].reshape(8, 1))
    w["eb"] = np.ascontiguousarray(g["L0_ffn_expert_bias"].reshape(8, 1))
    w["alpha"] = np.ascontiguousarray(g["L0_ffn_alpha_bi"].reshape(1, 1))
    w["ident"] = np.eye(128, dtype=np.float32)
    mask = np.full((128, 128), -30000.0, dtype=np.float32)
    for h in range(HEADS):
        mask[16 * h:16 * h + 16, 16 * h:16 * h + 16] = 0.0
    w["mask"] = mask
    nv = np.zeros((128, 32), dtype=np.float32)
    for blk in range(32):
        nv[:, blk] = 1.0 / (blk * 128 + np.arange(128) + 1.0)
    w["nvinv"] = nv
    esel = np.zeros((8, 512), dtype=np.float32)
    for c in range(4):
        for p in range(128):
            esel[2 * c + p // 64, c * 128 + p] = 1.0
    w["Esel"] = esel
    hsel = np.zeros((8, 128), dtype=np.float32)
    for p in range(128):
        hsel[p // 16, p] = 1.0
    w["hsel"] = hsel
    return xs, w


def build_program(taps=()):
    nc = bass.Bass("TRN2", target_bir_lowering=False)
    P = _Plan()

    dx = nc.dram_tensor("xT", [128, NCH * T], F32, kind="ExternalInput").ap()
    dout = nc.dram_tensor("outT", [128, NCH * T], F32, kind="ExternalOutput").ap()
    dw = {}
    for name, shape in [
        ("wqkvT", [128, NCH * 384]), ("woutT", [128, 1024]),
        ("WQT", [128, NCH * D]), ("WKT", [128, NCH * D]), ("WIN", [128, NCH * D]),
        ("R1T", [128, NCH * D]), ("WOT", [128, NCH * D]), ("R2T", [128, 64]),
        ("Vf", [128, NCH * KR]), ("Wf", [128, NCH * KR]),
        ("Vix", [128, NCH * KR]), ("Wiy", [128, NCH * KR]),
        ("Uf", [128, NKRCH * D]), ("Ui", [128, NKRCH * D]),
        ("bfw", [8, 1024]), ("biw", [8, 1024]),
        ("gamma", [128, 8]), ("r1b", [128, 8]),
        ("temp", [8, 1]), ("r2b", [8, 1]), ("eb", [8, 1]), ("alpha", [1, 1]),
        ("ident", [128, 128]), ("mask", [128, 128]), ("nvinv", [128, 32]),
        ("Esel", [8, 512]), ("hsel", [8, 128]),
    ]:
        dw[name] = nc.dram_tensor(name, shape, F32, kind="ExternalInput").ap()

    tap_drams = {}
    res = {}
    STREAM_W = ("WQT", "WKT", "R1T", "R1TI", "UFUI", "WOT")
    dscr = {}
    for nm in STREAM_W:
        dscr[nm] = nc.dram_tensor("scr_" + nm, [128, NCH * D], BF16).ap()
        res["scr_" + nm] = _Res("scr_" + nm)
    for nm in ("VIX2", "WIY2"):
        dscr[nm] = nc.dram_tensor("scr_" + nm, [128, NCH * KR], BF16).ap()
        res["scr_" + nm] = _Res("scr_" + nm)

    def sb(name, shape, dtype):
        h = nc.alloc_sbuf_tensor(name, shape, dtype)
        res[name] = _Res(name)
        return h.ap()

    WATb = sb("WATb", [128, 1024], BF16)
    vfm = sb("vfm", [128, T], BF16)
    wstr = [sb("wsA", [128, NCH * D], BF16), sb("wsB", [128, NCH * D], BF16),
            sb("wsC", [128, NCH * D], BF16)]
    wstr_res = [res["wsA"], res["wsB"], res["wsC"]]
    R2Tb = sb("R2Tb", [128, 64], BF16)
    Vfb = sb("Vfb", [128, NCH * KR], BF16)
    Wfb = sb("Wfb", [128, NCH * KR], BF16)
    Vixb = sb("Vixb", [128, NCH * KR], BF16)
    Wiyb = sb("Wiyb", [128, NCH * KR], BF16)
    bfb = sb("bfb", [8, 1024], BF16)
    bib = sb("bib", [8, 1024], BF16)
    identb = sb("identb", [128, 128], BF16)
    nvinvS = sb("nvinvS", [128, 32], F32)
    EselB = sb("EselB", [8, 512], BF16)
    gammaS = sb("gammaS", [128, 8], F32)
    r1bS = sb("r1bS", [128, 8], F32)
    tempS = sb("tempS", [8, 1], F32)
    r2bS = sb("r2bS", [8, 1], F32)
    ebS = sb("ebS", [8, 1], F32)
    alphaS = sb("alphaS", [1, 1], F32)
    onesB = sb("onesB", [128, 1], BF16)
    negidB = sb("negidB", [128, 1], BF16)
    invdB = sb("invdB", [128, 1], BF16)
    ones1F = sb("ones1F", [1, 128], F32)
    onesColF = sb("onesColF", [128, 1], F32)
    gSqkvB = sb("gSqkvB", [1, 384], BF16)
    bias2 = sb("bias2", [8, 1], F32)
    ethb = sb("ethb", [128, 1], F32)
    alphab = sb("alphab", [128, 1], F32)
    etmpb = sb("etmpb", [8, 1], BF16)
    negmB = sb("negmB", [1, TILE], BF16)
    negmB1 = sb("negmB1", [1, TILE], BF16)
    maskS = sb("maskS", [128, 128], F32)
    hselB = sb("hselB", [8, 128], BF16)
    rkS = sb("rkS", [128, 1], F32)
    st_a = sb("st_a", [1, TILE], F32)
    st_b = sb("st_b", [1, TILE], F32)
    st_c = sb("st_c", [1, TILE], F32)
    sqA = sb("sqA", [128, 1], F32)
    nrmA = sb("nrmA", [128, 1], F32)
    sclA = sb("sclA", [128, 1], F32)
    carryF = sb("carryF", [128, 4], F32)
    carryI = sb("carryI", [128, 4], F32)
    smx1 = sb("smx1", [128, 1], F32)
    smx2 = sb("smx2", [128, 1], F32)
    smxM = sb("smxM", [128, 4], F32)
    sumM = sb("sumM", [128, 4], F32)
    smxQ = sb("smxQ", [128, 4], F32)
    wtsBF = sb("wtsBF", [128, 32], BF16)
    wtsBI = sb("wtsBI", [128, 32], BF16)
    out_res = _Res("OUT")
    res["OUT"] = out_res

    SBUF_TOP = 229344
    arena_base = nc._sbuf_addr_for_side("left")
    arena_base = (arena_base + 31) // 32 * 32

    class _Arena:
        def __init__(self, base):
            self.off = base

        def alloc(self, name, shape, dtype):
            per_part = int(np.prod(shape[1:])) * mybir.dt.size(dtype)
            off = (self.off + 31) // 32 * 32
            assert off + per_part <= SBUF_TOP, (
                f"arena overflow: {name} at {off}+{per_part} > {SBUF_TOP}")
            h = nc.alloc_sbuf_tensor_at(name, shape, dtype, offset=off)
            self.off = off + per_part
            res[name] = _Res(name)
            return h.ap()

    banks = []
    for i in range(8):
        h = nc.alloc_psum_tensor(f"pb{i}", [128, 512], F32)
        banks.append({"f32": h.ap(), "b16": h.ap().bitcast(BF16),
                      "res": _Res(f"bank{i}"), "i": i})
    bank_rr = [0]
    pinned = set()

    def take_bank(pin=False):
        for _ in range(8):
            b = banks[bank_rr[0] % 8]
            bank_rr[0] += 1
            if b["i"] not in pinned:
                if pin:
                    pinned.add(b["i"])
                return b
        raise RuntimeError("all psum banks pinned")

    def unpin(b):
        pinned.discard(b["i"])

    def dma(queue, out_ap, in_ap, reads=(), writes=(), tag="", extra=()):
        return P.add(queue, lambda e, o=out_ap, i=in_ap: e.dma_start(out=o, in_=i),
                     reads=reads, writes=writes, tag=tag, extra_deps=extra,
                     semkey="sdma" if queue == "scalar" else None)

    def tap(name, ap_, rd):
        if name not in taps:
            return
        d = nc.dram_tensor("tap_" + name, list(ap_.shape), ap_.dtype,
                           kind="ExternalOutput").ap()
        tap_drams[name] = d
        dma("gpsimd", d, ap_, reads=[rd], tag="tap_" + name)

    def mm_group(ops, bank, reads, tag=""):
        n = len(ops)

        def em(eng, ops=ops, n=n):
            insts = []
            for i, (o, l, r) in enumerate(ops):
                insts.append(eng.matmul(o, l, r, start=(i == 0), stop=(i == n - 1)))
            return insts
        return P.add("tensor", em, reads=reads, writes=[bank["res"]], tag=tag)

    def ve(emit, reads=(), writes=(), tag=""):
        return P.add("vector", emit, reads=reads, writes=writes, tag=tag)

    def ae(emit, reads=(), writes=(), tag=""):
        return P.add("scalar", emit, reads=reads, writes=writes, tag=tag)

    def te(emit, reads=(), writes=(), tag=""):
        return P.add("tensor", emit, reads=reads, writes=writes, tag=tag)

    def ge(emit, reads=(), writes=(), tag=""):
        return P.add("gpsimd", emit, reads=reads, writes=writes, tag=tag)

    def rch(ap_, c, w=SB):
        return ap_.rearrange("p (c t) -> p c t", c=ap_.shape[1] // w)[:, c, :]

    arena = _Arena(arena_base)
    xt0 = arena.alloc("xt0", [128, NCH * TILE], F32)
    xt1 = arena.alloc("xt1", [128, NCH * TILE], F32)
    qfm = arena.alloc("qfm", [128, T], BF16)
    kfm = arena.alloc("kfm", [128, T], BF16)
    xb16 = arena.alloc("xb16", [128, NCH * TILE], BF16)
    xb16b = arena.alloc("xb16b", [128, NCH * TILE], BF16)
    xb16c = arena.alloc("xb16c", [128, NCH * TILE], BF16)
    xsq16 = arena.alloc("xsq16", [128, NCH * TILE], BF16)
    ssbA = arena.alloc("ssbA", [128, TILE], F32)
    ssbA1 = arena.alloc("ssbA1", [128, TILE], F32)
    qtm = arena.alloc("qtm", [128, 128], BF16)
    ktm = arena.alloc("ktm", [128, 128], BF16)
    qtm2 = arena.alloc("qtm2", [128, 128], BF16)
    ktm2 = arena.alloc("ktm2", [128, 128], BF16)
    simsb = arena.alloc("simsb", [128, 128], F32)
    Ab = arena.alloc("Ab", [128, 128], BF16)
    identf = arena.alloc("identf", [128, 128], F32)
    nsq = arena.alloc("nsq", [128, TILE], BF16)
    sqQa = arena.alloc("sqQa", [128, 1], F32)
    sqKa = arena.alloc("sqKa", [128, 1], F32)
    nred = arena.alloc("nred", [128, 1], F32)
    st_a2 = arena.alloc("st_a2", [1, TILE], F32)
    st_b2 = arena.alloc("st_b2", [1, TILE], F32)
    st_c2 = arena.alloc("st_c2", [1, TILE], F32)
    wqkvTb = arena.alloc("wqkvTb", [128, NCH * 384], BF16)
    woutTb = arena.alloc("woutTb", [128, 1024], BF16)


    def wload(dst_ap, dname, rkey):
        dma("gpsimd", dst_ap, dw[dname], writes=[res[rkey]], tag="ld_" + dname)

    dma("gpsimd", wstr[0], dw["WIN"], writes=[wstr_res[0]], tag="prep_ld_WIN")
    dma("gpsimd", qfm, dw["Vix"], writes=[res["qfm"]], tag="prep_ld_Vix")
    dma("gpsimd", wstr[1], dw["R1T"], writes=[wstr_res[1]], tag="prep_ld_R1T")
    wload(wqkvTb, "wqkvT", "wqkvTb")
    wload(identb, "ident", "identb")
    wload(hselB, "hsel", "hselB")
    wload(woutTb, "woutT", "woutTb")
    dma("gpsimd", maskS, dw["mask"], writes=[res["maskS"]])
    dma("gpsimd", identf, dw["ident"], writes=[res["identf"]])
    dma("gpsimd", gammaS, dw["gamma"], writes=[res["gammaS"]])
    dma("gpsimd", tempS, dw["temp"], writes=[res["tempS"]])

    for ap_, val, rk in ((sqQa, 0.0, "sqQa"), (sqKa, 0.0, "sqKa"),
                         (onesB, 1.0, "onesB"), (negidB, -1.0 / 1024.0, "negidB"),
                         (invdB, 1.0 / 1024.0, "invdB"), (ones1F, 1.0, "ones1F"),
                         (onesColF, 1.0, "onesColF"), (carryF, 0.0, "carryF"),
                         (carryI, 0.0, "carryI")):
        ve(lambda e, ap_=ap_, val=val: e.memset(ap_, val),
           writes=[res[rk]], tag="memset_" + rk)

    xts = [xt0, xt1]
    xts_res = [res["xt0"], res["xt1"]]
    xb16s = [xb16, xb16b, xb16c]
    xb16s_res = [res["xb16"], res["xb16b"], res["xb16c"]]
    ssbAs = [ssbA, ssbA1]
    ssbAs_res = [res["ssbA"], res["ssbA1"]]
    negmBs = [negmB, negmB1]
    negmBs_res = [res["negmB"], res["negmB1"]]
    qkv_dst = [(qfm, res["qfm"]), (kfm, res["kfm"]), (vfm, res["vfm"])]

    A_banks = {}

    def emit_A_head_dma(t):
        xt = xts[t % 2]
        xt_res = xts_res[t % 2]
        xb = xb16s[t % 3]
        xb_res = xb16s_res[t % 3]
        dma("sync", xt.rearrange("p (c t) -> p c t", c=NCH),
            dx.rearrange("p (c t) -> p c t", c=NCH)[:, :, t * TILE:(t + 1) * TILE],
            writes=[xt_res], tag=f"A_x{t}")
        ae(lambda e, xt=xt, xb=xb: e.copy(xb, xt),
           reads=[xt_res], writes=[xb_res], tag=f"A_cast{t}")

    def emit_A_head_mm(t):
        xb = xb16s[t % 3]
        xb_res = xb16s_res[t % 3]
        bnm = take_bank(pin=True)
        mm_group([(bnm["f32"][0:1, 0:TILE], negidB, rch(xb, c, TILE))
                  for c in range(NCH)], bnm,
                 reads=[xb_res, res["negidB"]], tag=f"A_negm{t}")
        ae(lambda e, xb=xb: e.activation(xsq16, xb, AF.Square),
           reads=[xb_res], writes=[res["xsq16"]], tag=f"A_sq{t}")
        mm_group([(bnm["f32"][32:33, 0:TILE], invdB, rch(xsq16, c, TILE))
                  for c in range(NCH)], bnm,
                 reads=[res["xsq16"], res["invdB"]], tag=f"A_msq{t}")
        A_banks[t] = bnm

    def emit_A_head(t):
        emit_A_head_dma(t)
        emit_A_head_mm(t)


    emit_A_head(0)
    emit_A_head(1)

    def prep_vw(nm, src_ap, src_res, stage_ap, stage_res):
        for a in range(NCH):
            bk_ = take_bank()
            mm_group([(bk_["f32"][:, 0:KR],
                       wstr[0][:, c * D + a * 128:c * D + (a + 1) * 128],
                       src_ap.rearrange("p (c t) -> p c t", c=NCH)[:, c, :])
                      for c in range(NCH)], bk_,
                     reads=[wstr_res[0], src_res], tag=f"prep_{nm}_{a}")
            ae(lambda e, b=bk_, a=a, stage_ap=stage_ap:
               e.copy(stage_ap[:, a * KR:(a + 1) * KR], b["f32"][:, 0:KR]),
               reads=[bk_["res"]], writes=[stage_res], tag=f"prep_{nm}ev_{a}")

    dma("gpsimd", xb16c, dw["Wiy"], writes=[res["xb16c"]], tag="prep_ld_Wiy")
    prep_vw("VIX2", qfm, res["qfm"], kfm, res["kfm"])
    xt1b = xt1.bitcast(BF16)
    for a in range(NCH):
        for hf in range(2):
            bk_ = take_bank()
            mm_group([(bk_["f32"][:, 0:512],
                       wstr[0][:, c * D + a * 128:c * D + (a + 1) * 128],
                       wstr[1][:, c * D + hf * 512:c * D + (hf + 1) * 512])
                      for c in range(NCH)], bk_,
                     reads=[wstr_res[0], wstr_res[1]], tag=f"prep_R1TI_{a}_{hf}")
            ae(lambda e, b=bk_, a=a, hf=hf:
               e.copy(xt1b[:, a * D + hf * 512:a * D + (hf + 1) * 512],
                      b["f32"][:, 0:512]),
               reads=[bk_["res"]], writes=[res["xt1"]], tag=f"prep_R1TIev_{a}_{hf}")
    prep_vw("WIY2", xb16c, res["xb16c"], qfm, res["qfm"])
    dma("gpsimd", dscr["R1T"], wstr[1], reads=[wstr_res[1]],
        writes=[res["scr_R1T"]], tag="precast_st_R1T")
    dma("gpsimd", dscr["R1TI"], xt1b, reads=[res["xt1"]],
        writes=[res["scr_R1TI"]], tag="prep_st_R1TI")
    dma("gpsimd", dscr["VIX2"], kfm, reads=[res["kfm"]], writes=[res["scr_VIX2"]],
        tag="prep_st_VIX2")
    dma("gpsimd", dscr["WIY2"], qfm, reads=[res["qfm"]], writes=[res["scr_WIY2"]],
        tag="prep_st_WIY2")

    for c in range(NCH):
        ve(lambda e, c=c: e.tensor_scalar_mul(wqkvTb[:, c * 384:(c + 1) * 384],
                                              wqkvTb[:, c * 384:(c + 1) * 384],
                                              gammaS[:, c:c + 1]),
           reads=[res["gammaS"]], writes=[res["wqkvTb"]], tag=f"gt_scale{c}")

    bk = take_bank()
    mm_group([(bk["f32"][0:1, 0:384], onesB, wqkvTb[:, c * 384:(c + 1) * 384])
              for c in range(NCH)], bk,
             reads=[res["wqkvTb"], res["onesB"]], tag="gSqkv_mm")
    ae(lambda e, b=bk: e.copy(gSqkvB, b["f32"][0:1, 0:384]),
       reads=[bk["res"]], writes=[res["gSqkvB"]], tag="gSqkv_cp")

    ae(lambda e: e.activation(tempS, tempS, AF.Exp), writes=[res["tempS"]], tag="exp_temp")
    ae(lambda e: e.copy(etmpb, tempS), reads=[res["tempS"]], writes=[res["etmpb"]])
    bk = take_bank()
    mm_group([(bk["f32"][0:128, 0:1], hselB, etmpb)], bk,
             reads=[res["hselB"], res["etmpb"]], tag="ethb_mm")
    ve(lambda e, b=bk: e.tensor_copy(ethb, b["f32"][0:128, 0:1]),
       reads=[bk["res"]], writes=[res["ethb"]])


    def emit_A_tail(t):
        bnm = A_banks.pop(t)
        bms = bnm
        sA, sB, sC = ((st_a, st_b, st_c) if t % 2 == 0 else (st_a2, st_b2, st_c2))
        rA, rB, rC = (("st_a", "st_b", "st_c") if t % 2 == 0
                      else ("st_a2", "st_b2", "st_c2"))
        ae(lambda e, b=bnm, sA=sA: e.activation(sA, b["f32"][0:1, 0:TILE], AF.Square),
           reads=[bnm["res"]], writes=[res[rA]], tag=f"A_m2{t}")
        ve(lambda e, b=bms, sA=sA, sB=sB: e.tensor_sub(sB, b["f32"][32:33, 0:TILE], sA),
           reads=[bms["res"], res[rA]], writes=[res[rB]], tag=f"A_var{t}")
        ve(lambda e, sA=sA, sB=sB: e.tensor_scalar(sA, sB, 1.0 + EPS_RMS,
                                                   EPS_LN * EPS_RMS,
                                                   ALU.mult, ALU.add),
           reads=[res[rB]], writes=[res[rA]])
        ae(lambda e, sA=sA: e.activation(sA, sA, AF.Ln), writes=[res[rA]])
        ae(lambda e, sA=sA, sC=sC: e.activation(sC, sA, AF.Exp, scale=-0.5),
           reads=[res[rA]], writes=[res[rC]], tag=f"A_s{t}")
        ae(lambda e, b=bnm, t=t: e.copy(negmBs[t % 2], b["f32"][0:1, 0:TILE]),
           reads=[bnm["res"]], writes=[negmBs_res[t % 2]], tag=f"A_negmB{t}")
        bsc = take_bank()
        mm_group([(bsc["f32"][:, 0:TILE], ones1F, sC)], bsc,
                 reads=[res[rC], res["ones1F"]], tag=f"A_sbc{t}")
        ve(lambda e, b=bsc, t=t: e.tensor_copy(ssbAs[t % 2], b["f32"][:, 0:TILE]),
           reads=[bsc["res"]], writes=[ssbAs_res[t % 2]], tag=f"A_ssb{t}")
        unpin(bnm)

    bG = take_bank(pin=True)
    qtms = [qtm, qtm2]
    ktms = [ktm, ktm2]
    qtms_res = [res["qtm"], res["qtm2"]]
    ktms_res = [res["ktm"], res["ktm2"]]

    def emit_A_gram(t):
        for b_ in range(TILE // 128):
            i = t * (TILE // 128) + b_
            bt1 = take_bank()
            te(lambda e, b=bt1, i=i: e.transpose(b["b16"][:, 0:128],
                                                 qfm[:, i * 128:(i + 1) * 128], identb),
               reads=[res["qfm"], res["identb"]], writes=[bt1["res"]], tag=f"G_tq{i}")
            ve(lambda e, b=bt1, i=i: e.tensor_copy(qtms[i % 2], b["b16"][:, 0:128]),
               reads=[bt1["res"]], writes=[qtms_res[i % 2]])
            bt2 = take_bank()
            te(lambda e, b=bt2, i=i: e.transpose(b["b16"][:, 0:128],
                                                 kfm[:, i * 128:(i + 1) * 128], identb),
               reads=[res["kfm"], res["identb"]], writes=[bt2["res"]], tag=f"G_tk{i}")
            ve(lambda e, b=bt2, i=i: e.tensor_copy(ktms[i % 2], b["b16"][:, 0:128]),
               reads=[bt2["res"]], writes=[ktms_res[i % 2]])
            te(lambda e, b=bG, i=i: e.matmul(b["f32"][:, 0:128], qtms[i % 2], ktms[i % 2],
                                             start=(i == 0), stop=(i == 31)),
               reads=[qtms_res[i % 2], ktms_res[i % 2]], writes=[bG["res"]],
               tag=f"G_mm{i}")

    def emit_A_qkv(t):
        xb = xb16s[t % 3]
        xb_res = xb16s_res[t % 3]
        for oc in range(3):
            bq = take_bank()
            ops = [(bq["f32"][:, 0:TILE],
                    wqkvTb[:, c * 384 + oc * 128: c * 384 + (oc + 1) * 128],
                    rch(xb, c, TILE)) for c in range(NCH)]
            ops.append((bq["f32"][:, 0:TILE], gSqkvB[:, oc * 128:(oc + 1) * 128],
                        negmBs[t % 2]))
            mm_group(ops, bq, reads=[res["wqkvTb"], xb_res, res["gSqkvB"],
                                     negmBs_res[t % 2]], tag=f"A_qkv{t}_{oc}")
            dst, dres = qkv_dst[oc]
            ve(lambda e, b=bq, dst=dst, t=t:
               e.tensor_mul(dst[:, t * TILE:(t + 1) * TILE], b["f32"][:, 0:TILE],
                            ssbAs[t % 2]),
               reads=[bq["res"], ssbAs_res[t % 2]], writes=[dres], tag=f"A_ev{t}_{oc}")
        for (fm2, fres2, acc, accr) in ((qfm, res["qfm"], sqQa, "sqQa"),
                                        (kfm, res["kfm"], sqKa, "sqKa")):
            ae(lambda e, fm2=fm2, t=t: e.activation(nsq, fm2[:, t * TILE:(t + 1) * TILE],
                                                    AF.Square),
               reads=[fres2], writes=[res["nsq"]], tag=f"A_nsq{t}")
            ve(lambda e: e.reduce_sum(nred, nsq, AX.X),
               reads=[res["nsq"]], writes=[res["nred"]], tag=f"A_nred{t}")
            ve(lambda e, acc=acc: e.tensor_add(acc, acc, nred),
               reads=[res["nred"]], writes=[res[accr]], tag=f"A_nacc{t}")

    emit_A_tail(0)
    for t in range(NT):
        if t + 2 < NT:
            emit_A_head(t + 2)
        if t + 1 < NT:
            emit_A_tail(t + 1)
        emit_A_qkv(t)
        emit_A_gram(t)

    tap("qfm", qfm, res["qfm"])
    tap("kfm", kfm, res["kfm"])
    tap("vfm", vfm, res["vfm"])

    for (acc, accr, which) in ((sqQa, "sqQa", "q"), (sqKa, "sqKa", "k")):
        ae(lambda e, acc=acc: e.activation(nrmA, acc, AF.Sqrt),
           reads=[res[accr]], writes=[res["nrmA"]])
        ve(lambda e: e.tensor_scalar_max(nrmA, nrmA, 1e-12), writes=[res["nrmA"]])
        ve(lambda e: e.reciprocal(nrmA, nrmA), writes=[res["nrmA"]])
        if which == "q":
            ve(lambda e: e.tensor_mul(sclA, nrmA, ethb),
               reads=[res["nrmA"], res["ethb"]], writes=[res["sclA"]], tag="N_qscale")
        else:
            ve(lambda e: e.tensor_copy(rkS, nrmA),
               reads=[res["nrmA"]], writes=[res["rkS"]], tag="N_kscale")

    brt = take_bank()
    te(lambda e, b=brt: e.transpose(b["f32"][0:1, 0:128], rkS, identf),
       reads=[res["rkS"], res["identf"]], writes=[brt["res"]], tag="S_rkT")
    ae(lambda e, b=brt: e.copy(st_a[:, 0:128], b["f32"][0:1, 0:128]),
       reads=[brt["res"]], writes=[res["st_a"]], tag="S_rkrow")
    bcb = take_bank()
    mm_group([(bcb["f32"][:, 0:128], ones1F, st_a[:, 0:128])], bcb,
             reads=[res["st_a"], res["ones1F"]], tag="S_colmm")
    ae(lambda e, b=bcb: e.copy(ssbA[:, 0:128], b["f32"][:, 0:128]),
       reads=[bcb["res"]], writes=[res["ssbA"]], tag="S_colcp")
    ve(lambda e, b=bG: e.scalar_tensor_tensor(simsb, b["f32"][:, 0:128], sclA,
                                              ssbA[:, 0:128],
                                              ALU.mult, ALU.mult),
       reads=[bG["res"], res["sclA"], res["ssbA"]], writes=[res["simsb"]],
       tag="S_scale")
    unpin(bG)
    ve(lambda e: e.tensor_add(simsb, simsb, maskS),
       reads=[res["maskS"]], writes=[res["simsb"]], tag="S_mask")
    ve(lambda e: e.reduce_max(smx1, simsb, AX.X, negate=True),
       reads=[res["simsb"]], writes=[res["smx1"]])
    ae(lambda e: e.activation(simsb, simsb, AF.Exp, bias=smx1),
       reads=[res["smx1"]], writes=[res["simsb"]], tag="S_exp")
    ve(lambda e: e.reduce_sum(smx2, simsb, AX.X),
       reads=[res["simsb"]], writes=[res["smx2"]])
    ve(lambda e: e.reciprocal(smx2, smx2), writes=[res["smx2"]])
    ve(lambda e: e.tensor_scalar_mul(Ab, simsb, smx2),
       reads=[res["simsb"], res["smx2"]], writes=[res["Ab"]], tag="S_A")
    tap("Ab", Ab, res["Ab"])
    for half in range(2):
        bw = take_bank()
        mm_group([(bw["f32"][:, 0:512], Ab, woutTb[:, half * 512:(half + 1) * 512])],
                 bw, reads=[res["Ab"], res["woutTb"]], tag=f"WAT{half}")
        ve(lambda e, b=bw, half=half:
           e.tensor_copy(WATb[:, half * 512:(half + 1) * 512], b["f32"][:, 0:512]),
           reads=[bw["res"]], writes=[res["WATb"]], tag=f"WATev{half}")
    tap("WATb", WATb, res["WATb"])

    for nm, dst, rkey in [("R2T", R2Tb, "R2Tb"),
                          ("Vf", Vfb, "Vfb"), ("Wf", Wfb, "Wfb"),
                          ("bfw", bfb, "bfb"), ("biw", bib, "bib"),
                          ("Esel", EselB, "EselB")]:
        wload(dst, nm, rkey)
    dma("gpsimd", Vixb, dscr["VIX2"], reads=[res["scr_VIX2"]],
        writes=[res["Vixb"]], tag="ld_VIX2")
    dma("gpsimd", Wiyb, dscr["WIY2"], reads=[res["scr_WIY2"]],
        writes=[res["Wiyb"]], tag="ld_WIY2")
    dma("gpsimd", r1bS, dw["r1b"], writes=[res["r1bS"]])
    dma("gpsimd", r2bS, dw["r2b"], writes=[res["r2bS"]])
    dma("gpsimd", ebS, dw["eb"], writes=[res["ebS"]])
    dma("gpsimd", alphaS, dw["alpha"], writes=[res["alphaS"]])
    dma("gpsimd", nvinvS, dw["nvinv"], writes=[res["nvinvS"]])

    ve(lambda e: e.tensor_add(bias2, r2bS, ebS),
       reads=[res["r2bS"], res["ebS"]], writes=[res["bias2"]], tag="bias2")
    bk = take_bank()
    mm_group([(bk["f32"][0:128, 0:1], ones1F, alphaS)], bk,
             reads=[res["ones1F"], res["alphaS"]], tag="alpha_mm")
    ve(lambda e, b=bk: e.tensor_copy(alphab, b["f32"][0:128, 0:1]),
       reads=[bk["res"]], writes=[res["alphab"]])
    ve(lambda e: e.tensor_scalar_mul(bib, bib, alphab[0:8, 0:1]),
       reads=[res["alphab"]], writes=[res["bib"]], tag="abi")

    for nm in ("WQT", "WKT"):
        buf = wstr[0] if nm != "WKT" else wstr[1]
        bres = wstr_res[0] if nm != "WKT" else wstr_res[1]
        dma("gpsimd", buf, dw[nm], writes=[bres], tag=f"precast_ld_{nm}")
        dma("gpsimd", dscr[nm], buf, reads=[bres], writes=[res["scr_" + nm]],
            tag=f"precast_st_{nm}")

    def _steps_of(*names):
        out, seen = [], set()
        for nm_ in names:
            r_ = res[nm_]
            for st_ in r_.writers + r_.readers:
                if id(st_) not in seen:
                    seen.add(id(st_))
                    out.append(st_)
        return out

    fence0 = _steps_of("xt0")
    fence1 = _steps_of("xt1")

    base_al = (arena_base + 31) // 32 * 32
    ubnc0 = nc.alloc_sbuf_tensor_at("ubnc0", [128, 2 * NKRCH * D], BF16,
                                    offset=base_al).ap()
    ubnc1 = nc.alloc_sbuf_tensor_at("ubnc1", [128, NCH * D], BF16,
                                    offset=base_al + 16384).ap()
    res["ubnc0"] = _Res("ubnc0")
    res["ubnc1"] = _Res("ubnc1")
    dma("gpsimd", ubnc0[:, 0:NKRCH * D], dw["Uf"], writes=[res["ubnc0"]],
        tag="precast_ld_Uf", extra=fence0)
    dma("gpsimd", ubnc0[:, NKRCH * D:2 * NKRCH * D], dw["Ui"],
        writes=[res["ubnc0"]], tag="precast_ld_Ui", extra=fence0)
    ve(lambda e: e.tensor_scalar_mul(
        ubnc0[:, NKRCH * D:2 * NKRCH * D], ubnc0[:, NKRCH * D:2 * NKRCH * D],
        alphab),
       reads=[res["alphab"]], writes=[res["ubnc0"]], tag="precast_aUi")
    dma("gpsimd", dscr["UFUI"], ubnc0, reads=[res["ubnc0"]],
        writes=[res["scr_UFUI"]], tag="precast_st_UFUI")
    dma("gpsimd", ubnc1, dw["WOT"], writes=[res["ubnc1"]],
        tag="precast_ld_WOT", extra=fence1)
    dma("gpsimd", dscr["WOT"], ubnc1, reads=[res["ubnc1"]],
        writes=[res["scr_WOT"]], tag="precast_st_WOT")

    fence0 = fence0 + _steps_of("ubnc0")
    fence1 = fence1 + _steps_of("ubnc1")

    arena = _Arena(arena_base)
    ybufs = [arena.alloc("ybuf0", [128, NCH * SB], F32),
             arena.alloc("ybuf1", [128, NCH * SB], F32)]
    ybufs_res = [res["ybuf0"], res["ybuf1"]]
    yb16 = arena.alloc("yb16", [128, NCH * SB], BF16)
    lntmp = arena.alloc("lntmp", [128, NCH * SB], BF16)
    lntmpb = lntmp
    Qfm = arena.alloc("Qfm", [128, NCH * SB], BF16)
    Kfm = arena.alloc("Kfm", [128, NCH * SB], BF16)
    hsb = [arena.alloc("hsb0", [128, SB], BF16), arena.alloc("hsb1", [128, SB], BF16)]
    hsb_res = [res["hsb0"], res["hsb1"]]
    lgsbF = arena.alloc("lgsbF", [8, SB], BF16)
    lgsbI = arena.alloc("lgsbI", [8, SB], BF16)
    wfmF = arena.alloc("wfmF", [8, SB], BF16)
    wfmI = arena.alloc("wfmI", [8, SB], BF16)
    cumS = arena.alloc("cumS", [128, NKRCH * SB], F32)
    P2F = arena.alloc("P2F", [128, NKRCH * SB], BF16)
    P2I = arena.alloc("P2I", [128, NKRCH * SB], BF16)
    ff16 = arena.alloc("ff16", [128, NCH * SB], BF16)

    def wuse(s, nm):
        idx = list(STREAM_W).index(nm)
        return wstr[idx % 3], wstr_res[idx % 3]

    def wstream(s, nm):
        buf, bres = wuse(s, nm)
        dma("gpsimd", buf, dscr[nm], reads=[res["scr_" + nm]], writes=[bres],
            tag=f"B_wld{s}_{nm}")

    def emit_load(s, extra=()):
        yb = ybufs[s % 2]
        P.add("sync",
              lambda e, yb=yb, s=s: e.dma_start(
                  out=yb.rearrange("p (c t) -> p c t", c=NCH),
                  in_=dx.rearrange("p (c t) -> p c t", c=NCH)[:, :, s * SB:(s + 1) * SB]),
              writes=[ybufs_res[s % 2]], extra_deps=extra, tag=f"B_x{s}")

    def emit_proj(s):
        yb = ybufs[s % 2]
        yb_res = ybufs_res[s % 2]
        for oc in range(NCH):
            bp = take_bank()
            mm_group([(bp["f32"][:, 0:SB], WATb[:, oc * 128:(oc + 1) * 128],
                       vfm[:, s * SB:(s + 1) * SB])], bp,
                     reads=[res["WATb"], res["vfm"]], tag=f"B_proj{s}_{oc}")
            ve(lambda e, b=bp, oc=oc, yb=yb:
               e.tensor_add(yb[:, oc * SB:(oc + 1) * SB],
                            yb[:, oc * SB:(oc + 1) * SB], b["f32"][:, 0:SB]),
               reads=[bp["res"]], writes=[yb_res], tag=f"B_res{s}_{oc}")

    B_banks = {}

    def emit_stats_head(s):
        yb = ybufs[s % 2]
        yb_res = ybufs_res[s % 2]
        ae(lambda e, yb=yb: e.copy(yb16, yb),
           reads=[yb_res], writes=[res["yb16"]], tag=f"B_cast{s}")
        bnm = take_bank(pin=True)
        mm_group([(bnm["f32"][0:1, 0:SB], negidB, rch(yb16, c)) for c in range(NCH)],
                 bnm, reads=[res["yb16"], res["negidB"]], tag=f"B_negm{s}")
        ae(lambda e: e.activation(lntmp, yb16, AF.Square),
           reads=[res["yb16"]], writes=[res["lntmp"]], tag=f"B_sq{s}")
        bms = take_bank(pin=True)
        mm_group([(bms["f32"][0:1, 0:SB], invdB, rch(lntmp, c)) for c in range(NCH)],
                 bms, reads=[res["lntmp"], res["invdB"]], tag=f"B_msq{s}")
        B_banks[s] = (bnm, bms)

    def emit_stats_tail(s):
        yb = ybufs[s % 2]
        yb_res = ybufs_res[s % 2]
        bnm, bms = B_banks.pop(s)
        ae(lambda e, b=bnm: e.activation(st_a, b["f32"][0:1, 0:SB], AF.Square),
           reads=[bnm["res"]], writes=[res["st_a"]], tag=f"B_m2{s}")
        ve(lambda e, b=bms: e.tensor_sub(st_b, b["f32"][0:1, 0:SB], st_a),
           reads=[bms["res"], res["st_a"]], writes=[res["st_b"]])
        ve(lambda e: e.tensor_scalar_add(st_a, st_b, EPS_LN),
           reads=[res["st_b"]], writes=[res["st_a"]])
        ae(lambda e: e.activation(st_a, st_a, AF.Ln), writes=[res["st_a"]])
        ae(lambda e: e.activation(st_c, st_a, AF.Exp, scale=-0.5),
           reads=[res["st_a"]], writes=[res["st_c"]], tag=f"B_s{s}")
        ae(lambda e, b=bnm: e.copy(st_b, b["f32"][0:1, 0:SB]),
           reads=[bnm["res"]], writes=[res["st_b"]], tag=f"B_negm_sb{s}")
        unpin(bnm)
        unpin(bms)

    def emit_stats_bcast(s):
        yb = ybufs[s % 2]
        yb_res = ybufs_res[s % 2]
        bnb = take_bank()
        mm_group([(bnb["f32"][:, 0:SB], ones1F, st_b)], bnb,
                 reads=[res["st_b"], res["ones1F"]], tag=f"B_nmb{s}")
        bsb = take_bank()
        mm_group([(bsb["f32"][:, 0:SB], ones1F, st_c)], bsb,
                 reads=[res["st_c"], res["ones1F"]], tag=f"B_sbc{s}")
        ve(lambda e, b=bnb, yb=yb:
           e.tensor_add(lntmp.rearrange("p (c t) -> p c t", c=NCH),
                        yb.rearrange("p (c t) -> p c t", c=NCH),
                        b["f32"][:, 0:SB].unsqueeze(1).broadcast_to([128, NCH, SB]),
                        ),
           reads=[bnb["res"], yb_res], writes=[res["lntmp"]], tag=f"B_ln1_{s}")
        ve(lambda e, b=bsb:
           e.tensor_mul(yb16.rearrange("p (c t) -> p c t", c=NCH),
                        lntmp.rearrange("p (c t) -> p c t", c=NCH),
                        b["f32"][:, 0:SB].unsqueeze(1).broadcast_to([128, NCH, SB])),
           reads=[bsb["res"], res["lntmp"]], writes=[res["yb16"]], tag=f"B_ln2_{s}")

    def emit_qk(s):
        for (nm, dst, dkey) in (("WQT", Qfm, "Qfm"), ("WKT", Kfm, "Kfm")):
            wt, wres = wuse(s, nm)
            for oc in range(NCH):
                bq = take_bank()
                mm_group([(bq["f32"][:, 0:SB],
                           wt[:, c * D + oc * 128: c * D + (oc + 1) * 128],
                           rch(yb16, c)) for c in range(NCH)], bq,
                         reads=[wres, res["yb16"]], tag=f"B_{nm}{s}_{oc}")
                ae(lambda e, b=bq, dst=dst, oc=oc:
                   e.copy(dst[:, oc * SB:(oc + 1) * SB], b["f32"][:, 0:SB]),
                   reads=[bq["res"]], writes=[res[dkey]], tag=f"B_{nm}ev{s}_{oc}")

    def emit_h(s, bs, wname, lgsb, lgkey):
        wt, wres = wuse(s, wname)
        blg = take_bank(pin=True)
        for jc in range(NCH):
            bh = take_bank()
            mm_group([(bh["f32"][:, 0:SB], wt[:, c * D + jc * 128: c * D + (jc + 1) * 128],
                       rch(Qfm, c)) for c in range(NCH)], bh,
                     reads=[wres, res["Qfm"]], tag=f"B{bs}_h{s}_{jc}")
            hb = hsb[jc % 2]
            hbr = hsb_res[jc % 2]
            ae(lambda e, b=bh, hb=hb, jc=jc:
               e.activation(hb, b["f32"][:, 0:SB], AF.Gelu, bias=r1bS[:, jc:jc + 1]),
               reads=[bh["res"], res["r1bS"]], writes=[hbr], tag=f"B{bs}_hev{s}_{jc}")
            te(lambda e, b=blg, hb=hb, jc=jc:
               e.matmul(b["f32"][0:8, 0:SB], R2Tb[:, jc * 8:(jc + 1) * 8], hb,
                        start=(jc == 0), stop=(jc == NCH - 1)),
               reads=[hbr, res["R2Tb"]], writes=[blg["res"]], tag=f"B{bs}_lg{s}_{jc}")
        ve(lambda e, b=blg, lgsb=lgsb: e.tensor_scalar_add(lgsb, b["f32"][0:8, 0:SB], bias2),
           reads=[blg["res"], res["bias2"]], writes=[res[lgkey]], tag=f"B{bs}_lgev{s}")
        unpin(blg)

    def emit_smx_pre(s, bs, lgsb, lgkey, wts, wtsk):
        btA = take_bank(pin=True)
        for bi in range(SB // 128):
            te(lambda e, b=btA, bi=bi, lgsb=lgsb:
               e.transpose(b["b16"][:, bi * 8:(bi + 1) * 8],
                           lgsb[:, bi * 128:(bi + 1) * 128], identb[0:8, 0:8]),
               reads=[res[lgkey], res["identb"]], writes=[btA["res"]],
               tag=f"B{bs}_lt{s}_{bi}")
        ve(lambda e, b=btA:
           e.reduce_max(smxM, b["b16"][:, 0:32].rearrange("p (c e) -> p c e", e=8),
                        AX.X, negate=True),
           reads=[btA["res"]], writes=[res["smxM"]], tag=f"B{bs}_smax{s}")
        ve(lambda e, b=btA, wts=wts:
           e.tensor_add(wts.rearrange("p (c e) -> p c e", e=8),
                        b["b16"][:, 0:32].rearrange("p (c e) -> p c e", e=8),
                        smxM.unsqueeze(2).broadcast_to([128, 4, 8])),
           reads=[btA["res"], res["smxM"]], writes=[res[wtsk]],
           tag=f"B{bs}_ssub{s}")
        unpin(btA)

    def emit_smx_mid(s, bs, wts, wtsk):
        ae(lambda e, wts=wts: e.activation(wts, wts, AF.Exp, bias=0.0),
           writes=[res[wtsk]], tag=f"B{bs}_wexp{s}")
        for c in range(SB // 128):
            ve(lambda e, c=c, wts=wts: e.reduce_sum(sumM[:, c:c + 1],
                                                    wts[:, c * 8:(c + 1) * 8], AX.X),
               reads=[res[wtsk]], writes=[res["sumM"]], tag=f"B{bs}_ssum{s}_{c}")
        ve(lambda e: e.reciprocal(sumM, sumM), writes=[res["sumM"]],
           tag=f"B{bs}_srec{s}")
        ve(lambda e, s=s: e.tensor_mul(smxQ, nvinvS[:, s * 4:(s + 1) * 4], sumM),
           reads=[res["nvinvS"], res["sumM"]], writes=[res["smxQ"]],
           tag=f"B{bs}_snv{s}")
        ve(lambda e, wts=wts: e.tensor_mul(wts.rearrange("p (c e) -> p c e", e=8),
                                           wts.rearrange("p (c e) -> p c e", e=8),
                                           smxQ.unsqueeze(2).broadcast_to([128, 4, 8])),
           reads=[res["smxQ"]], writes=[res[wtsk]], tag=f"B{bs}_wts{s}")

    def emit_smx_back(s, bs, wfm, wfmk, wts, wtsk):
        btB = take_bank()
        for bi in range(SB // 128):
            te(lambda e, b=btB, bi=bi, wts=wts:
               e.transpose(b["b16"][0:8, bi * 128:(bi + 1) * 128],
                           wts[:, bi * 8:(bi + 1) * 8], identb),
               reads=[res[wtsk], res["identb"]], writes=[btB["res"]],
               tag=f"B{bs}_wt{s}_{bi}")
        ae(lambda e, b=btB, wfm=wfm: e.copy(wfm, b["b16"][0:8, 0:SB]),
           reads=[btB["res"]], writes=[res[wfmk]], tag=f"B{bs}_wfm{s}")

    def emit_ywxv(s, bs, Vp, Vpk, Wp, Wpk, carry, carryk):
        for kc in range(NKRCH):
            byw = take_bank()
            mm_group([(byw["f32"][:, 0:SB],
                       Wp[:, c * KR + kc * 128: c * KR + (kc + 1) * 128],
                       rch(Kfm, c)) for c in range(NCH)], byw,
                     reads=[res[Wpk], res["Kfm"]], tag=f"B{bs}_yw{s}_{kc}")
            ve(lambda e, b=byw, kc=kc, carry=carry, s=s:
               e.tensor_tensor_scan(rch(cumS, kc), onesColF.broadcast_to([128, SB]),
                                    b["f32"][:, 0:SB],
                                    carry[:, kc:kc + 1] if s > 0 else 0.0,
                                    ALU.mult, ALU.add),
               reads=[byw["res"], res["onesColF"]] + ([res[carryk]] if s > 0 else []),
               writes=[res["cumS"]], tag=f"B{bs}_scan{s}_{kc}")
            ve(lambda e, kc=kc, carry=carry:
               e.tensor_copy(carry[:, kc:kc + 1], rch(cumS, kc)[:, SB - 1:SB]),
               reads=[res["cumS"]], writes=[res[carryk]], tag=f"B{bs}_carry{s}_{kc}")
            bxv = take_bank()
            mm_group([(bxv["f32"][:, 0:SB],
                       Vp[:, c * KR + kc * 128: c * KR + (kc + 1) * 128],
                       rch(Qfm, c)) for c in range(NCH)], bxv,
                     reads=[res[Vpk], res["Qfm"]], tag=f"B{bs}_xv{s}_{kc}")
            ve(lambda e, b=bxv, kc=kc: e.tensor_tensor(rch(cumS, kc), rch(cumS, kc),
                                                       b["f32"][:, 0:SB], ALU.mult),
               reads=[bxv["res"]], writes=[res["cumS"]], tag=f"B{bs}_p1{s}_{kc}")
            bwr = take_bank()
            mm_group([(bwr["f32"][:, 0:SB], EselB[:, kc * 128:(kc + 1) * 128], wfmF if bs == "F" else wfmI)],
                     bwr, reads=[res["EselB"], res["wfmF" if bs == "F" else "wfmI"]],
                     tag=f"B{bs}_wrep{s}_{kc}")
            P2, P2k = (P2F, "P2F") if bs == "F" else (P2I, "P2I")
            ve(lambda e, b=bwr, kc=kc, P2=P2:
               e.tensor_mul(rch(P2, kc), rch(cumS, kc), b["f32"][:, 0:SB]),
               reads=[res["cumS"], bwr["res"]], writes=[res[P2k]],
               tag=f"B{bs}_p2{s}_{kc}")

    def emit_out(s):
        wt, wres = wuse(s, "UFUI")
        ufo = 0
        uio = NKRCH * D
        for dc in range(NCH):
            bo = take_bank()
            ops = []
            out = bo["f32"][:, 0:SB]
            for kc in range(NKRCH):
                ops.append((out, wt[:, ufo + kc * D + dc * 128: ufo + kc * D + (dc + 1) * 128],
                            rch(P2F, kc)))
            ops.append((out, bfb[0:8, dc * 128:(dc + 1) * 128], wfmF))
            for kc in range(NKRCH):
                ops.append((out, wt[:, uio + kc * D + dc * 128: uio + kc * D + (dc + 1) * 128],
                            rch(P2I, kc)))
            ops.append((out, bib[0:8, dc * 128:(dc + 1) * 128], wfmI))
            mm_group(ops, bo,
                     reads=[wres, res["bfb"], res["bib"],
                            res["P2F"], res["P2I"], res["wfmF"], res["wfmI"]],
                     tag=f"B_out{s}_{dc}")
            ae(lambda e, b=bo, dc=dc: e.copy(ff16[:, dc * SB:(dc + 1) * SB],
                                             b["f32"][:, 0:SB]),
               reads=[bo["res"]], writes=[res["ff16"]], tag=f"B_outev{s}_{dc}")

    def emit_wo(s):
        yb = ybufs[s % 2]
        yb_res = ybufs_res[s % 2]
        wt, wres = wuse(s, "WOT")
        for oc in range(NCH):
            bw = take_bank()
            mm_group([(bw["f32"][:, 0:SB], wt[:, c * D + oc * 128: c * D + (oc + 1) * 128],
                       rch(ff16, c)) for c in range(NCH)], bw,
                     reads=[wres, res["ff16"]], tag=f"B_wo{s}_{oc}")
            ve(lambda e, b=bw, oc=oc, yb=yb:
               e.tensor_add(yb[:, oc * SB:(oc + 1) * SB],
                            yb[:, oc * SB:(oc + 1) * SB], b["f32"][:, 0:SB]),
               reads=[bw["res"]], writes=[yb_res], tag=f"B_wores{s}_{oc}")
        dma("sync", dout.rearrange("p (c t) -> p c t", c=NCH)[:, :, s * SB:(s + 1) * SB],
            yb.rearrange("p (c t) -> p c t", c=NCH),
            reads=[yb_res], writes=[out_res], tag=f"B_store{s}")

    emit_load(0, extra=fence0)
    wstream(0, "WQT")
    wstream(0, "WKT")
    wstream(0, "R1T")
    emit_proj(0)
    emit_stats_head(0)
    emit_stats_tail(0)
    emit_stats_bcast(0)
    for s in range(NSB):
        emit_qk(s)
        wstream(s, "R1TI")
        if s + 1 < NSB:
            emit_load(s + 1, extra=(fence1 if s + 1 == 1 else ()))
            emit_proj(s + 1)
        emit_h(s, "F", "R1T", lgsbF, "lgsbF")
        emit_smx_pre(s, "F", lgsbF, "lgsbF", wtsBF, "wtsBF")
        wstream(s, "UFUI")
        wstream(s, "WOT")
        emit_h(s, "I", "R1TI", lgsbI, "lgsbI")
        emit_smx_pre(s, "I", lgsbI, "lgsbI", wtsBI, "wtsBI")
        emit_smx_mid(s, "F", wtsBF, "wtsBF")
        emit_smx_back(s, "F", wfmF, "wfmF", wtsBF, "wtsBF")
        if s + 1 < NSB:
            wstream(s + 1, "WQT")
        emit_smx_mid(s, "I", wtsBI, "wtsBI")
        emit_ywxv(s, "F", Vfb, "Vfb", Wfb, "Wfb", carryF, "carryF")
        emit_smx_back(s, "I", wfmI, "wfmI", wtsBI, "wtsBI")
        emit_ywxv(s, "I", Vixb, "Vixb", Wiyb, "Wiyb", carryI, "carryI")
        if s + 1 < NSB:
            emit_stats_head(s + 1)
            emit_stats_tail(s + 1)
        emit_out(s)
        if s + 1 < NSB:
            wstream(s + 1, "WKT")
            emit_stats_bcast(s + 1)
        emit_wo(s)
        if s + 1 < NSB:
            wstream(s + 1, "R1T")
        if s == 0:
            tap("y0", ybufs[0], ybufs_res[0])
            tap("Q0", Qfm, res["Qfm"])
            tap("cum0", cumS, res["cumS"])

    P.add("sync", lambda e: e.nop(), reads=[out_res], tag="final_wait")

    P.finalize()

    with (nc.semaphore("sem_t") as s_t, nc.semaphore("sem_v") as s_v,
          nc.semaphore("sem_a") as s_a, nc.semaphore("sem_s") as s_s,
          nc.semaphore("sem_g") as s_g, nc.semaphore("sem_d") as s_d,
          nc.Block() as block):
        sems = {"tensor": s_t, "vector": s_v, "scalar": s_a,
                "sync": s_s, "gpsimd": s_g, "sdma": s_d}

        @block.tensor
        def _(eng):
            P.emit_engine("tensor", eng, sems)

        @block.vector
        def _(eng):
            P.emit_engine("vector", eng, sems)

        @block.scalar
        def _(eng):
            P.emit_engine("scalar", eng, sems)

        @block.sync
        def _(eng):
            P.emit_engine("sync", eng, sems)

        @block.gpsimd
        def _(eng):
            P.emit_engine("gpsimd", eng, sems)

    return nc, tap_drams


def kernel(**inputs):
    xs, w = _host_prep(inputs)
    taps = tuple(_DEBUG_TAPS)
    nc, tap_drams = build_program(taps)
    in_maps = []
    for b in range(8):
        m = {"xT": xs[b]}
        m.update(w)
        in_maps.append(m)
    r = run_bass_kernel_spmd(nc, in_maps, core_ids=list(range(8)))
    outs = []
    for b in range(8):
        o = np.asarray(r.results[b]["outT"], dtype=np.float32)
        outs.append(o.reshape(128, NCH, T).transpose(1, 0, 2).reshape(D, T).T)
    if taps:
        kernel.last_taps = [{k: np.asarray(r.results[b]["tap_" + k]) for k in taps
                             if ("tap_" + k) in r.results[b]} for b in range(8)]
    return np.ascontiguousarray(np.stack(outs, axis=0))


# version 50
# speedup vs baseline: 1.0401x; 1.0213x over previous
import numpy as np

import concourse.bass as bass
import concourse.mybir as mybir
import concourse.bass_utils as _bu
from concourse.bass_utils import run_bass_kernel_spmd

if not getattr(_bu, "_ldw_patch", False):
    _orig_run_command = _bu.run_command

    def _run_command_ldw(argv, **kw):
        if isinstance(argv, list):
            argv = ["--enable-ldw-opt=true" if a == "--enable-ldw-opt=false" else a
                    for a in argv]
        return _orig_run_command(argv, **kw)

    _bu.run_command = _run_command_ldw
    _bu._ldw_patch = True

F32 = mybir.dt.float32
BF16 = mybir.dt.bfloat16
AF = mybir.ActivationFunctionType
ALU = mybir.AluOpType
AX = mybir.AxisListType

D = 1024
T = 4096
NCH = 8
HEADS = 8
KEXP = 8
R = 64
KR = 512
NKRCH = 4
TILE = 512
NT = T // TILE
SB = 512
NSB = T // SB
EPS_LN = 1e-5
EPS_RMS = 1e-6

_ENGINES = ("tensor", "vector", "scalar", "sync", "gpsimd")
_DMA_ENGINES = ("sync", "gpsimd")

_DEBUG_TAPS = []


class _Step:
    __slots__ = ("eng", "emit", "deps", "inc_needed", "sem_val", "tag", "semkey")

    def __init__(self, eng, emit, tag="", semkey=None):
        self.eng = eng
        self.emit = emit
        self.deps = []
        self.inc_needed = eng in _DMA_ENGINES or semkey == "sdma"
        self.sem_val = None
        self.tag = tag
        self.semkey = semkey or eng


class _Res:
    __slots__ = ("writers", "readers", "rsw", "name")

    def __init__(self, name=""):
        self.writers = []
        self.readers = []
        self.rsw = False
        self.name = name


class _Plan:
    def __init__(self):
        self.steps = []

    def add(self, eng, emit, reads=(), writes=(), tag="", extra_deps=(), semkey=None):
        st = _Step(eng, emit, tag, semkey)
        deps = []
        for r in reads:
            deps.extend(r.writers)
        for w in writes:
            deps.extend(w.writers)
            deps.extend(w.readers)
        deps.extend(extra_deps)
        seen = set()
        for d in deps:
            if d is st or id(d) in seen:
                continue
            seen.add(id(d))
            st.deps.append(d)
        for r in reads:
            r.readers.append(st)
            r.rsw = True
        for w in writes:
            if w.rsw:
                w.writers = [st]
                w.readers = []
                w.rsw = False
            else:
                w.writers.append(st)
        self.steps.append(st)
        return st

    def finalize(self):
        for st in self.steps:
            for d in st.deps:
                d.inc_needed = True
        counters = {e: 0 for e in _ENGINES}
        counters["sdma"] = 0
        for st in self.steps:
            if st.inc_needed:
                dma = st.eng in _DMA_ENGINES or st.semkey == "sdma"
                counters[st.semkey] += 16 if dma else 1
                st.sem_val = counters[st.semkey]

    def emit_engine(self, name, eng, sems):
        watermark = {}
        for st in self.steps:
            if st.eng != name:
                continue
            for d in st.deps:
                if watermark.get(d.semkey, 0) >= d.sem_val:
                    continue
                eng.wait_ge(sems[d.semkey], d.sem_val)
                watermark[d.semkey] = d.sem_val
            insts = st.emit(eng)
            if st.inc_needed:
                if not isinstance(insts, (list, tuple)):
                    insts = [insts]
                if name in _DMA_ENGINES:
                    for i in insts:
                        i.then_inc(sems[name], 16)
                    eng.wait_ge(sems[name], st.sem_val)
                    watermark[name] = st.sem_val
                elif st.semkey == "sdma":
                    for i in insts:
                        i.then_inc(sems["sdma"], 16)
                else:
                    insts[-1].then_inc(sems[name], 1)


def _pack_cm(m):
    r, c = m.shape
    return np.ascontiguousarray(
        m.reshape(r // 128, 128, c).transpose(1, 0, 2).reshape(128, -1))


def _host_prep(inp):
    g = {k: np.asarray(v, dtype=np.float32) for k, v in inp.items()}
    w = {}
    x = g["L0_attn_x"]
    xs = [_pack_cm(np.ascontiguousarray(x[b].T)) for b in range(8)]
    w["wqkvT"] = _pack_cm(np.ascontiguousarray(g["L0_attn_w_qkv"].T))
    w["woutT"] = _pack_cm(np.ascontiguousarray(g["L0_attn_w_out"].T))
    w["WQT"] = _pack_cm(np.ascontiguousarray(g["L0_ffn_W_Q"].T))
    w["WKT"] = _pack_cm(np.ascontiguousarray(g["L0_ffn_W_K"].T))
    w["WIN"] = _pack_cm(np.ascontiguousarray(g["L0_ffn_W_inv"]))
    w["R1T"] = _pack_cm(np.ascontiguousarray(g["L0_ffn_router_w1"].T))
    w["WOT"] = _pack_cm(np.ascontiguousarray(g["L0_ffn_W_O"].T))
    w["R2T"] = _pack_cm(np.ascontiguousarray(g["L0_ffn_router_w2"].T))
    w["Vf"] = _pack_cm(g["L0_ffn_V_fwd"].transpose(1, 0, 2).reshape(D, KR))
    w["Wf"] = _pack_cm(g["L0_ffn_W_fwd"].transpose(1, 0, 2).reshape(D, KR))
    w["Vix"] = _pack_cm(g["L0_ffn_W_inv_exp"].transpose(1, 0, 2).reshape(D, KR))
    w["Wiy"] = _pack_cm(g["L0_ffn_V_inv"].transpose(1, 0, 2).reshape(D, KR))
    w["Uf"] = _pack_cm(g["L0_ffn_U_fwd"].transpose(0, 2, 1).reshape(KR, D))
    w["Ui"] = _pack_cm(g["L0_ffn_U_inv"].transpose(0, 2, 1).reshape(KR, D))
    w["bfw"] = np.ascontiguousarray(g["L0_ffn_b_fwd"])
    w["biw"] = np.ascontiguousarray(g["L0_ffn_b_inv"])
    w["gamma"] = np.ascontiguousarray(g["L0_attn_norm_weight"].reshape(NCH, 128).T)
    w["r1b"] = np.ascontiguousarray(g["L0_ffn_router_b1"].reshape(NCH, 128).T)
    w["temp"] = np.ascontiguousarray(g["L0_attn_temperature"].reshape(8, 1))
    w["r2b"] = np.ascontiguousarray(g["L0_ffn_router_b2"].reshape(8, 1))
    w["eb"] = np.ascontiguousarray(g["L0_ffn_expert_bias"].reshape(8, 1))
    w["alpha"] = np.ascontiguousarray(g["L0_ffn_alpha_bi"].reshape(1, 1))
    w["ident"] = np.eye(128, dtype=np.float32)
    mask = np.full((128, 128), -30000.0, dtype=np.float32)
    for h in range(HEADS):
        mask[16 * h:16 * h + 16, 16 * h:16 * h + 16] = 0.0
    w["mask"] = mask
    nv = np.zeros((128, 32), dtype=np.float32)
    for blk in range(32):
        nv[:, blk] = 1.0 / (blk * 128 + np.arange(128) + 1.0)
    w["nvinv"] = nv
    esel = np.zeros((8, 512), dtype=np.float32)
    for c in range(4):
        for p in range(128):
            esel[2 * c + p // 64, c * 128 + p] = 1.0
    w["Esel"] = esel
    hsel = np.zeros((8, 128), dtype=np.float32)
    for p in range(128):
        hsel[p // 16, p] = 1.0
    w["hsel"] = hsel
    return xs, w


def build_program(taps=()):
    nc = bass.Bass("TRN2", target_bir_lowering=False)
    P = _Plan()

    dx = nc.dram_tensor("xT", [128, NCH * T], F32, kind="ExternalInput").ap()
    dout = nc.dram_tensor("outT", [128, NCH * T], F32, kind="ExternalOutput").ap()
    dw = {}
    for name, shape in [
        ("wqkvT", [128, NCH * 384]), ("woutT", [128, 1024]),
        ("WQT", [128, NCH * D]), ("WKT", [128, NCH * D]), ("WIN", [128, NCH * D]),
        ("R1T", [128, NCH * D]), ("WOT", [128, NCH * D]), ("R2T", [128, 64]),
        ("Vf", [128, NCH * KR]), ("Wf", [128, NCH * KR]),
        ("Vix", [128, NCH * KR]), ("Wiy", [128, NCH * KR]),
        ("Uf", [128, NKRCH * D]), ("Ui", [128, NKRCH * D]),
        ("bfw", [8, 1024]), ("biw", [8, 1024]),
        ("gamma", [128, 8]), ("r1b", [128, 8]),
        ("temp", [8, 1]), ("r2b", [8, 1]), ("eb", [8, 1]), ("alpha", [1, 1]),
        ("ident", [128, 128]), ("mask", [128, 128]), ("nvinv", [128, 32]),
        ("Esel", [8, 512]), ("hsel", [8, 128]),
    ]:
        dw[name] = nc.dram_tensor(name, shape, F32, kind="ExternalInput").ap()

    tap_drams = {}
    res = {}
    STREAM_W = ("WQT", "WKT", "R1T", "R1TI", "UFUI", "WOT")
    dscr = {}
    for nm in STREAM_W:
        dscr[nm] = nc.dram_tensor("scr_" + nm, [128, NCH * D], BF16).ap()
        res["scr_" + nm] = _Res("scr_" + nm)
    for nm in ("VIX2", "WIY2"):
        dscr[nm] = nc.dram_tensor("scr_" + nm, [128, NCH * KR], BF16).ap()
        res["scr_" + nm] = _Res("scr_" + nm)

    def sb(name, shape, dtype):
        h = nc.alloc_sbuf_tensor(name, shape, dtype)
        res[name] = _Res(name)
        return h.ap()

    WATb = sb("WATb", [128, 1024], BF16)
    vfm = sb("vfm", [128, T], BF16)
    wstr = [sb("wsA", [128, NCH * D], BF16), sb("wsB", [128, NCH * D], BF16),
            sb("wsC", [128, NCH * D], BF16)]
    wstr_res = [res["wsA"], res["wsB"], res["wsC"]]
    R2Tb = sb("R2Tb", [128, 64], BF16)
    Vfb = sb("Vfb", [128, NCH * KR], BF16)
    Wfb = sb("Wfb", [128, NCH * KR], BF16)
    Vixb = sb("Vixb", [128, NCH * KR], BF16)
    Wiyb = sb("Wiyb", [128, NCH * KR], BF16)
    bfb = sb("bfb", [8, 1024], BF16)
    bib = sb("bib", [8, 1024], BF16)
    identb = sb("identb", [128, 128], BF16)
    nvinvS = sb("nvinvS", [128, 32], F32)
    EselB = sb("EselB", [8, 512], BF16)
    gammaS = sb("gammaS", [128, 8], F32)
    r1bS = sb("r1bS", [128, 8], F32)
    tempS = sb("tempS", [8, 1], F32)
    r2bS = sb("r2bS", [8, 1], F32)
    ebS = sb("ebS", [8, 1], F32)
    alphaS = sb("alphaS", [1, 1], F32)
    onesB = sb("onesB", [128, 1], BF16)
    negidB = sb("negidB", [128, 1], BF16)
    invdB = sb("invdB", [128, 1], BF16)
    ones1F = sb("ones1F", [1, 128], F32)
    onesColF = sb("onesColF", [128, 1], F32)
    gSqkvB = sb("gSqkvB", [1, 384], BF16)
    bias2 = sb("bias2", [8, 1], F32)
    ethb = sb("ethb", [128, 1], F32)
    alphab = sb("alphab", [128, 1], F32)
    etmpb = sb("etmpb", [8, 1], BF16)
    negmB = sb("negmB", [1, TILE], BF16)
    negmB1 = sb("negmB1", [1, TILE], BF16)
    maskS = sb("maskS", [128, 128], F32)
    hselB = sb("hselB", [8, 128], BF16)
    rkS = sb("rkS", [128, 1], F32)
    st_a = sb("st_a", [1, TILE], F32)
    st_b = sb("st_b", [1, TILE], F32)
    st_c = sb("st_c", [1, TILE], F32)
    sqA = sb("sqA", [128, 1], F32)
    nrmA = sb("nrmA", [128, 1], F32)
    sclA = sb("sclA", [128, 1], F32)
    carryF = sb("carryF", [128, 4], F32)
    carryI = sb("carryI", [128, 4], F32)
    smx1 = sb("smx1", [128, 1], F32)
    smx2 = sb("smx2", [128, 1], F32)
    smxM = sb("smxM", [128, 4], F32)
    sumM = sb("sumM", [128, 4], F32)
    smxQ = sb("smxQ", [128, 4], F32)
    wtsBF = sb("wtsBF", [128, 32], BF16)
    wtsBI = sb("wtsBI", [128, 32], BF16)
    out_res = _Res("OUT")
    res["OUT"] = out_res

    SBUF_TOP = 229344
    arena_base = nc._sbuf_addr_for_side("left")
    arena_base = (arena_base + 31) // 32 * 32

    class _Arena:
        def __init__(self, base):
            self.off = base

        def alloc(self, name, shape, dtype):
            per_part = int(np.prod(shape[1:])) * mybir.dt.size(dtype)
            off = (self.off + 31) // 32 * 32
            assert off + per_part <= SBUF_TOP, (
                f"arena overflow: {name} at {off}+{per_part} > {SBUF_TOP}")
            h = nc.alloc_sbuf_tensor_at(name, shape, dtype, offset=off)
            self.off = off + per_part
            res[name] = _Res(name)
            return h.ap()

    banks = []
    for i in range(8):
        h = nc.alloc_psum_tensor(f"pb{i}", [128, 512], F32)
        banks.append({"f32": h.ap(), "b16": h.ap().bitcast(BF16),
                      "res": _Res(f"bank{i}"), "i": i})
    bank_rr = [0]
    pinned = set()

    def take_bank(pin=False):
        for _ in range(8):
            b = banks[bank_rr[0] % 8]
            bank_rr[0] += 1
            if b["i"] not in pinned:
                if pin:
                    pinned.add(b["i"])
                return b
        raise RuntimeError("all psum banks pinned")

    def unpin(b):
        pinned.discard(b["i"])

    def dma(queue, out_ap, in_ap, reads=(), writes=(), tag=""):
        return P.add(queue, lambda e, o=out_ap, i=in_ap: e.dma_start(out=o, in_=i),
                     reads=reads, writes=writes, tag=tag,
                     semkey="sdma" if queue == "scalar" else None)

    def tap(name, ap_, rd):
        if name not in taps:
            return
        d = nc.dram_tensor("tap_" + name, list(ap_.shape), ap_.dtype,
                           kind="ExternalOutput").ap()
        tap_drams[name] = d
        dma("gpsimd", d, ap_, reads=[rd], tag="tap_" + name)

    def mm_group(ops, bank, reads, tag=""):
        n = len(ops)

        def em(eng, ops=ops, n=n):
            insts = []
            for i, (o, l, r) in enumerate(ops):
                insts.append(eng.matmul(o, l, r, start=(i == 0), stop=(i == n - 1)))
            return insts
        return P.add("tensor", em, reads=reads, writes=[bank["res"]], tag=tag)

    def ve(emit, reads=(), writes=(), tag=""):
        return P.add("vector", emit, reads=reads, writes=writes, tag=tag)

    def ae(emit, reads=(), writes=(), tag=""):
        return P.add("scalar", emit, reads=reads, writes=writes, tag=tag)

    def te(emit, reads=(), writes=(), tag=""):
        return P.add("tensor", emit, reads=reads, writes=writes, tag=tag)

    def ge(emit, reads=(), writes=(), tag=""):
        return P.add("gpsimd", emit, reads=reads, writes=writes, tag=tag)

    def rch(ap_, c, w=SB):
        return ap_.rearrange("p (c t) -> p c t", c=ap_.shape[1] // w)[:, c, :]

    arena = _Arena(arena_base)
    xt0 = arena.alloc("xt0", [128, NCH * TILE], F32)
    xt1 = arena.alloc("xt1", [128, NCH * TILE], F32)
    qfm = arena.alloc("qfm", [128, T], BF16)
    kfm = arena.alloc("kfm", [128, T], BF16)
    xb16 = arena.alloc("xb16", [128, NCH * TILE], BF16)
    xb16b = arena.alloc("xb16b", [128, NCH * TILE], BF16)
    xb16c = arena.alloc("xb16c", [128, NCH * TILE], BF16)
    xsq16 = arena.alloc("xsq16", [128, NCH * TILE], BF16)
    ssbA = arena.alloc("ssbA", [128, TILE], F32)
    ssbA1 = arena.alloc("ssbA1", [128, TILE], F32)
    qtm = arena.alloc("qtm", [128, 128], BF16)
    ktm = arena.alloc("ktm", [128, 128], BF16)
    qtm2 = arena.alloc("qtm2", [128, 128], BF16)
    ktm2 = arena.alloc("ktm2", [128, 128], BF16)
    simsb = arena.alloc("simsb", [128, 128], F32)
    Ab = arena.alloc("Ab", [128, 128], BF16)
    identf = arena.alloc("identf", [128, 128], F32)
    nsq = arena.alloc("nsq", [128, TILE], BF16)
    sqQa = arena.alloc("sqQa", [128, 1], F32)
    sqKa = arena.alloc("sqKa", [128, 1], F32)
    nred = arena.alloc("nred", [128, 1], F32)
    st_a2 = arena.alloc("st_a2", [1, TILE], F32)
    st_b2 = arena.alloc("st_b2", [1, TILE], F32)
    st_c2 = arena.alloc("st_c2", [1, TILE], F32)
    wqkvTb = arena.alloc("wqkvTb", [128, NCH * 384], BF16)
    woutTb = arena.alloc("woutTb", [128, 1024], BF16)


    def wload(dst_ap, dname, rkey):
        dma("gpsimd", dst_ap, dw[dname], writes=[res[rkey]], tag="ld_" + dname)

    dma("gpsimd", wstr[0], dw["WIN"], writes=[wstr_res[0]], tag="prep_ld_WIN")
    dma("gpsimd", qfm, dw["Vix"], writes=[res["qfm"]], tag="prep_ld_Vix")
    dma("gpsimd", wstr[1], dw["R1T"], writes=[wstr_res[1]], tag="prep_ld_R1T")
    wload(wqkvTb, "wqkvT", "wqkvTb")
    wload(identb, "ident", "identb")
    wload(hselB, "hsel", "hselB")
    wload(woutTb, "woutT", "woutTb")
    dma("gpsimd", maskS, dw["mask"], writes=[res["maskS"]])
    dma("gpsimd", identf, dw["ident"], writes=[res["identf"]])
    dma("gpsimd", gammaS, dw["gamma"], writes=[res["gammaS"]])
    dma("gpsimd", tempS, dw["temp"], writes=[res["tempS"]])

    for ap_, val, rk in ((sqQa, 0.0, "sqQa"), (sqKa, 0.0, "sqKa"),
                         (onesB, 1.0, "onesB"), (negidB, -1.0 / 1024.0, "negidB"),
                         (invdB, 1.0 / 1024.0, "invdB"), (ones1F, 1.0, "ones1F"),
                         (onesColF, 1.0, "onesColF"), (carryF, 0.0, "carryF"),
                         (carryI, 0.0, "carryI")):
        ve(lambda e, ap_=ap_, val=val: e.memset(ap_, val),
           writes=[res[rk]], tag="memset_" + rk)

    xts = [xt0, xt1]
    xts_res = [res["xt0"], res["xt1"]]
    xb16s = [xb16, xb16b, xb16c]
    xb16s_res = [res["xb16"], res["xb16b"], res["xb16c"]]
    ssbAs = [ssbA, ssbA1]
    ssbAs_res = [res["ssbA"], res["ssbA1"]]
    negmBs = [negmB, negmB1]
    negmBs_res = [res["negmB"], res["negmB1"]]
    qkv_dst = [(qfm, res["qfm"]), (kfm, res["kfm"]), (vfm, res["vfm"])]

    A_banks = {}

    def emit_A_head_dma(t):
        xt = xts[t % 2]
        xt_res = xts_res[t % 2]
        xb = xb16s[t % 3]
        xb_res = xb16s_res[t % 3]
        dma("sync", xt.rearrange("p (c t) -> p c t", c=NCH),
            dx.rearrange("p (c t) -> p c t", c=NCH)[:, :, t * TILE:(t + 1) * TILE],
            writes=[xt_res], tag=f"A_x{t}")
        ae(lambda e, xt=xt, xb=xb: e.copy(xb, xt),
           reads=[xt_res], writes=[xb_res], tag=f"A_cast{t}")

    def emit_A_head_mm(t):
        xb = xb16s[t % 3]
        xb_res = xb16s_res[t % 3]
        bnm = take_bank(pin=True)
        mm_group([(bnm["f32"][0:1, 0:TILE], negidB, rch(xb, c, TILE))
                  for c in range(NCH)], bnm,
                 reads=[xb_res, res["negidB"]], tag=f"A_negm{t}")
        ae(lambda e, xb=xb: e.activation(xsq16, xb, AF.Square),
           reads=[xb_res], writes=[res["xsq16"]], tag=f"A_sq{t}")
        mm_group([(bnm["f32"][32:33, 0:TILE], invdB, rch(xsq16, c, TILE))
                  for c in range(NCH)], bnm,
                 reads=[res["xsq16"], res["invdB"]], tag=f"A_msq{t}")
        A_banks[t] = bnm

    def emit_A_head(t):
        emit_A_head_dma(t)
        emit_A_head_mm(t)


    emit_A_head(0)
    emit_A_head(1)

    def prep_vw(nm, src_ap, src_res, stage_ap, stage_res):
        for a in range(NCH):
            bk_ = take_bank()
            mm_group([(bk_["f32"][:, 0:KR],
                       wstr[0][:, c * D + a * 128:c * D + (a + 1) * 128],
                       src_ap.rearrange("p (c t) -> p c t", c=NCH)[:, c, :])
                      for c in range(NCH)], bk_,
                     reads=[wstr_res[0], src_res], tag=f"prep_{nm}_{a}")
            ae(lambda e, b=bk_, a=a, stage_ap=stage_ap:
               e.copy(stage_ap[:, a * KR:(a + 1) * KR], b["f32"][:, 0:KR]),
               reads=[bk_["res"]], writes=[stage_res], tag=f"prep_{nm}ev_{a}")

    dma("gpsimd", xb16c, dw["Wiy"], writes=[res["xb16c"]], tag="prep_ld_Wiy")
    prep_vw("VIX2", qfm, res["qfm"], kfm, res["kfm"])
    xt1b = xt1.bitcast(BF16)
    for a in range(NCH):
        for hf in range(2):
            bk_ = take_bank()
            mm_group([(bk_["f32"][:, 0:512],
                       wstr[0][:, c * D + a * 128:c * D + (a + 1) * 128],
                       wstr[1][:, c * D + hf * 512:c * D + (hf + 1) * 512])
                      for c in range(NCH)], bk_,
                     reads=[wstr_res[0], wstr_res[1]], tag=f"prep_R1TI_{a}_{hf}")
            ae(lambda e, b=bk_, a=a, hf=hf:
               e.copy(xt1b[:, a * D + hf * 512:a * D + (hf + 1) * 512],
                      b["f32"][:, 0:512]),
               reads=[bk_["res"]], writes=[res["xt1"]], tag=f"prep_R1TIev_{a}_{hf}")
    prep_vw("WIY2", xb16c, res["xb16c"], qfm, res["qfm"])
    dma("gpsimd", dscr["R1T"], wstr[1], reads=[wstr_res[1]],
        writes=[res["scr_R1T"]], tag="precast_st_R1T")
    dma("gpsimd", dscr["R1TI"], xt1b, reads=[res["xt1"]],
        writes=[res["scr_R1TI"]], tag="prep_st_R1TI")
    dma("gpsimd", dscr["VIX2"], kfm, reads=[res["kfm"]], writes=[res["scr_VIX2"]],
        tag="prep_st_VIX2")
    dma("gpsimd", dscr["WIY2"], qfm, reads=[res["qfm"]], writes=[res["scr_WIY2"]],
        tag="prep_st_WIY2")

    for c in range(NCH):
        ve(lambda e, c=c: e.tensor_scalar_mul(wqkvTb[:, c * 384:(c + 1) * 384],
                                              wqkvTb[:, c * 384:(c + 1) * 384],
                                              gammaS[:, c:c + 1]),
           reads=[res["gammaS"]], writes=[res["wqkvTb"]], tag=f"gt_scale{c}")

    bk = take_bank()
    mm_group([(bk["f32"][0:1, 0:384], onesB, wqkvTb[:, c * 384:(c + 1) * 384])
              for c in range(NCH)], bk,
             reads=[res["wqkvTb"], res["onesB"]], tag="gSqkv_mm")
    ae(lambda e, b=bk: e.copy(gSqkvB, b["f32"][0:1, 0:384]),
       reads=[bk["res"]], writes=[res["gSqkvB"]], tag="gSqkv_cp")

    ae(lambda e: e.activation(tempS, tempS, AF.Exp), writes=[res["tempS"]], tag="exp_temp")
    ae(lambda e: e.copy(etmpb, tempS), reads=[res["tempS"]], writes=[res["etmpb"]])
    bk = take_bank()
    mm_group([(bk["f32"][0:128, 0:1], hselB, etmpb)], bk,
             reads=[res["hselB"], res["etmpb"]], tag="ethb_mm")
    ve(lambda e, b=bk: e.tensor_copy(ethb, b["f32"][0:128, 0:1]),
       reads=[bk["res"]], writes=[res["ethb"]])


    def emit_A_tail(t):
        bnm = A_banks.pop(t)
        bms = bnm
        sA, sB, sC = ((st_a, st_b, st_c) if t % 2 == 0 else (st_a2, st_b2, st_c2))
        rA, rB, rC = (("st_a", "st_b", "st_c") if t % 2 == 0
                      else ("st_a2", "st_b2", "st_c2"))
        ae(lambda e, b=bnm, sA=sA: e.activation(sA, b["f32"][0:1, 0:TILE], AF.Square),
           reads=[bnm["res"]], writes=[res[rA]], tag=f"A_m2{t}")
        ve(lambda e, b=bms, sA=sA, sB=sB: e.tensor_sub(sB, b["f32"][32:33, 0:TILE], sA),
           reads=[bms["res"], res[rA]], writes=[res[rB]], tag=f"A_var{t}")
        ve(lambda e, sA=sA, sB=sB: e.tensor_scalar(sA, sB, 1.0 + EPS_RMS,
                                                   EPS_LN * EPS_RMS,
                                                   ALU.mult, ALU.add),
           reads=[res[rB]], writes=[res[rA]])
        ae(lambda e, sA=sA: e.activation(sA, sA, AF.Ln), writes=[res[rA]])
        ae(lambda e, sA=sA, sC=sC: e.activation(sC, sA, AF.Exp, scale=-0.5),
           reads=[res[rA]], writes=[res[rC]], tag=f"A_s{t}")
        ae(lambda e, b=bnm, t=t: e.copy(negmBs[t % 2], b["f32"][0:1, 0:TILE]),
           reads=[bnm["res"]], writes=[negmBs_res[t % 2]], tag=f"A_negmB{t}")
        bsc = take_bank()
        mm_group([(bsc["f32"][:, 0:TILE], ones1F, sC)], bsc,
                 reads=[res[rC], res["ones1F"]], tag=f"A_sbc{t}")
        ve(lambda e, b=bsc, t=t: e.tensor_copy(ssbAs[t % 2], b["f32"][:, 0:TILE]),
           reads=[bsc["res"]], writes=[ssbAs_res[t % 2]], tag=f"A_ssb{t}")
        unpin(bnm)

    bG = take_bank(pin=True)
    qtms = [qtm, qtm2]
    ktms = [ktm, ktm2]
    qtms_res = [res["qtm"], res["qtm2"]]
    ktms_res = [res["ktm"], res["ktm2"]]

    def emit_A_gram(t):
        for b_ in range(TILE // 128):
            i = t * (TILE // 128) + b_
            bt1 = take_bank()
            te(lambda e, b=bt1, i=i: e.transpose(b["b16"][:, 0:128],
                                                 qfm[:, i * 128:(i + 1) * 128], identb),
               reads=[res["qfm"], res["identb"]], writes=[bt1["res"]], tag=f"G_tq{i}")
            ve(lambda e, b=bt1, i=i: e.tensor_copy(qtms[i % 2], b["b16"][:, 0:128]),
               reads=[bt1["res"]], writes=[qtms_res[i % 2]])
            bt2 = take_bank()
            te(lambda e, b=bt2, i=i: e.transpose(b["b16"][:, 0:128],
                                                 kfm[:, i * 128:(i + 1) * 128], identb),
               reads=[res["kfm"], res["identb"]], writes=[bt2["res"]], tag=f"G_tk{i}")
            ve(lambda e, b=bt2, i=i: e.tensor_copy(ktms[i % 2], b["b16"][:, 0:128]),
               reads=[bt2["res"]], writes=[ktms_res[i % 2]])
            te(lambda e, b=bG, i=i: e.matmul(b["f32"][:, 0:128], qtms[i % 2], ktms[i % 2],
                                             start=(i == 0), stop=(i == 31)),
               reads=[qtms_res[i % 2], ktms_res[i % 2]], writes=[bG["res"]],
               tag=f"G_mm{i}")

    def emit_A_qkv(t):
        xb = xb16s[t % 3]
        xb_res = xb16s_res[t % 3]
        for oc in range(3):
            bq = take_bank()
            ops = [(bq["f32"][:, 0:TILE],
                    wqkvTb[:, c * 384 + oc * 128: c * 384 + (oc + 1) * 128],
                    rch(xb, c, TILE)) for c in range(NCH)]
            ops.append((bq["f32"][:, 0:TILE], gSqkvB[:, oc * 128:(oc + 1) * 128],
                        negmBs[t % 2]))
            mm_group(ops, bq, reads=[res["wqkvTb"], xb_res, res["gSqkvB"],
                                     negmBs_res[t % 2]], tag=f"A_qkv{t}_{oc}")
            dst, dres = qkv_dst[oc]
            ve(lambda e, b=bq, dst=dst, t=t:
               e.tensor_mul(dst[:, t * TILE:(t + 1) * TILE], b["f32"][:, 0:TILE],
                            ssbAs[t % 2]),
               reads=[bq["res"], ssbAs_res[t % 2]], writes=[dres], tag=f"A_ev{t}_{oc}")

    def emit_A_nsq(t):
        for (fm2, fres2, acc, accr) in ((qfm, res["qfm"], sqQa, "sqQa"),
                                        (kfm, res["kfm"], sqKa, "sqKa")):
            ae(lambda e, fm2=fm2, t=t: e.activation(nsq, fm2[:, t * TILE:(t + 1) * TILE],
                                                    AF.Square),
               reads=[fres2], writes=[res["nsq"]], tag=f"A_nsq{t}")
            ve(lambda e: e.reduce_sum(nred, nsq, AX.X),
               reads=[res["nsq"]], writes=[res["nred"]], tag=f"A_nred{t}")
            ve(lambda e, acc=acc: e.tensor_add(acc, acc, nred),
               reads=[res["nred"]], writes=[res[accr]], tag=f"A_nacc{t}")

    emit_A_tail(0)
    for t in range(NT):
        if t + 2 < NT:
            emit_A_head(t + 2)
        if t >= 1:
            emit_A_nsq(t - 1)
        if t + 1 < NT:
            emit_A_tail(t + 1)
        emit_A_qkv(t)
        emit_A_gram(t)
    emit_A_nsq(NT - 1)

    tap("qfm", qfm, res["qfm"])
    tap("kfm", kfm, res["kfm"])
    tap("vfm", vfm, res["vfm"])

    for (acc, accr, which) in ((sqQa, "sqQa", "q"), (sqKa, "sqKa", "k")):
        ae(lambda e, acc=acc: e.activation(nrmA, acc, AF.Sqrt),
           reads=[res[accr]], writes=[res["nrmA"]])
        ve(lambda e: e.tensor_scalar_max(nrmA, nrmA, 1e-12), writes=[res["nrmA"]])
        ve(lambda e: e.reciprocal(nrmA, nrmA), writes=[res["nrmA"]])
        if which == "q":
            ve(lambda e: e.tensor_mul(sclA, nrmA, ethb),
               reads=[res["nrmA"], res["ethb"]], writes=[res["sclA"]], tag="N_qscale")
        else:
            ve(lambda e: e.tensor_copy(rkS, nrmA),
               reads=[res["nrmA"]], writes=[res["rkS"]], tag="N_kscale")

    brt = take_bank()
    te(lambda e, b=brt: e.transpose(b["f32"][0:1, 0:128], rkS, identf),
       reads=[res["rkS"], res["identf"]], writes=[brt["res"]], tag="S_rkT")
    ae(lambda e, b=brt: e.copy(st_a[:, 0:128], b["f32"][0:1, 0:128]),
       reads=[brt["res"]], writes=[res["st_a"]], tag="S_rkrow")
    bcb = take_bank()
    mm_group([(bcb["f32"][:, 0:128], ones1F, st_a[:, 0:128])], bcb,
             reads=[res["st_a"], res["ones1F"]], tag="S_colmm")
    ae(lambda e, b=bcb: e.copy(ssbA[:, 0:128], b["f32"][:, 0:128]),
       reads=[bcb["res"]], writes=[res["ssbA"]], tag="S_colcp")
    ve(lambda e, b=bG: e.scalar_tensor_tensor(simsb, b["f32"][:, 0:128], sclA,
                                              ssbA[:, 0:128],
                                              ALU.mult, ALU.mult),
       reads=[bG["res"], res["sclA"], res["ssbA"]], writes=[res["simsb"]],
       tag="S_scale")
    unpin(bG)
    ve(lambda e: e.tensor_add(simsb, simsb, maskS),
       reads=[res["maskS"]], writes=[res["simsb"]], tag="S_mask")
    ve(lambda e: e.reduce_max(smx1, simsb, AX.X, negate=True),
       reads=[res["simsb"]], writes=[res["smx1"]])
    ae(lambda e: e.activation(simsb, simsb, AF.Exp, bias=smx1),
       reads=[res["smx1"]], writes=[res["simsb"]], tag="S_exp")
    ve(lambda e: e.reduce_sum(smx2, simsb, AX.X),
       reads=[res["simsb"]], writes=[res["smx2"]])
    ve(lambda e: e.reciprocal(smx2, smx2), writes=[res["smx2"]])
    ve(lambda e: e.tensor_scalar_mul(Ab, simsb, smx2),
       reads=[res["simsb"], res["smx2"]], writes=[res["Ab"]], tag="S_A")
    tap("Ab", Ab, res["Ab"])
    for half in range(2):
        bw = take_bank()
        mm_group([(bw["f32"][:, 0:512], Ab, woutTb[:, half * 512:(half + 1) * 512])],
                 bw, reads=[res["Ab"], res["woutTb"]], tag=f"WAT{half}")
        ve(lambda e, b=bw, half=half:
           e.tensor_copy(WATb[:, half * 512:(half + 1) * 512], b["f32"][:, 0:512]),
           reads=[bw["res"]], writes=[res["WATb"]], tag=f"WATev{half}")
    tap("WATb", WATb, res["WATb"])

    for nm, dst, rkey in [("R2T", R2Tb, "R2Tb"),
                          ("Vf", Vfb, "Vfb"), ("Wf", Wfb, "Wfb"),
                          ("bfw", bfb, "bfb"), ("biw", bib, "bib"),
                          ("Esel", EselB, "EselB")]:
        wload(dst, nm, rkey)
    dma("gpsimd", Vixb, dscr["VIX2"], reads=[res["scr_VIX2"]],
        writes=[res["Vixb"]], tag="ld_VIX2")
    dma("gpsimd", Wiyb, dscr["WIY2"], reads=[res["scr_WIY2"]],
        writes=[res["Wiyb"]], tag="ld_WIY2")
    dma("gpsimd", r1bS, dw["r1b"], writes=[res["r1bS"]])
    dma("gpsimd", r2bS, dw["r2b"], writes=[res["r2bS"]])
    dma("gpsimd", ebS, dw["eb"], writes=[res["ebS"]])
    dma("gpsimd", alphaS, dw["alpha"], writes=[res["alphaS"]])
    dma("gpsimd", nvinvS, dw["nvinv"], writes=[res["nvinvS"]])

    ve(lambda e: e.tensor_add(bias2, r2bS, ebS),
       reads=[res["r2bS"], res["ebS"]], writes=[res["bias2"]], tag="bias2")
    bk = take_bank()
    mm_group([(bk["f32"][0:128, 0:1], ones1F, alphaS)], bk,
             reads=[res["ones1F"], res["alphaS"]], tag="alpha_mm")
    ve(lambda e, b=bk: e.tensor_copy(alphab, b["f32"][0:128, 0:1]),
       reads=[bk["res"]], writes=[res["alphab"]])
    ve(lambda e: e.tensor_scalar_mul(bib, bib, alphab[0:8, 0:1]),
       reads=[res["alphab"]], writes=[res["bib"]], tag="abi")

    for nm in ("WQT", "WKT", "WOT"):
        buf = wstr[0] if nm != "WKT" else wstr[1]
        bres = wstr_res[0] if nm != "WKT" else wstr_res[1]
        dma("gpsimd", buf, dw[nm], writes=[bres], tag=f"precast_ld_{nm}")
        dma("gpsimd", dscr[nm], buf, reads=[bres], writes=[res["scr_" + nm]],
            tag=f"precast_st_{nm}")
    buf, bres = wstr[1], wstr_res[1]
    dma("gpsimd", buf[:, 0:NKRCH * D], dw["Uf"], writes=[bres], tag="precast_ld_Uf")
    dma("gpsimd", buf[:, NKRCH * D:2 * NKRCH * D], dw["Ui"], writes=[bres],
        tag="precast_ld_Ui")
    ve(lambda e, buf=buf: e.tensor_scalar_mul(
        buf[:, NKRCH * D:2 * NKRCH * D], buf[:, NKRCH * D:2 * NKRCH * D], alphab),
       reads=[res["alphab"]], writes=[bres], tag="precast_aUi")
    dma("gpsimd", dscr["UFUI"], buf, reads=[bres], writes=[res["scr_UFUI"]],
        tag="precast_st_UFUI")

    def _steps_of(*names):
        out, seen = [], set()
        for nm_ in names:
            r_ = res[nm_]
            for st_ in r_.writers + r_.readers:
                if id(st_) not in seen:
                    seen.add(id(st_))
                    out.append(st_)
        return out

    fence0 = _steps_of("xt0")
    fence1 = _steps_of("xt1")

    arena = _Arena(arena_base)
    ybufs = [arena.alloc("ybuf0", [128, NCH * SB], F32),
             arena.alloc("ybuf1", [128, NCH * SB], F32)]
    ybufs_res = [res["ybuf0"], res["ybuf1"]]
    yb16 = arena.alloc("yb16", [128, NCH * SB], BF16)
    lntmp = arena.alloc("lntmp", [128, NCH * SB], BF16)
    lntmpb = lntmp
    Qfm = arena.alloc("Qfm", [128, NCH * SB], BF16)
    Kfm = arena.alloc("Kfm", [128, NCH * SB], BF16)
    hsb = [arena.alloc("hsb0", [128, SB], BF16), arena.alloc("hsb1", [128, SB], BF16)]
    hsb_res = [res["hsb0"], res["hsb1"]]
    lgsbF = arena.alloc("lgsbF", [8, SB], BF16)
    lgsbI = arena.alloc("lgsbI", [8, SB], BF16)
    wfmF = arena.alloc("wfmF", [8, SB], BF16)
    wfmI = arena.alloc("wfmI", [8, SB], BF16)
    cumS = arena.alloc("cumS", [128, NKRCH * SB], F32)
    P2F = arena.alloc("P2F", [128, NKRCH * SB], BF16)
    P2I = arena.alloc("P2I", [128, NKRCH * SB], BF16)
    ff16 = arena.alloc("ff16", [128, NCH * SB], BF16)

    def wuse(s, nm):
        idx = list(STREAM_W).index(nm)
        return wstr[idx % 3], wstr_res[idx % 3]

    def wstream(s, nm):
        buf, bres = wuse(s, nm)
        dma("gpsimd", buf, dscr[nm], reads=[res["scr_" + nm]], writes=[bres],
            tag=f"B_wld{s}_{nm}")

    def emit_load(s, extra=()):
        yb = ybufs[s % 2]
        P.add("sync",
              lambda e, yb=yb, s=s: e.dma_start(
                  out=yb.rearrange("p (c t) -> p c t", c=NCH),
                  in_=dx.rearrange("p (c t) -> p c t", c=NCH)[:, :, s * SB:(s + 1) * SB]),
              writes=[ybufs_res[s % 2]], extra_deps=extra, tag=f"B_x{s}")

    def emit_proj(s):
        yb = ybufs[s % 2]
        yb_res = ybufs_res[s % 2]
        for oc in range(NCH):
            bp = take_bank()
            mm_group([(bp["f32"][:, 0:SB], WATb[:, oc * 128:(oc + 1) * 128],
                       vfm[:, s * SB:(s + 1) * SB])], bp,
                     reads=[res["WATb"], res["vfm"]], tag=f"B_proj{s}_{oc}")
            ve(lambda e, b=bp, oc=oc, yb=yb:
               e.tensor_add(yb[:, oc * SB:(oc + 1) * SB],
                            yb[:, oc * SB:(oc + 1) * SB], b["f32"][:, 0:SB]),
               reads=[bp["res"]], writes=[yb_res], tag=f"B_res{s}_{oc}")

    B_banks = {}

    def emit_stats_head(s):
        yb = ybufs[s % 2]
        yb_res = ybufs_res[s % 2]
        ae(lambda e, yb=yb: e.copy(yb16, yb),
           reads=[yb_res], writes=[res["yb16"]], tag=f"B_cast{s}")
        bnm = take_bank(pin=True)
        mm_group([(bnm["f32"][0:1, 0:SB], negidB, rch(yb16, c)) for c in range(NCH)],
                 bnm, reads=[res["yb16"], res["negidB"]], tag=f"B_negm{s}")
        ae(lambda e: e.activation(lntmp, yb16, AF.Square),
           reads=[res["yb16"]], writes=[res["lntmp"]], tag=f"B_sq{s}")
        bms = take_bank(pin=True)
        mm_group([(bms["f32"][0:1, 0:SB], invdB, rch(lntmp, c)) for c in range(NCH)],
                 bms, reads=[res["lntmp"], res["invdB"]], tag=f"B_msq{s}")
        B_banks[s] = (bnm, bms)

    def emit_stats_tail(s):
        yb = ybufs[s % 2]
        yb_res = ybufs_res[s % 2]
        bnm, bms = B_banks.pop(s)
        ae(lambda e, b=bnm: e.activation(st_a, b["f32"][0:1, 0:SB], AF.Square),
           reads=[bnm["res"]], writes=[res["st_a"]], tag=f"B_m2{s}")
        ve(lambda e, b=bms: e.tensor_sub(st_b, b["f32"][0:1, 0:SB], st_a),
           reads=[bms["res"], res["st_a"]], writes=[res["st_b"]])
        ve(lambda e: e.tensor_scalar_add(st_a, st_b, EPS_LN),
           reads=[res["st_b"]], writes=[res["st_a"]])
        ae(lambda e: e.activation(st_a, st_a, AF.Ln), writes=[res["st_a"]])
        ae(lambda e: e.activation(st_c, st_a, AF.Exp, scale=-0.5),
           reads=[res["st_a"]], writes=[res["st_c"]], tag=f"B_s{s}")
        ae(lambda e, b=bnm: e.copy(st_b, b["f32"][0:1, 0:SB]),
           reads=[bnm["res"]], writes=[res["st_b"]], tag=f"B_negm_sb{s}")
        unpin(bnm)
        unpin(bms)

    def emit_stats_bcast(s):
        yb = ybufs[s % 2]
        yb_res = ybufs_res[s % 2]
        bnb = take_bank()
        mm_group([(bnb["f32"][:, 0:SB], ones1F, st_b)], bnb,
                 reads=[res["st_b"], res["ones1F"]], tag=f"B_nmb{s}")
        bsb = take_bank()
        mm_group([(bsb["f32"][:, 0:SB], ones1F, st_c)], bsb,
                 reads=[res["st_c"], res["ones1F"]], tag=f"B_sbc{s}")
        ve(lambda e, b=bnb, yb=yb:
           e.tensor_add(lntmp.rearrange("p (c t) -> p c t", c=NCH),
                        yb.rearrange("p (c t) -> p c t", c=NCH),
                        b["f32"][:, 0:SB].unsqueeze(1).broadcast_to([128, NCH, SB]),
                        ),
           reads=[bnb["res"], yb_res], writes=[res["lntmp"]], tag=f"B_ln1_{s}")
        ve(lambda e, b=bsb:
           e.tensor_mul(yb16.rearrange("p (c t) -> p c t", c=NCH),
                        lntmp.rearrange("p (c t) -> p c t", c=NCH),
                        b["f32"][:, 0:SB].unsqueeze(1).broadcast_to([128, NCH, SB])),
           reads=[bsb["res"], res["lntmp"]], writes=[res["yb16"]], tag=f"B_ln2_{s}")

    def emit_qk(s):
        for (nm, dst, dkey) in (("WQT", Qfm, "Qfm"), ("WKT", Kfm, "Kfm")):
            wt, wres = wuse(s, nm)
            for oc in range(NCH):
                bq = take_bank()
                mm_group([(bq["f32"][:, 0:SB],
                           wt[:, c * D + oc * 128: c * D + (oc + 1) * 128],
                           rch(yb16, c)) for c in range(NCH)], bq,
                         reads=[wres, res["yb16"]], tag=f"B_{nm}{s}_{oc}")
                ae(lambda e, b=bq, dst=dst, oc=oc:
                   e.copy(dst[:, oc * SB:(oc + 1) * SB], b["f32"][:, 0:SB]),
                   reads=[bq["res"]], writes=[res[dkey]], tag=f"B_{nm}ev{s}_{oc}")

    def emit_h(s, bs, wname, lgsb, lgkey):
        wt, wres = wuse(s, wname)
        blg = take_bank(pin=True)
        for jc in range(NCH):
            bh = take_bank()
            mm_group([(bh["f32"][:, 0:SB], wt[:, c * D + jc * 128: c * D + (jc + 1) * 128],
                       rch(Qfm, c)) for c in range(NCH)], bh,
                     reads=[wres, res["Qfm"]], tag=f"B{bs}_h{s}_{jc}")
            hb = hsb[jc % 2]
            hbr = hsb_res[jc % 2]
            ae(lambda e, b=bh, hb=hb, jc=jc:
               e.activation(hb, b["f32"][:, 0:SB], AF.Gelu, bias=r1bS[:, jc:jc + 1]),
               reads=[bh["res"], res["r1bS"]], writes=[hbr], tag=f"B{bs}_hev{s}_{jc}")
            te(lambda e, b=blg, hb=hb, jc=jc:
               e.matmul(b["f32"][0:8, 0:SB], R2Tb[:, jc * 8:(jc + 1) * 8], hb,
                        start=(jc == 0), stop=(jc == NCH - 1)),
               reads=[hbr, res["R2Tb"]], writes=[blg["res"]], tag=f"B{bs}_lg{s}_{jc}")
        ve(lambda e, b=blg, lgsb=lgsb: e.tensor_scalar_add(lgsb, b["f32"][0:8, 0:SB], bias2),
           reads=[blg["res"], res["bias2"]], writes=[res[lgkey]], tag=f"B{bs}_lgev{s}")
        unpin(blg)

    def emit_smx_pre(s, bs, lgsb, lgkey, wts, wtsk):
        btA = take_bank(pin=True)
        for bi in range(SB // 128):
            te(lambda e, b=btA, bi=bi, lgsb=lgsb:
               e.transpose(b["b16"][:, bi * 8:(bi + 1) * 8],
                           lgsb[:, bi * 128:(bi + 1) * 128], identb[0:8, 0:8]),
               reads=[res[lgkey], res["identb"]], writes=[btA["res"]],
               tag=f"B{bs}_lt{s}_{bi}")
        ve(lambda e, b=btA:
           e.reduce_max(smxM, b["b16"][:, 0:32].rearrange("p (c e) -> p c e", e=8),
                        AX.X, negate=True),
           reads=[btA["res"]], writes=[res["smxM"]], tag=f"B{bs}_smax{s}")
        ve(lambda e, b=btA, wts=wts:
           e.tensor_add(wts.rearrange("p (c e) -> p c e", e=8),
                        b["b16"][:, 0:32].rearrange("p (c e) -> p c e", e=8),
                        smxM.unsqueeze(2).broadcast_to([128, 4, 8])),
           reads=[btA["res"], res["smxM"]], writes=[res[wtsk]],
           tag=f"B{bs}_ssub{s}")
        unpin(btA)

    def emit_smx_mid(s, bs, wts, wtsk):
        ae(lambda e, wts=wts: e.activation(wts, wts, AF.Exp, bias=0.0),
           writes=[res[wtsk]], tag=f"B{bs}_wexp{s}")
        for c in range(SB // 128):
            ve(lambda e, c=c, wts=wts: e.reduce_sum(sumM[:, c:c + 1],
                                                    wts[:, c * 8:(c + 1) * 8], AX.X),
               reads=[res[wtsk]], writes=[res["sumM"]], tag=f"B{bs}_ssum{s}_{c}")
        ve(lambda e: e.reciprocal(sumM, sumM), writes=[res["sumM"]],
           tag=f"B{bs}_srec{s}")
        ve(lambda e, s=s: e.tensor_mul(smxQ, nvinvS[:, s * 4:(s + 1) * 4], sumM),
           reads=[res["nvinvS"], res["sumM"]], writes=[res["smxQ"]],
           tag=f"B{bs}_snv{s}")
        ve(lambda e, wts=wts: e.tensor_mul(wts.rearrange("p (c e) -> p c e", e=8),
                                           wts.rearrange("p (c e) -> p c e", e=8),
                                           smxQ.unsqueeze(2).broadcast_to([128, 4, 8])),
           reads=[res["smxQ"]], writes=[res[wtsk]], tag=f"B{bs}_wts{s}")

    def emit_smx_back(s, bs, wfm, wfmk, wts, wtsk):
        btB = take_bank()
        for bi in range(SB // 128):
            te(lambda e, b=btB, bi=bi, wts=wts:
               e.transpose(b["b16"][0:8, bi * 128:(bi + 1) * 128],
                           wts[:, bi * 8:(bi + 1) * 8], identb),
               reads=[res[wtsk], res["identb"]], writes=[btB["res"]],
               tag=f"B{bs}_wt{s}_{bi}")
        ae(lambda e, b=btB, wfm=wfm: e.copy(wfm, b["b16"][0:8, 0:SB]),
           reads=[btB["res"]], writes=[res[wfmk]], tag=f"B{bs}_wfm{s}")

    def emit_ywxv(s, bs, Vp, Vpk, Wp, Wpk, carry, carryk, hook=None):
        for kc in range(NKRCH):
            byw = take_bank()
            mm_group([(byw["f32"][:, 0:SB],
                       Wp[:, c * KR + kc * 128: c * KR + (kc + 1) * 128],
                       rch(Kfm, c)) for c in range(NCH)], byw,
                     reads=[res[Wpk], res["Kfm"]], tag=f"B{bs}_yw{s}_{kc}")
            ve(lambda e, b=byw, kc=kc, carry=carry, s=s:
               e.tensor_tensor_scan(rch(cumS, kc), onesColF.broadcast_to([128, SB]),
                                    b["f32"][:, 0:SB],
                                    carry[:, kc:kc + 1] if s > 0 else 0.0,
                                    ALU.mult, ALU.add),
               reads=[byw["res"], res["onesColF"]] + ([res[carryk]] if s > 0 else []),
               writes=[res["cumS"]], tag=f"B{bs}_scan{s}_{kc}")
            ve(lambda e, kc=kc, carry=carry:
               e.tensor_copy(carry[:, kc:kc + 1], rch(cumS, kc)[:, SB - 1:SB]),
               reads=[res["cumS"]], writes=[res[carryk]], tag=f"B{bs}_carry{s}_{kc}")
            bxv = take_bank()
            mm_group([(bxv["f32"][:, 0:SB],
                       Vp[:, c * KR + kc * 128: c * KR + (kc + 1) * 128],
                       rch(Qfm, c)) for c in range(NCH)], bxv,
                     reads=[res[Vpk], res["Qfm"]], tag=f"B{bs}_xv{s}_{kc}")
            if kc == 0 and hook is not None:
                hook()
            ve(lambda e, b=bxv, kc=kc: e.tensor_tensor(rch(cumS, kc), rch(cumS, kc),
                                                       b["f32"][:, 0:SB], ALU.mult),
               reads=[bxv["res"]], writes=[res["cumS"]], tag=f"B{bs}_p1{s}_{kc}")
            bwr = take_bank()
            mm_group([(bwr["f32"][:, 0:SB], EselB[:, kc * 128:(kc + 1) * 128], wfmF if bs == "F" else wfmI)],
                     bwr, reads=[res["EselB"], res["wfmF" if bs == "F" else "wfmI"]],
                     tag=f"B{bs}_wrep{s}_{kc}")
            P2, P2k = (P2F, "P2F") if bs == "F" else (P2I, "P2I")
            ve(lambda e, b=bwr, kc=kc, P2=P2:
               e.tensor_mul(rch(P2, kc), rch(cumS, kc), b["f32"][:, 0:SB]),
               reads=[res["cumS"], bwr["res"]], writes=[res[P2k]],
               tag=f"B{bs}_p2{s}_{kc}")

    def emit_out(s):
        wt, wres = wuse(s, "UFUI")
        ufo = 0
        uio = NKRCH * D
        for dc in range(NCH):
            bo = take_bank()
            ops = []
            out = bo["f32"][:, 0:SB]
            for kc in range(NKRCH):
                ops.append((out, wt[:, ufo + kc * D + dc * 128: ufo + kc * D + (dc + 1) * 128],
                            rch(P2F, kc)))
            ops.append((out, bfb[0:8, dc * 128:(dc + 1) * 128], wfmF))
            for kc in range(NKRCH):
                ops.append((out, wt[:, uio + kc * D + dc * 128: uio + kc * D + (dc + 1) * 128],
                            rch(P2I, kc)))
            ops.append((out, bib[0:8, dc * 128:(dc + 1) * 128], wfmI))
            mm_group(ops, bo,
                     reads=[wres, res["bfb"], res["bib"],
                            res["P2F"], res["P2I"], res["wfmF"], res["wfmI"]],
                     tag=f"B_out{s}_{dc}")
            ae(lambda e, b=bo, dc=dc: e.copy(ff16[:, dc * SB:(dc + 1) * SB],
                                             b["f32"][:, 0:SB]),
               reads=[bo["res"]], writes=[res["ff16"]], tag=f"B_outev{s}_{dc}")

    def emit_wo(s):
        yb = ybufs[s % 2]
        yb_res = ybufs_res[s % 2]
        wt, wres = wuse(s, "WOT")
        for oc in range(NCH):
            bw = take_bank()
            mm_group([(bw["f32"][:, 0:SB], wt[:, c * D + oc * 128: c * D + (oc + 1) * 128],
                       rch(ff16, c)) for c in range(NCH)], bw,
                     reads=[wres, res["ff16"]], tag=f"B_wo{s}_{oc}")
            ve(lambda e, b=bw, oc=oc, yb=yb:
               e.tensor_add(yb[:, oc * SB:(oc + 1) * SB],
                            yb[:, oc * SB:(oc + 1) * SB], b["f32"][:, 0:SB]),
               reads=[bw["res"]], writes=[yb_res], tag=f"B_wores{s}_{oc}")
        dma("sync", dout.rearrange("p (c t) -> p c t", c=NCH)[:, :, s * SB:(s + 1) * SB],
            yb.rearrange("p (c t) -> p c t", c=NCH),
            reads=[yb_res], writes=[out_res], tag=f"B_store{s}")

    emit_load(0, extra=fence0)
    wstream(0, "WQT")
    wstream(0, "WKT")
    wstream(0, "R1T")
    emit_proj(0)
    emit_stats_head(0)
    emit_stats_tail(0)
    emit_stats_bcast(0)
    for s in range(NSB):
        emit_qk(s)
        wstream(s, "R1TI")
        if s + 1 < NSB:
            emit_load(s + 1, extra=(fence1 if s + 1 == 1 else ()))
            emit_proj(s + 1)
        emit_h(s, "F", "R1T", lgsbF, "lgsbF")
        emit_smx_pre(s, "F", lgsbF, "lgsbF", wtsBF, "wtsBF")
        wstream(s, "UFUI")
        wstream(s, "WOT")
        emit_h(s, "I", "R1TI", lgsbI, "lgsbI")
        emit_smx_pre(s, "I", lgsbI, "lgsbI", wtsBI, "wtsBI")
        emit_smx_mid(s, "F", wtsBF, "wtsBF")
        if s + 1 < NSB:
            wstream(s + 1, "WQT")
        emit_smx_mid(s, "I", wtsBI, "wtsBI")
        emit_ywxv(s, "F", Vfb, "Vfb", Wfb, "Wfb", carryF, "carryF",
                  hook=lambda s=s: emit_smx_back(s, "F", wfmF, "wfmF",
                                                 wtsBF, "wtsBF"))
        emit_ywxv(s, "I", Vixb, "Vixb", Wiyb, "Wiyb", carryI, "carryI",
                  hook=lambda s=s: emit_smx_back(s, "I", wfmI, "wfmI",
                                                 wtsBI, "wtsBI"))
        if s + 1 < NSB:
            emit_stats_head(s + 1)
            emit_stats_tail(s + 1)
        emit_out(s)
        if s + 1 < NSB:
            wstream(s + 1, "WKT")
            emit_stats_bcast(s + 1)
        emit_wo(s)
        if s + 1 < NSB:
            wstream(s + 1, "R1T")
        if s == 0:
            tap("y0", ybufs[0], ybufs_res[0])
            tap("Q0", Qfm, res["Qfm"])
            tap("cum0", cumS, res["cumS"])

    P.add("sync", lambda e: e.nop(), reads=[out_res], tag="final_wait")

    P.finalize()

    with (nc.semaphore("sem_t") as s_t, nc.semaphore("sem_v") as s_v,
          nc.semaphore("sem_a") as s_a, nc.semaphore("sem_s") as s_s,
          nc.semaphore("sem_g") as s_g, nc.semaphore("sem_d") as s_d,
          nc.Block() as block):
        sems = {"tensor": s_t, "vector": s_v, "scalar": s_a,
                "sync": s_s, "gpsimd": s_g, "sdma": s_d}

        @block.tensor
        def _(eng):
            P.emit_engine("tensor", eng, sems)

        @block.vector
        def _(eng):
            P.emit_engine("vector", eng, sems)

        @block.scalar
        def _(eng):
            P.emit_engine("scalar", eng, sems)

        @block.sync
        def _(eng):
            P.emit_engine("sync", eng, sems)

        @block.gpsimd
        def _(eng):
            P.emit_engine("gpsimd", eng, sems)

    return nc, tap_drams


def kernel(**inputs):
    xs, w = _host_prep(inputs)
    taps = tuple(_DEBUG_TAPS)
    nc, tap_drams = build_program(taps)
    in_maps = []
    for b in range(8):
        m = {"xT": xs[b]}
        m.update(w)
        in_maps.append(m)
    r = run_bass_kernel_spmd(nc, in_maps, core_ids=list(range(8)))
    outs = []
    for b in range(8):
        o = np.asarray(r.results[b]["outT"], dtype=np.float32)
        outs.append(o.reshape(128, NCH, T).transpose(1, 0, 2).reshape(D, T).T)
    if taps:
        kernel.last_taps = [{k: np.asarray(r.results[b]["tap_" + k]) for k in taps
                             if ("tap_" + k) in r.results[b]} for b in range(8)]
    return np.ascontiguousarray(np.stack(outs, axis=0))


# version 51
# speedup vs baseline: 1.1779x; 1.1325x over previous
import numpy as np

import concourse.bass as bass
import concourse.mybir as mybir
import concourse.bass_utils as _bu
from concourse.bass_utils import run_bass_kernel_spmd

if not getattr(_bu, "_ldw_patch", False):
    _orig_run_command = _bu.run_command

    def _run_command_ldw(argv, **kw):
        if isinstance(argv, list):
            argv = ["--enable-ldw-opt=true" if a == "--enable-ldw-opt=false" else a
                    for a in argv]
        return _orig_run_command(argv, **kw)

    _bu.run_command = _run_command_ldw
    _bu._ldw_patch = True

F32 = mybir.dt.float32
BF16 = mybir.dt.bfloat16
AF = mybir.ActivationFunctionType
ALU = mybir.AluOpType
AX = mybir.AxisListType

D = 1024
T = 4096
NCH = 8
HEADS = 8
KEXP = 8
R = 64
KR = 512
NKRCH = 4
TILE = 512
NT = T // TILE
SB = 512
NSB = T // SB
EPS_LN = 1e-5
EPS_RMS = 1e-6

_ENGINES = ("tensor", "vector", "scalar", "sync", "gpsimd")
_DMA_ENGINES = ("sync", "gpsimd")

_DEBUG_TAPS = []


class _Step:
    __slots__ = ("eng", "emit", "deps", "inc_needed", "sem_val", "tag", "semkey")

    def __init__(self, eng, emit, tag="", semkey=None):
        self.eng = eng
        self.emit = emit
        self.deps = []
        self.inc_needed = eng in _DMA_ENGINES or semkey == "sdma"
        self.sem_val = None
        self.tag = tag
        self.semkey = semkey or eng


class _Res:
    __slots__ = ("writers", "readers", "rsw", "name")

    def __init__(self, name=""):
        self.writers = []
        self.readers = []
        self.rsw = False
        self.name = name


class _Plan:
    def __init__(self):
        self.steps = []

    def add(self, eng, emit, reads=(), writes=(), tag="", extra_deps=(), semkey=None):
        st = _Step(eng, emit, tag, semkey)
        deps = []
        for r in reads:
            deps.extend(r.writers)
        for w in writes:
            deps.extend(w.writers)
            deps.extend(w.readers)
        deps.extend(extra_deps)
        seen = set()
        for d in deps:
            if d is st or id(d) in seen:
                continue
            seen.add(id(d))
            st.deps.append(d)
        for r in reads:
            r.readers.append(st)
            r.rsw = True
        for w in writes:
            if w.rsw:
                w.writers = [st]
                w.readers = []
                w.rsw = False
            else:
                w.writers.append(st)
        self.steps.append(st)
        return st

    def finalize(self):
        for st in self.steps:
            for d in st.deps:
                d.inc_needed = True
        counters = {e: 0 for e in _ENGINES}
        counters["sdma"] = 0
        for st in self.steps:
            if st.inc_needed:
                dma = st.eng in _DMA_ENGINES or st.semkey == "sdma"
                counters[st.semkey] += 16 if dma else 1
                st.sem_val = counters[st.semkey]

    def emit_engine(self, name, eng, sems):
        watermark = {}
        for st in self.steps:
            if st.eng != name:
                continue
            for d in st.deps:
                if watermark.get(d.semkey, 0) >= d.sem_val:
                    continue
                eng.wait_ge(sems[d.semkey], d.sem_val)
                watermark[d.semkey] = d.sem_val
            insts = st.emit(eng)
            if st.inc_needed:
                if not isinstance(insts, (list, tuple)):
                    insts = [insts]
                if name in _DMA_ENGINES:
                    for i in insts:
                        i.then_inc(sems[name], 16)
                    eng.wait_ge(sems[name], st.sem_val)
                    watermark[name] = st.sem_val
                elif st.semkey == "sdma":
                    for i in insts:
                        i.then_inc(sems["sdma"], 16)
                else:
                    insts[-1].then_inc(sems[name], 1)


def _pack_cm(m):
    r, c = m.shape
    return np.ascontiguousarray(
        m.reshape(r // 128, 128, c).transpose(1, 0, 2).reshape(128, -1))


def _host_prep(inp):
    g = {k: np.asarray(v, dtype=np.float32) for k, v in inp.items()}
    w = {}
    x = g["L0_attn_x"]
    xs = [_pack_cm(np.ascontiguousarray(x[b].T)) for b in range(8)]
    w["wqkvT"] = _pack_cm(np.ascontiguousarray(g["L0_attn_w_qkv"].T))
    w["woutT"] = _pack_cm(np.ascontiguousarray(g["L0_attn_w_out"].T))
    w["WQT"] = _pack_cm(np.ascontiguousarray(g["L0_ffn_W_Q"].T))
    w["WKT"] = _pack_cm(np.ascontiguousarray(g["L0_ffn_W_K"].T))
    w["WIN"] = _pack_cm(np.ascontiguousarray(g["L0_ffn_W_inv"]))
    w["R1T"] = _pack_cm(np.ascontiguousarray(g["L0_ffn_router_w1"].T))
    w["WOT"] = _pack_cm(np.ascontiguousarray(g["L0_ffn_W_O"].T))
    w["R2T"] = _pack_cm(np.ascontiguousarray(g["L0_ffn_router_w2"].T))
    w["Vf"] = _pack_cm(g["L0_ffn_V_fwd"].transpose(1, 0, 2).reshape(D, KR))
    w["Wf"] = _pack_cm(g["L0_ffn_W_fwd"].transpose(1, 0, 2).reshape(D, KR))
    w["Vix"] = _pack_cm(g["L0_ffn_W_inv_exp"].transpose(1, 0, 2).reshape(D, KR))
    w["Wiy"] = _pack_cm(g["L0_ffn_V_inv"].transpose(1, 0, 2).reshape(D, KR))
    w["Uf"] = _pack_cm(g["L0_ffn_U_fwd"].transpose(0, 2, 1).reshape(KR, D))
    w["Ui"] = _pack_cm(g["L0_ffn_U_inv"].transpose(0, 2, 1).reshape(KR, D))
    w["bfw"] = np.ascontiguousarray(g["L0_ffn_b_fwd"])
    w["biw"] = np.ascontiguousarray(g["L0_ffn_b_inv"])
    w["gamma"] = np.ascontiguousarray(g["L0_attn_norm_weight"].reshape(NCH, 128).T)
    w["r1b"] = np.ascontiguousarray(g["L0_ffn_router_b1"].reshape(NCH, 128).T)
    w["temp"] = np.ascontiguousarray(g["L0_attn_temperature"].reshape(8, 1))
    w["r2b"] = np.ascontiguousarray(g["L0_ffn_router_b2"].reshape(8, 1))
    w["eb"] = np.ascontiguousarray(g["L0_ffn_expert_bias"].reshape(8, 1))
    w["alpha"] = np.ascontiguousarray(g["L0_ffn_alpha_bi"].reshape(1, 1))
    w["ident"] = np.eye(128, dtype=np.float32)
    mask = np.full((128, 128), -30000.0, dtype=np.float32)
    for h in range(HEADS):
        mask[16 * h:16 * h + 16, 16 * h:16 * h + 16] = 0.0
    w["mask"] = mask
    nv = np.zeros((128, 32), dtype=np.float32)
    for blk in range(32):
        nv[:, blk] = 1.0 / (blk * 128 + np.arange(128) + 1.0)
    w["nvinv"] = nv
    esel = np.zeros((8, 512), dtype=np.float32)
    for c in range(4):
        for p in range(128):
            esel[2 * c + p // 64, c * 128 + p] = 1.0
    w["Esel"] = esel
    hsel = np.zeros((8, 128), dtype=np.float32)
    for p in range(128):
        hsel[p // 16, p] = 1.0
    w["hsel"] = hsel
    return xs, w


def build_program(taps=()):
    nc = bass.Bass("TRN2", target_bir_lowering=False)
    P = _Plan()

    dx = nc.dram_tensor("xT", [128, NCH * T], F32, kind="ExternalInput").ap()
    dout = nc.dram_tensor("outT", [128, NCH * T], F32, kind="ExternalOutput").ap()
    dw = {}
    for name, shape in [
        ("wqkvT", [128, NCH * 384]), ("woutT", [128, 1024]),
        ("WQT", [128, NCH * D]), ("WKT", [128, NCH * D]), ("WIN", [128, NCH * D]),
        ("R1T", [128, NCH * D]), ("WOT", [128, NCH * D]), ("R2T", [128, 64]),
        ("Vf", [128, NCH * KR]), ("Wf", [128, NCH * KR]),
        ("Vix", [128, NCH * KR]), ("Wiy", [128, NCH * KR]),
        ("Uf", [128, NKRCH * D]), ("Ui", [128, NKRCH * D]),
        ("bfw", [8, 1024]), ("biw", [8, 1024]),
        ("gamma", [128, 8]), ("r1b", [128, 8]),
        ("temp", [8, 1]), ("r2b", [8, 1]), ("eb", [8, 1]), ("alpha", [1, 1]),
        ("ident", [128, 128]), ("mask", [128, 128]), ("nvinv", [128, 32]),
        ("Esel", [8, 512]), ("hsel", [8, 128]),
    ]:
        dw[name] = nc.dram_tensor(name, shape, F32, kind="ExternalInput").ap()

    tap_drams = {}
    res = {}
    STREAM_W = ("WQT", "WKT", "R1T", "R1TI", "UFUI", "WOT")
    dscr = {}
    for nm in STREAM_W:
        dscr[nm] = nc.dram_tensor("scr_" + nm, [128, NCH * D], BF16).ap()
        res["scr_" + nm] = _Res("scr_" + nm)
    for nm in ("VIX2", "WIY2"):
        dscr[nm] = nc.dram_tensor("scr_" + nm, [128, NCH * KR], BF16).ap()
        res["scr_" + nm] = _Res("scr_" + nm)

    def sb(name, shape, dtype):
        h = nc.alloc_sbuf_tensor(name, shape, dtype)
        res[name] = _Res(name)
        return h.ap()

    WATb = sb("WATb", [128, 1024], BF16)
    vfm = sb("vfm", [128, T], BF16)
    wstr = [sb("wsA", [128, NCH * D], BF16), sb("wsB", [128, NCH * D], BF16),
            sb("wsC", [128, NCH * D], BF16)]
    wstr_res = [res["wsA"], res["wsB"], res["wsC"]]
    R2Tb = sb("R2Tb", [128, 64], BF16)
    Vfb = sb("Vfb", [128, NCH * KR], BF16)
    Wfb = sb("Wfb", [128, NCH * KR], BF16)
    Vixb = sb("Vixb", [128, NCH * KR], BF16)
    Wiyb = sb("Wiyb", [128, NCH * KR], BF16)
    bfb = sb("bfb", [8, 1024], BF16)
    bib = sb("bib", [8, 1024], BF16)
    identb = sb("identb", [128, 128], BF16)
    nvinvS = sb("nvinvS", [128, 32], F32)
    EselB = sb("EselB", [8, 512], BF16)
    gammaS = sb("gammaS", [128, 8], F32)
    r1bS = sb("r1bS", [128, 8], F32)
    tempS = sb("tempS", [8, 1], F32)
    r2bS = sb("r2bS", [8, 1], F32)
    ebS = sb("ebS", [8, 1], F32)
    alphaS = sb("alphaS", [1, 1], F32)
    onesB = sb("onesB", [128, 1], BF16)
    negidB = sb("negidB", [128, 1], BF16)
    invdB = sb("invdB", [128, 1], BF16)
    ones1F = sb("ones1F", [1, 128], F32)
    onesColF = sb("onesColF", [128, 1], F32)
    gSqkvB = sb("gSqkvB", [1, 384], BF16)
    bias2 = sb("bias2", [8, 1], F32)
    ethb = sb("ethb", [128, 1], F32)
    alphab = sb("alphab", [128, 1], F32)
    etmpb = sb("etmpb", [8, 1], BF16)
    negmB = sb("negmB", [1, TILE], BF16)
    negmB1 = sb("negmB1", [1, TILE], BF16)
    maskS = sb("maskS", [128, 128], F32)
    hselB = sb("hselB", [8, 128], BF16)
    rkS = sb("rkS", [128, 1], F32)
    st_a = sb("st_a", [1, TILE], F32)
    st_b = sb("st_b", [1, TILE], F32)
    st_c = sb("st_c", [1, TILE], F32)
    sqA = sb("sqA", [128, 1], F32)
    nrmA = sb("nrmA", [128, 1], F32)
    sclA = sb("sclA", [128, 1], F32)
    carryF = sb("carryF", [128, 4], F32)
    carryI = sb("carryI", [128, 4], F32)
    smx1 = sb("smx1", [128, 1], F32)
    smx2 = sb("smx2", [128, 1], F32)
    smxM = sb("smxM", [128, 4], F32)
    sumM = sb("sumM", [128, 4], F32)
    smxQ = sb("smxQ", [128, 4], F32)
    wtsBF = sb("wtsBF", [128, 32], BF16)
    wtsBI = sb("wtsBI", [128, 32], BF16)
    out_res = _Res("OUT")
    res["OUT"] = out_res

    SBUF_TOP = 229344
    arena_base = nc._sbuf_addr_for_side("left")
    arena_base = (arena_base + 31) // 32 * 32

    class _Arena:
        def __init__(self, base):
            self.off = base

        def alloc(self, name, shape, dtype):
            per_part = int(np.prod(shape[1:])) * mybir.dt.size(dtype)
            off = (self.off + 31) // 32 * 32
            assert off + per_part <= SBUF_TOP, (
                f"arena overflow: {name} at {off}+{per_part} > {SBUF_TOP}")
            h = nc.alloc_sbuf_tensor_at(name, shape, dtype, offset=off)
            self.off = off + per_part
            res[name] = _Res(name)
            return h.ap()

    banks = []
    for i in range(8):
        h = nc.alloc_psum_tensor(f"pb{i}", [128, 512], F32)
        banks.append({"f32": h.ap(), "b16": h.ap().bitcast(BF16),
                      "res": _Res(f"bank{i}"), "i": i})
    bank_rr = [0]
    pinned = set()

    def take_bank(pin=False):
        for _ in range(8):
            b = banks[bank_rr[0] % 8]
            bank_rr[0] += 1
            if b["i"] not in pinned:
                if pin:
                    pinned.add(b["i"])
                return b
        raise RuntimeError("all psum banks pinned")

    def unpin(b):
        pinned.discard(b["i"])

    def dma(queue, out_ap, in_ap, reads=(), writes=(), tag=""):
        return P.add(queue, lambda e, o=out_ap, i=in_ap: e.dma_start(out=o, in_=i),
                     reads=reads, writes=writes, tag=tag,
                     semkey="sdma" if queue == "scalar" else None)

    def tap(name, ap_, rd):
        if name not in taps:
            return
        d = nc.dram_tensor("tap_" + name, list(ap_.shape), ap_.dtype,
                           kind="ExternalOutput").ap()
        tap_drams[name] = d
        dma("gpsimd", d, ap_, reads=[rd], tag="tap_" + name)

    def mm_group(ops, bank, reads, tag=""):
        n = len(ops)

        def em(eng, ops=ops, n=n):
            insts = []
            for i, (o, l, r) in enumerate(ops):
                insts.append(eng.matmul(o, l, r, start=(i == 0), stop=(i == n - 1)))
            return insts
        return P.add("tensor", em, reads=reads, writes=[bank["res"]], tag=tag)

    def ve(emit, reads=(), writes=(), tag=""):
        return P.add("vector", emit, reads=reads, writes=writes, tag=tag)

    def ae(emit, reads=(), writes=(), tag=""):
        return P.add("scalar", emit, reads=reads, writes=writes, tag=tag)

    def te(emit, reads=(), writes=(), tag=""):
        return P.add("tensor", emit, reads=reads, writes=writes, tag=tag)

    def ge(emit, reads=(), writes=(), tag=""):
        return P.add("gpsimd", emit, reads=reads, writes=writes, tag=tag)

    def rch(ap_, c, w=SB):
        return ap_.rearrange("p (c t) -> p c t", c=ap_.shape[1] // w)[:, c, :]

    arena = _Arena(arena_base)
    xt0 = arena.alloc("xt0", [128, NCH * TILE], F32)
    xt1 = arena.alloc("xt1", [128, NCH * TILE], F32)
    qfm = arena.alloc("qfm", [128, T], BF16)
    kfm = arena.alloc("kfm", [128, T], BF16)
    xb16 = arena.alloc("xb16", [128, NCH * TILE], BF16)
    xb16b = arena.alloc("xb16b", [128, NCH * TILE], BF16)
    xb16c = arena.alloc("xb16c", [128, NCH * TILE], BF16)
    xsq16 = arena.alloc("xsq16", [128, NCH * TILE], BF16)
    ssbA = arena.alloc("ssbA", [128, TILE], F32)
    ssbA1 = arena.alloc("ssbA1", [128, TILE], F32)
    qtm = arena.alloc("qtm", [128, 128], BF16)
    ktm = arena.alloc("ktm", [128, 128], BF16)
    qtm2 = arena.alloc("qtm2", [128, 128], BF16)
    ktm2 = arena.alloc("ktm2", [128, 128], BF16)
    simsb = arena.alloc("simsb", [128, 128], F32)
    Ab = arena.alloc("Ab", [128, 128], BF16)
    identf = arena.alloc("identf", [128, 128], F32)
    nsq = arena.alloc("nsq", [128, TILE], BF16)
    sqQa = arena.alloc("sqQa", [128, 1], F32)
    sqKa = arena.alloc("sqKa", [128, 1], F32)
    nred = arena.alloc("nred", [128, 1], F32)
    st_a2 = arena.alloc("st_a2", [1, TILE], F32)
    st_b2 = arena.alloc("st_b2", [1, TILE], F32)
    st_c2 = arena.alloc("st_c2", [1, TILE], F32)
    wqkvTb = arena.alloc("wqkvTb", [128, NCH * 384], BF16)
    woutTb = arena.alloc("woutTb", [128, 1024], BF16)


    def wload(dst_ap, dname, rkey):
        dma("gpsimd", dst_ap, dw[dname], writes=[res[rkey]], tag="ld_" + dname)

    dma("gpsimd", wstr[0], dw["WIN"], writes=[wstr_res[0]], tag="prep_ld_WIN")
    dma("gpsimd", qfm, dw["Vix"], writes=[res["qfm"]], tag="prep_ld_Vix")
    dma("gpsimd", wstr[1], dw["R1T"], writes=[wstr_res[1]], tag="prep_ld_R1T")
    wload(wqkvTb, "wqkvT", "wqkvTb")
    wload(identb, "ident", "identb")
    wload(hselB, "hsel", "hselB")
    wload(woutTb, "woutT", "woutTb")
    dma("gpsimd", maskS, dw["mask"], writes=[res["maskS"]])
    dma("gpsimd", identf, dw["ident"], writes=[res["identf"]])
    dma("gpsimd", gammaS, dw["gamma"], writes=[res["gammaS"]])
    dma("gpsimd", tempS, dw["temp"], writes=[res["tempS"]])

    for ap_, val, rk in ((sqQa, 0.0, "sqQa"), (sqKa, 0.0, "sqKa"),
                         (onesB, 1.0, "onesB"), (negidB, -1.0 / 1024.0, "negidB"),
                         (invdB, 1.0 / 1024.0, "invdB"), (ones1F, 1.0, "ones1F"),
                         (onesColF, 1.0, "onesColF"), (carryF, 0.0, "carryF"),
                         (carryI, 0.0, "carryI")):
        ve(lambda e, ap_=ap_, val=val: e.memset(ap_, val),
           writes=[res[rk]], tag="memset_" + rk)

    xts = [xt0, xt1]
    xts_res = [res["xt0"], res["xt1"]]
    xb16s = [xb16, xb16b, xb16c]
    xb16s_res = [res["xb16"], res["xb16b"], res["xb16c"]]
    ssbAs = [ssbA, ssbA1]
    ssbAs_res = [res["ssbA"], res["ssbA1"]]
    negmBs = [negmB, negmB1]
    negmBs_res = [res["negmB"], res["negmB1"]]
    qkv_dst = [(qfm, res["qfm"]), (kfm, res["kfm"]), (vfm, res["vfm"])]

    A_banks = {}

    xbh_res = [[_Res(f"xbh{i}_{h}") for h in range(2)] for i in range(3)]
    xsqh_res = [_Res(f"xsqh{h}") for h in range(2)]
    HL = NCH // 2 * TILE

    def emit_A_head_dma(t):
        xt = xts[t % 2]
        xt_res = xts_res[t % 2]
        xb = xb16s[t % 3]
        xb_res = xb16s_res[t % 3]
        dma("sync", xt.rearrange("p (c t) -> p c t", c=NCH),
            dx.rearrange("p (c t) -> p c t", c=NCH)[:, :, t * TILE:(t + 1) * TILE],
            writes=[xt_res], tag=f"A_x{t}")
        for h in range(2):
            ae(lambda e, xt=xt, xb=xb, h=h:
               e.copy(xb[:, h * HL:(h + 1) * HL], xt[:, h * HL:(h + 1) * HL]),
               reads=[xt_res], writes=[xbh_res[t % 3][h], xb_res],
               tag=f"A_cast{t}_{h}")

    def emit_A_head_mm(t):
        xb = xb16s[t % 3]
        bnm = take_bank(pin=True)
        for h in range(2):
            ae(lambda e, xb=xb, h=h:
               e.activation(xsq16[:, h * HL:(h + 1) * HL],
                            xb[:, h * HL:(h + 1) * HL], AF.Square),
               reads=[xbh_res[t % 3][h]], writes=[xsqh_res[h], res["xsq16"]],
               tag=f"A_sq{t}_{h}")
        for h in range(2):
            te(lambda e, b=bnm, h=h, xb=xb:
               [e.matmul(b["f32"][0:1, 0:TILE], negidB, rch(xb, c, TILE),
                         start=(c == 0), stop=(c == NCH - 1))
                for c in range(h * NCH // 2, (h + 1) * NCH // 2)],
               reads=[xbh_res[t % 3][h], res["negidB"]], writes=[bnm["res"]],
               tag=f"A_negm{t}_{h}")
        for h in range(2):
            te(lambda e, b=bnm, h=h:
               [e.matmul(b["f32"][32:33, 0:TILE], invdB, rch(xsq16, c, TILE),
                         start=(c == 0), stop=(c == NCH - 1))
                for c in range(h * NCH // 2, (h + 1) * NCH // 2)],
               reads=[xsqh_res[h], res["invdB"]], writes=[bnm["res"]],
               tag=f"A_msq{t}_{h}")
        A_banks[t] = bnm

    def emit_A_head(t):
        emit_A_head_dma(t)
        emit_A_head_mm(t)


    emit_A_head(0)
    emit_A_head(1)

    def prep_vw(nm, src_ap, src_res, stage_ap, stage_res):
        for a in range(NCH):
            bk_ = take_bank()
            mm_group([(bk_["f32"][:, 0:KR],
                       wstr[0][:, c * D + a * 128:c * D + (a + 1) * 128],
                       src_ap.rearrange("p (c t) -> p c t", c=NCH)[:, c, :])
                      for c in range(NCH)], bk_,
                     reads=[wstr_res[0], src_res], tag=f"prep_{nm}_{a}")
            ae(lambda e, b=bk_, a=a, stage_ap=stage_ap:
               e.copy(stage_ap[:, a * KR:(a + 1) * KR], b["f32"][:, 0:KR]),
               reads=[bk_["res"]], writes=[stage_res], tag=f"prep_{nm}ev_{a}")

    dma("gpsimd", xb16c, dw["Wiy"], writes=[res["xb16c"]], tag="prep_ld_Wiy")
    prep_vw("VIX2", qfm, res["qfm"], kfm, res["kfm"])
    xt1b = xt1.bitcast(BF16)
    for a in range(NCH):
        for hf in range(2):
            bk_ = take_bank()
            mm_group([(bk_["f32"][:, 0:512],
                       wstr[0][:, c * D + a * 128:c * D + (a + 1) * 128],
                       wstr[1][:, c * D + hf * 512:c * D + (hf + 1) * 512])
                      for c in range(NCH)], bk_,
                     reads=[wstr_res[0], wstr_res[1]], tag=f"prep_R1TI_{a}_{hf}")
            ae(lambda e, b=bk_, a=a, hf=hf:
               e.copy(xt1b[:, a * D + hf * 512:a * D + (hf + 1) * 512],
                      b["f32"][:, 0:512]),
               reads=[bk_["res"]], writes=[res["xt1"]], tag=f"prep_R1TIev_{a}_{hf}")
    prep_vw("WIY2", xb16c, res["xb16c"], qfm, res["qfm"])
    dma("gpsimd", dscr["R1T"], wstr[1], reads=[wstr_res[1]],
        writes=[res["scr_R1T"]], tag="precast_st_R1T")
    dma("gpsimd", dscr["R1TI"], xt1b, reads=[res["xt1"]],
        writes=[res["scr_R1TI"]], tag="prep_st_R1TI")
    dma("gpsimd", dscr["VIX2"], kfm, reads=[res["kfm"]], writes=[res["scr_VIX2"]],
        tag="prep_st_VIX2")
    dma("gpsimd", dscr["WIY2"], qfm, reads=[res["qfm"]], writes=[res["scr_WIY2"]],
        tag="prep_st_WIY2")

    for c in range(NCH):
        ve(lambda e, c=c: e.tensor_scalar_mul(wqkvTb[:, c * 384:(c + 1) * 384],
                                              wqkvTb[:, c * 384:(c + 1) * 384],
                                              gammaS[:, c:c + 1]),
           reads=[res["gammaS"]], writes=[res["wqkvTb"]], tag=f"gt_scale{c}")

    bk = take_bank()
    mm_group([(bk["f32"][0:1, 0:384], onesB, wqkvTb[:, c * 384:(c + 1) * 384])
              for c in range(NCH)], bk,
             reads=[res["wqkvTb"], res["onesB"]], tag="gSqkv_mm")
    ae(lambda e, b=bk: e.copy(gSqkvB, b["f32"][0:1, 0:384]),
       reads=[bk["res"]], writes=[res["gSqkvB"]], tag="gSqkv_cp")

    ae(lambda e: e.activation(tempS, tempS, AF.Exp), writes=[res["tempS"]], tag="exp_temp")
    ae(lambda e: e.copy(etmpb, tempS), reads=[res["tempS"]], writes=[res["etmpb"]])
    bk = take_bank()
    mm_group([(bk["f32"][0:128, 0:1], hselB, etmpb)], bk,
             reads=[res["hselB"], res["etmpb"]], tag="ethb_mm")
    ve(lambda e, b=bk: e.tensor_copy(ethb, b["f32"][0:128, 0:1]),
       reads=[bk["res"]], writes=[res["ethb"]])


    def emit_A_tail(t):
        bnm = A_banks.pop(t)
        bms = bnm
        sA, sB, sC = ((st_a, st_b, st_c) if t % 2 == 0 else (st_a2, st_b2, st_c2))
        rA, rB, rC = (("st_a", "st_b", "st_c") if t % 2 == 0
                      else ("st_a2", "st_b2", "st_c2"))
        ae(lambda e, b=bnm, sA=sA: e.activation(sA, b["f32"][0:1, 0:TILE], AF.Square),
           reads=[bnm["res"]], writes=[res[rA]], tag=f"A_m2{t}")
        ve(lambda e, b=bms, sA=sA, sB=sB: e.tensor_sub(sB, b["f32"][32:33, 0:TILE], sA),
           reads=[bms["res"], res[rA]], writes=[res[rB]], tag=f"A_var{t}")
        ve(lambda e, sA=sA, sB=sB: e.tensor_scalar(sA, sB, 1.0 + EPS_RMS,
                                                   EPS_LN * EPS_RMS,
                                                   ALU.mult, ALU.add),
           reads=[res[rB]], writes=[res[rA]])
        ae(lambda e, sA=sA: e.activation(sA, sA, AF.Ln), writes=[res[rA]])
        ae(lambda e, sA=sA, sC=sC: e.activation(sC, sA, AF.Exp, scale=-0.5),
           reads=[res[rA]], writes=[res[rC]], tag=f"A_s{t}")
        ae(lambda e, b=bnm, t=t: e.copy(negmBs[t % 2], b["f32"][0:1, 0:TILE]),
           reads=[bnm["res"]], writes=[negmBs_res[t % 2]], tag=f"A_negmB{t}")
        bsc = take_bank()
        mm_group([(bsc["f32"][:, 0:TILE], ones1F, sC)], bsc,
                 reads=[res[rC], res["ones1F"]], tag=f"A_sbc{t}")
        ve(lambda e, b=bsc, t=t: e.tensor_copy(ssbAs[t % 2], b["f32"][:, 0:TILE]),
           reads=[bsc["res"]], writes=[ssbAs_res[t % 2]], tag=f"A_ssb{t}")
        unpin(bnm)

    bG = take_bank(pin=True)
    qtms = [qtm, qtm2]
    ktms = [ktm, ktm2]
    qtms_res = [res["qtm"], res["qtm2"]]
    ktms_res = [res["ktm"], res["ktm2"]]

    def emit_A_gram(t):
        for b_ in range(TILE // 128):
            i = t * (TILE // 128) + b_
            bt1 = take_bank()
            te(lambda e, b=bt1, i=i: e.transpose(b["b16"][:, 0:128],
                                                 qfm[:, i * 128:(i + 1) * 128], identb),
               reads=[res["qfm"], res["identb"]], writes=[bt1["res"]], tag=f"G_tq{i}")
            ve(lambda e, b=bt1, i=i: e.tensor_copy(qtms[i % 2], b["b16"][:, 0:128]),
               reads=[bt1["res"]], writes=[qtms_res[i % 2]])
            bt2 = take_bank()
            te(lambda e, b=bt2, i=i: e.transpose(b["b16"][:, 0:128],
                                                 kfm[:, i * 128:(i + 1) * 128], identb),
               reads=[res["kfm"], res["identb"]], writes=[bt2["res"]], tag=f"G_tk{i}")
            ve(lambda e, b=bt2, i=i: e.tensor_copy(ktms[i % 2], b["b16"][:, 0:128]),
               reads=[bt2["res"]], writes=[ktms_res[i % 2]])
            te(lambda e, b=bG, i=i: e.matmul(b["f32"][:, 0:128], qtms[i % 2], ktms[i % 2],
                                             start=(i == 0), stop=(i == 31)),
               reads=[qtms_res[i % 2], ktms_res[i % 2]], writes=[bG["res"]],
               tag=f"G_mm{i}")

    def emit_A_qkv(t):
        xb = xb16s[t % 3]
        xb_res = xb16s_res[t % 3]
        for oc in range(3):
            bq = take_bank()
            ops = [(bq["f32"][:, 0:TILE],
                    wqkvTb[:, c * 384 + oc * 128: c * 384 + (oc + 1) * 128],
                    rch(xb, c, TILE)) for c in range(NCH)]
            ops.append((bq["f32"][:, 0:TILE], gSqkvB[:, oc * 128:(oc + 1) * 128],
                        negmBs[t % 2]))
            mm_group(ops, bq, reads=[res["wqkvTb"], xb_res, res["gSqkvB"],
                                     negmBs_res[t % 2]], tag=f"A_qkv{t}_{oc}")
            dst, dres = qkv_dst[oc]
            ve(lambda e, b=bq, dst=dst, t=t:
               e.tensor_mul(dst[:, t * TILE:(t + 1) * TILE], b["f32"][:, 0:TILE],
                            ssbAs[t % 2]),
               reads=[bq["res"], ssbAs_res[t % 2]], writes=[dres], tag=f"A_ev{t}_{oc}")
        for (fm2, fres2, acc, accr) in ((qfm, res["qfm"], sqQa, "sqQa"),
                                        (kfm, res["kfm"], sqKa, "sqKa")):
            ae(lambda e, fm2=fm2, t=t: e.activation(nsq, fm2[:, t * TILE:(t + 1) * TILE],
                                                    AF.Square),
               reads=[fres2], writes=[res["nsq"]], tag=f"A_nsq{t}")
            ve(lambda e: e.reduce_sum(nred, nsq, AX.X),
               reads=[res["nsq"]], writes=[res["nred"]], tag=f"A_nred{t}")
            ve(lambda e, acc=acc: e.tensor_add(acc, acc, nred),
               reads=[res["nred"]], writes=[res[accr]], tag=f"A_nacc{t}")

    emit_A_tail(0)
    for t in range(NT):
        if t + 2 < NT:
            emit_A_head(t + 2)
        if t + 1 < NT:
            emit_A_tail(t + 1)
        emit_A_qkv(t)
        emit_A_gram(t)

    tap("qfm", qfm, res["qfm"])
    tap("kfm", kfm, res["kfm"])
    tap("vfm", vfm, res["vfm"])

    for (acc, accr, which) in ((sqQa, "sqQa", "q"), (sqKa, "sqKa", "k")):
        ae(lambda e, acc=acc: e.activation(nrmA, acc, AF.Sqrt),
           reads=[res[accr]], writes=[res["nrmA"]])
        ve(lambda e: e.tensor_scalar_max(nrmA, nrmA, 1e-12), writes=[res["nrmA"]])
        ve(lambda e: e.reciprocal(nrmA, nrmA), writes=[res["nrmA"]])
        if which == "q":
            ve(lambda e: e.tensor_mul(sclA, nrmA, ethb),
               reads=[res["nrmA"], res["ethb"]], writes=[res["sclA"]], tag="N_qscale")
        else:
            ve(lambda e: e.tensor_copy(rkS, nrmA),
               reads=[res["nrmA"]], writes=[res["rkS"]], tag="N_kscale")

    brt = take_bank()
    te(lambda e, b=brt: e.transpose(b["f32"][0:1, 0:128], rkS, identf),
       reads=[res["rkS"], res["identf"]], writes=[brt["res"]], tag="S_rkT")
    ae(lambda e, b=brt: e.copy(st_a[:, 0:128], b["f32"][0:1, 0:128]),
       reads=[brt["res"]], writes=[res["st_a"]], tag="S_rkrow")
    bcb = take_bank()
    mm_group([(bcb["f32"][:, 0:128], ones1F, st_a[:, 0:128])], bcb,
             reads=[res["st_a"], res["ones1F"]], tag="S_colmm")
    ae(lambda e, b=bcb: e.copy(ssbA[:, 0:128], b["f32"][:, 0:128]),
       reads=[bcb["res"]], writes=[res["ssbA"]], tag="S_colcp")
    ve(lambda e, b=bG: e.scalar_tensor_tensor(simsb, b["f32"][:, 0:128], sclA,
                                              ssbA[:, 0:128],
                                              ALU.mult, ALU.mult),
       reads=[bG["res"], res["sclA"], res["ssbA"]], writes=[res["simsb"]],
       tag="S_scale")
    unpin(bG)
    ve(lambda e: e.tensor_add(simsb, simsb, maskS),
       reads=[res["maskS"]], writes=[res["simsb"]], tag="S_mask")
    ve(lambda e: e.reduce_max(smx1, simsb, AX.X, negate=True),
       reads=[res["simsb"]], writes=[res["smx1"]])
    ae(lambda e: e.activation(simsb, simsb, AF.Exp, bias=smx1),
       reads=[res["smx1"]], writes=[res["simsb"]], tag="S_exp")
    ve(lambda e: e.reduce_sum(smx2, simsb, AX.X),
       reads=[res["simsb"]], writes=[res["smx2"]])
    ve(lambda e: e.reciprocal(smx2, smx2), writes=[res["smx2"]])
    ve(lambda e: e.tensor_scalar_mul(Ab, simsb, smx2),
       reads=[res["simsb"], res["smx2"]], writes=[res["Ab"]], tag="S_A")
    tap("Ab", Ab, res["Ab"])
    for half in range(2):
        bw = take_bank()
        mm_group([(bw["f32"][:, 0:512], Ab, woutTb[:, half * 512:(half + 1) * 512])],
                 bw, reads=[res["Ab"], res["woutTb"]], tag=f"WAT{half}")
        ve(lambda e, b=bw, half=half:
           e.tensor_copy(WATb[:, half * 512:(half + 1) * 512], b["f32"][:, 0:512]),
           reads=[bw["res"]], writes=[res["WATb"]], tag=f"WATev{half}")
    tap("WATb", WATb, res["WATb"])

    for nm, dst, rkey in [("R2T", R2Tb, "R2Tb"),
                          ("Vf", Vfb, "Vfb"), ("Wf", Wfb, "Wfb"),
                          ("bfw", bfb, "bfb"), ("biw", bib, "bib"),
                          ("Esel", EselB, "EselB")]:
        wload(dst, nm, rkey)
    dma("gpsimd", Vixb, dscr["VIX2"], reads=[res["scr_VIX2"]],
        writes=[res["Vixb"]], tag="ld_VIX2")
    dma("gpsimd", Wiyb, dscr["WIY2"], reads=[res["scr_WIY2"]],
        writes=[res["Wiyb"]], tag="ld_WIY2")
    dma("gpsimd", r1bS, dw["r1b"], writes=[res["r1bS"]])
    dma("gpsimd", r2bS, dw["r2b"], writes=[res["r2bS"]])
    dma("gpsimd", ebS, dw["eb"], writes=[res["ebS"]])
    dma("gpsimd", alphaS, dw["alpha"], writes=[res["alphaS"]])
    dma("gpsimd", nvinvS, dw["nvinv"], writes=[res["nvinvS"]])

    ve(lambda e: e.tensor_add(bias2, r2bS, ebS),
       reads=[res["r2bS"], res["ebS"]], writes=[res["bias2"]], tag="bias2")
    bk = take_bank()
    mm_group([(bk["f32"][0:128, 0:1], ones1F, alphaS)], bk,
             reads=[res["ones1F"], res["alphaS"]], tag="alpha_mm")
    ve(lambda e, b=bk: e.tensor_copy(alphab, b["f32"][0:128, 0:1]),
       reads=[bk["res"]], writes=[res["alphab"]])
    ve(lambda e: e.tensor_scalar_mul(bib, bib, alphab[0:8, 0:1]),
       reads=[res["alphab"]], writes=[res["bib"]], tag="abi")

    for nm in ("WQT", "WKT", "WOT"):
        buf = wstr[0] if nm != "WKT" else wstr[1]
        bres = wstr_res[0] if nm != "WKT" else wstr_res[1]
        dma("gpsimd", buf, dw[nm], writes=[bres], tag=f"precast_ld_{nm}")
        dma("gpsimd", dscr[nm], buf, reads=[bres], writes=[res["scr_" + nm]],
            tag=f"precast_st_{nm}")
    buf, bres = wstr[1], wstr_res[1]
    dma("gpsimd", buf[:, 0:NKRCH * D], dw["Uf"], writes=[bres], tag="precast_ld_Uf")
    dma("gpsimd", buf[:, NKRCH * D:2 * NKRCH * D], dw["Ui"], writes=[bres],
        tag="precast_ld_Ui")
    ve(lambda e, buf=buf: e.tensor_scalar_mul(
        buf[:, NKRCH * D:2 * NKRCH * D], buf[:, NKRCH * D:2 * NKRCH * D], alphab),
       reads=[res["alphab"]], writes=[bres], tag="precast_aUi")
    dma("gpsimd", dscr["UFUI"], buf, reads=[bres], writes=[res["scr_UFUI"]],
        tag="precast_st_UFUI")

    def _steps_of(*names):
        out, seen = [], set()
        for nm_ in names:
            r_ = res[nm_]
            for st_ in r_.writers + r_.readers:
                if id(st_) not in seen:
                    seen.add(id(st_))
                    out.append(st_)
        return out

    fence0 = _steps_of("xt0")
    fence1 = _steps_of("xt1")

    arena = _Arena(arena_base)
    ybufs = [arena.alloc("ybuf0", [128, NCH * SB], F32),
             arena.alloc("ybuf1", [128, NCH * SB], F32)]
    ybufs_res = [res["ybuf0"], res["ybuf1"]]
    yb16 = arena.alloc("yb16", [128, NCH * SB], BF16)
    lntmp = arena.alloc("lntmp", [128, NCH * SB], BF16)
    lntmpb = lntmp
    Qfm = arena.alloc("Qfm", [128, NCH * SB], BF16)
    Kfm = arena.alloc("Kfm", [128, NCH * SB], BF16)
    hsb = [arena.alloc("hsb0", [128, SB], BF16), arena.alloc("hsb1", [128, SB], BF16)]
    hsb_res = [res["hsb0"], res["hsb1"]]
    lgsbF = arena.alloc("lgsbF", [8, SB], BF16)
    lgsbI = arena.alloc("lgsbI", [8, SB], BF16)
    wfmF = arena.alloc("wfmF", [8, SB], BF16)
    wfmI = arena.alloc("wfmI", [8, SB], BF16)
    cumS = arena.alloc("cumS", [128, NKRCH * SB], F32)
    P2F = arena.alloc("P2F", [128, NKRCH * SB], BF16)
    P2I = arena.alloc("P2I", [128, NKRCH * SB], BF16)
    ff16 = arena.alloc("ff16", [128, NCH * SB], BF16)

    def wuse(s, nm):
        idx = list(STREAM_W).index(nm)
        return wstr[idx % 3], wstr_res[idx % 3]

    def wstream(s, nm):
        buf, bres = wuse(s, nm)
        dma("gpsimd", buf, dscr[nm], reads=[res["scr_" + nm]], writes=[bres],
            tag=f"B_wld{s}_{nm}")

    def emit_load(s, extra=()):
        yb = ybufs[s % 2]
        P.add("sync",
              lambda e, yb=yb, s=s: e.dma_start(
                  out=yb.rearrange("p (c t) -> p c t", c=NCH),
                  in_=dx.rearrange("p (c t) -> p c t", c=NCH)[:, :, s * SB:(s + 1) * SB]),
              writes=[ybufs_res[s % 2]], extra_deps=extra, tag=f"B_x{s}")

    def emit_proj(s):
        yb = ybufs[s % 2]
        yb_res = ybufs_res[s % 2]
        for oc in range(NCH):
            bp = take_bank()
            mm_group([(bp["f32"][:, 0:SB], WATb[:, oc * 128:(oc + 1) * 128],
                       vfm[:, s * SB:(s + 1) * SB])], bp,
                     reads=[res["WATb"], res["vfm"]], tag=f"B_proj{s}_{oc}")
            ve(lambda e, b=bp, oc=oc, yb=yb:
               e.tensor_add(yb[:, oc * SB:(oc + 1) * SB],
                            yb[:, oc * SB:(oc + 1) * SB], b["f32"][:, 0:SB]),
               reads=[bp["res"]], writes=[yb_res], tag=f"B_res{s}_{oc}")

    B_banks = {}

    def emit_stats_head(s):
        yb = ybufs[s % 2]
        yb_res = ybufs_res[s % 2]
        ae(lambda e, yb=yb: e.copy(yb16, yb),
           reads=[yb_res], writes=[res["yb16"]], tag=f"B_cast{s}")
        bnm = take_bank(pin=True)
        mm_group([(bnm["f32"][0:1, 0:SB], negidB, rch(yb16, c)) for c in range(NCH)],
                 bnm, reads=[res["yb16"], res["negidB"]], tag=f"B_negm{s}")
        ae(lambda e: e.activation(lntmp, yb16, AF.Square),
           reads=[res["yb16"]], writes=[res["lntmp"]], tag=f"B_sq{s}")
        bms = take_bank(pin=True)
        mm_group([(bms["f32"][0:1, 0:SB], invdB, rch(lntmp, c)) for c in range(NCH)],
                 bms, reads=[res["lntmp"], res["invdB"]], tag=f"B_msq{s}")
        B_banks[s] = (bnm, bms)

    def emit_stats_tail(s):
        yb = ybufs[s % 2]
        yb_res = ybufs_res[s % 2]
        bnm, bms = B_banks.pop(s)
        ae(lambda e, b=bnm: e.activation(st_a, b["f32"][0:1, 0:SB], AF.Square),
           reads=[bnm["res"]], writes=[res["st_a"]], tag=f"B_m2{s}")
        ve(lambda e, b=bms: e.tensor_sub(st_b, b["f32"][0:1, 0:SB], st_a),
           reads=[bms["res"], res["st_a"]], writes=[res["st_b"]])
        ve(lambda e: e.tensor_scalar_add(st_a, st_b, EPS_LN),
           reads=[res["st_b"]], writes=[res["st_a"]])
        ae(lambda e: e.activation(st_a, st_a, AF.Ln), writes=[res["st_a"]])
        ae(lambda e: e.activation(st_c, st_a, AF.Exp, scale=-0.5),
           reads=[res["st_a"]], writes=[res["st_c"]], tag=f"B_s{s}")
        ae(lambda e, b=bnm: e.copy(st_b, b["f32"][0:1, 0:SB]),
           reads=[bnm["res"]], writes=[res["st_b"]], tag=f"B_negm_sb{s}")
        unpin(bnm)
        unpin(bms)

    def emit_stats_bcast(s):
        yb = ybufs[s % 2]
        yb_res = ybufs_res[s % 2]
        bnb = take_bank()
        mm_group([(bnb["f32"][:, 0:SB], ones1F, st_b)], bnb,
                 reads=[res["st_b"], res["ones1F"]], tag=f"B_nmb{s}")
        bsb = take_bank()
        mm_group([(bsb["f32"][:, 0:SB], ones1F, st_c)], bsb,
                 reads=[res["st_c"], res["ones1F"]], tag=f"B_sbc{s}")
        ve(lambda e, b=bnb, yb=yb:
           e.tensor_add(lntmp.rearrange("p (c t) -> p c t", c=NCH),
                        yb.rearrange("p (c t) -> p c t", c=NCH),
                        b["f32"][:, 0:SB].unsqueeze(1).broadcast_to([128, NCH, SB]),
                        ),
           reads=[bnb["res"], yb_res], writes=[res["lntmp"]], tag=f"B_ln1_{s}")
        ve(lambda e, b=bsb:
           e.tensor_mul(yb16.rearrange("p (c t) -> p c t", c=NCH),
                        lntmp.rearrange("p (c t) -> p c t", c=NCH),
                        b["f32"][:, 0:SB].unsqueeze(1).broadcast_to([128, NCH, SB])),
           reads=[bsb["res"], res["lntmp"]], writes=[res["yb16"]], tag=f"B_ln2_{s}")

    def emit_qk(s):
        for (nm, dst, dkey) in (("WQT", Qfm, "Qfm"), ("WKT", Kfm, "Kfm")):
            wt, wres = wuse(s, nm)
            for oc in range(NCH):
                bq = take_bank()
                mm_group([(bq["f32"][:, 0:SB],
                           wt[:, c * D + oc * 128: c * D + (oc + 1) * 128],
                           rch(yb16, c)) for c in range(NCH)], bq,
                         reads=[wres, res["yb16"]], tag=f"B_{nm}{s}_{oc}")
                ae(lambda e, b=bq, dst=dst, oc=oc:
                   e.copy(dst[:, oc * SB:(oc + 1) * SB], b["f32"][:, 0:SB]),
                   reads=[bq["res"]], writes=[res[dkey]], tag=f"B_{nm}ev{s}_{oc}")

    def emit_h(s, bs, wname, lgsb, lgkey):
        wt, wres = wuse(s, wname)
        blg = take_bank(pin=True)

        def lg_mm(jc, hb, hbr):
            te(lambda e, b=blg, hb=hb, jc=jc:
               e.matmul(b["f32"][0:8, 0:SB], R2Tb[:, jc * 8:(jc + 1) * 8], hb,
                        start=(jc == 0), stop=(jc == NCH - 1)),
               reads=[hbr, res["R2Tb"]], writes=[blg["res"]], tag=f"B{bs}_lg{s}_{jc}")

        pend = None
        for jc in range(NCH):
            bh = take_bank()
            mm_group([(bh["f32"][:, 0:SB], wt[:, c * D + jc * 128: c * D + (jc + 1) * 128],
                       rch(Qfm, c)) for c in range(NCH)], bh,
                     reads=[wres, res["Qfm"]], tag=f"B{bs}_h{s}_{jc}")
            hb = hsb[jc % 2]
            hbr = hsb_res[jc % 2]
            ae(lambda e, b=bh, hb=hb, jc=jc:
               e.activation(hb, b["f32"][:, 0:SB], AF.Gelu, bias=r1bS[:, jc:jc + 1]),
               reads=[bh["res"], res["r1bS"]], writes=[hbr], tag=f"B{bs}_hev{s}_{jc}")
            if pend is not None:
                lg_mm(*pend)
            pend = (jc, hb, hbr)
        lg_mm(*pend)
        ve(lambda e, b=blg, lgsb=lgsb: e.tensor_scalar_add(lgsb, b["f32"][0:8, 0:SB], bias2),
           reads=[blg["res"], res["bias2"]], writes=[res[lgkey]], tag=f"B{bs}_lgev{s}")
        unpin(blg)

    def emit_smx_pre(s, bs, lgsb, lgkey, wts, wtsk):
        btA = take_bank(pin=True)
        for bi in range(SB // 128):
            te(lambda e, b=btA, bi=bi, lgsb=lgsb:
               e.transpose(b["b16"][:, bi * 8:(bi + 1) * 8],
                           lgsb[:, bi * 128:(bi + 1) * 128], identb[0:8, 0:8]),
               reads=[res[lgkey], res["identb"]], writes=[btA["res"]],
               tag=f"B{bs}_lt{s}_{bi}")
        ve(lambda e, b=btA:
           e.reduce_max(smxM, b["b16"][:, 0:32].rearrange("p (c e) -> p c e", e=8),
                        AX.X, negate=True),
           reads=[btA["res"]], writes=[res["smxM"]], tag=f"B{bs}_smax{s}")
        ve(lambda e, b=btA, wts=wts:
           e.tensor_add(wts.rearrange("p (c e) -> p c e", e=8),
                        b["b16"][:, 0:32].rearrange("p (c e) -> p c e", e=8),
                        smxM.unsqueeze(2).broadcast_to([128, 4, 8])),
           reads=[btA["res"], res["smxM"]], writes=[res[wtsk]],
           tag=f"B{bs}_ssub{s}")
        unpin(btA)

    def emit_smx_mid(s, bs, wts, wtsk):
        ae(lambda e, wts=wts: e.activation(wts, wts, AF.Exp, bias=0.0),
           writes=[res[wtsk]], tag=f"B{bs}_wexp{s}")
        for c in range(SB // 128):
            ve(lambda e, c=c, wts=wts: e.reduce_sum(sumM[:, c:c + 1],
                                                    wts[:, c * 8:(c + 1) * 8], AX.X),
               reads=[res[wtsk]], writes=[res["sumM"]], tag=f"B{bs}_ssum{s}_{c}")
        ve(lambda e: e.reciprocal(sumM, sumM), writes=[res["sumM"]],
           tag=f"B{bs}_srec{s}")
        ve(lambda e, s=s: e.tensor_mul(smxQ, nvinvS[:, s * 4:(s + 1) * 4], sumM),
           reads=[res["nvinvS"], res["sumM"]], writes=[res["smxQ"]],
           tag=f"B{bs}_snv{s}")
        ve(lambda e, wts=wts: e.tensor_mul(wts.rearrange("p (c e) -> p c e", e=8),
                                           wts.rearrange("p (c e) -> p c e", e=8),
                                           smxQ.unsqueeze(2).broadcast_to([128, 4, 8])),
           reads=[res["smxQ"]], writes=[res[wtsk]], tag=f"B{bs}_wts{s}")

    def emit_smx_back(s, bs, wfm, wfmk, wts, wtsk):
        btB = take_bank()
        for bi in range(SB // 128):
            te(lambda e, b=btB, bi=bi, wts=wts:
               e.transpose(b["b16"][0:8, bi * 128:(bi + 1) * 128],
                           wts[:, bi * 8:(bi + 1) * 8], identb),
               reads=[res[wtsk], res["identb"]], writes=[btB["res"]],
               tag=f"B{bs}_wt{s}_{bi}")
        ae(lambda e, b=btB, wfm=wfm: e.copy(wfm, b["b16"][0:8, 0:SB]),
           reads=[btB["res"]], writes=[res[wfmk]], tag=f"B{bs}_wfm{s}")

    def emit_ywxv(s, bs, Vp, Vpk, Wp, Wpk, carry, carryk, hook=None):
        P2, P2k = (P2F, "P2F") if bs == "F" else (P2I, "P2I")

        def wrep_p2(kc):
            bwr = take_bank()
            mm_group([(bwr["f32"][:, 0:SB], EselB[:, kc * 128:(kc + 1) * 128],
                       wfmF if bs == "F" else wfmI)],
                     bwr, reads=[res["EselB"], res["wfmF" if bs == "F" else "wfmI"]],
                     tag=f"B{bs}_wrep{s}_{kc}")
            ve(lambda e, b=bwr, kc=kc:
               e.tensor_mul(rch(P2, kc), rch(cumS, kc), b["f32"][:, 0:SB]),
               reads=[res["cumS"], bwr["res"]], writes=[res[P2k]],
               tag=f"B{bs}_p2{s}_{kc}")

        pend_kc = None
        for kc in range(NKRCH):
            byw = take_bank()
            mm_group([(byw["f32"][:, 0:SB],
                       Wp[:, c * KR + kc * 128: c * KR + (kc + 1) * 128],
                       rch(Kfm, c)) for c in range(NCH)], byw,
                     reads=[res[Wpk], res["Kfm"]], tag=f"B{bs}_yw{s}_{kc}")
            ve(lambda e, b=byw, kc=kc, carry=carry, s=s:
               e.tensor_tensor_scan(rch(cumS, kc), onesColF.broadcast_to([128, SB]),
                                    b["f32"][:, 0:SB],
                                    carry[:, kc:kc + 1] if s > 0 else 0.0,
                                    ALU.mult, ALU.add),
               reads=[byw["res"], res["onesColF"]] + ([res[carryk]] if s > 0 else []),
               writes=[res["cumS"]], tag=f"B{bs}_scan{s}_{kc}")
            ve(lambda e, kc=kc, carry=carry:
               e.tensor_copy(carry[:, kc:kc + 1], rch(cumS, kc)[:, SB - 1:SB]),
               reads=[res["cumS"]], writes=[res[carryk]], tag=f"B{bs}_carry{s}_{kc}")
            bxv = take_bank()
            mm_group([(bxv["f32"][:, 0:SB],
                       Vp[:, c * KR + kc * 128: c * KR + (kc + 1) * 128],
                       rch(Qfm, c)) for c in range(NCH)], bxv,
                     reads=[res[Vpk], res["Qfm"]], tag=f"B{bs}_xv{s}_{kc}")
            if kc == 0 and hook is not None:
                hook()
            ve(lambda e, b=bxv, kc=kc: e.tensor_tensor(rch(cumS, kc), rch(cumS, kc),
                                                       b["f32"][:, 0:SB], ALU.mult),
               reads=[bxv["res"]], writes=[res["cumS"]], tag=f"B{bs}_p1{s}_{kc}")
            if pend_kc is not None:
                wrep_p2(pend_kc)
            pend_kc = kc
        wrep_p2(pend_kc)

    def emit_out(s):
        wt, wres = wuse(s, "UFUI")
        ufo = 0
        uio = NKRCH * D
        for dc in range(NCH):
            bo = take_bank()
            ops = []
            out = bo["f32"][:, 0:SB]
            for kc in range(NKRCH):
                ops.append((out, wt[:, ufo + kc * D + dc * 128: ufo + kc * D + (dc + 1) * 128],
                            rch(P2F, kc)))
            ops.append((out, bfb[0:8, dc * 128:(dc + 1) * 128], wfmF))
            for kc in range(NKRCH):
                ops.append((out, wt[:, uio + kc * D + dc * 128: uio + kc * D + (dc + 1) * 128],
                            rch(P2I, kc)))
            ops.append((out, bib[0:8, dc * 128:(dc + 1) * 128], wfmI))
            mm_group(ops, bo,
                     reads=[wres, res["bfb"], res["bib"],
                            res["P2F"], res["P2I"], res["wfmF"], res["wfmI"]],
                     tag=f"B_out{s}_{dc}")
            ae(lambda e, b=bo, dc=dc: e.copy(ff16[:, dc * SB:(dc + 1) * SB],
                                             b["f32"][:, 0:SB]),
               reads=[bo["res"]], writes=[res["ff16"]], tag=f"B_outev{s}_{dc}")

    def emit_wo(s):
        yb = ybufs[s % 2]
        yb_res = ybufs_res[s % 2]
        wt, wres = wuse(s, "WOT")
        for oc in range(NCH):
            bw = take_bank()
            mm_group([(bw["f32"][:, 0:SB], wt[:, c * D + oc * 128: c * D + (oc + 1) * 128],
                       rch(ff16, c)) for c in range(NCH)], bw,
                     reads=[wres, res["ff16"]], tag=f"B_wo{s}_{oc}")
            ve(lambda e, b=bw, oc=oc, yb=yb:
               e.tensor_add(yb[:, oc * SB:(oc + 1) * SB],
                            yb[:, oc * SB:(oc + 1) * SB], b["f32"][:, 0:SB]),
               reads=[bw["res"]], writes=[yb_res], tag=f"B_wores{s}_{oc}")
        dma("sync", dout.rearrange("p (c t) -> p c t", c=NCH)[:, :, s * SB:(s + 1) * SB],
            yb.rearrange("p (c t) -> p c t", c=NCH),
            reads=[yb_res], writes=[out_res], tag=f"B_store{s}")

    emit_load(0, extra=fence0)
    wstream(0, "WQT")
    wstream(0, "WKT")
    wstream(0, "R1T")
    emit_proj(0)
    emit_stats_head(0)
    emit_stats_tail(0)
    emit_stats_bcast(0)
    for s in range(NSB):
        emit_qk(s)
        wstream(s, "R1TI")
        if s + 1 < NSB:
            emit_load(s + 1, extra=(fence1 if s + 1 == 1 else ()))
            emit_proj(s + 1)
        emit_h(s, "F", "R1T", lgsbF, "lgsbF")
        emit_smx_pre(s, "F", lgsbF, "lgsbF", wtsBF, "wtsBF")
        wstream(s, "UFUI")
        wstream(s, "WOT")
        emit_h(s, "I", "R1TI", lgsbI, "lgsbI")
        emit_smx_pre(s, "I", lgsbI, "lgsbI", wtsBI, "wtsBI")
        emit_smx_mid(s, "F", wtsBF, "wtsBF")
        if s + 1 < NSB:
            wstream(s + 1, "WQT")
        emit_smx_mid(s, "I", wtsBI, "wtsBI")
        emit_ywxv(s, "F", Vfb, "Vfb", Wfb, "Wfb", carryF, "carryF",
                  hook=lambda s=s: emit_smx_back(s, "F", wfmF, "wfmF",
                                                 wtsBF, "wtsBF"))
        emit_ywxv(s, "I", Vixb, "Vixb", Wiyb, "Wiyb", carryI, "carryI",
                  hook=lambda s=s: emit_smx_back(s, "I", wfmI, "wfmI",
                                                 wtsBI, "wtsBI"))
        if s + 1 < NSB:
            emit_stats_head(s + 1)
            emit_stats_tail(s + 1)
        emit_out(s)
        if s + 1 < NSB:
            wstream(s + 1, "WKT")
            emit_stats_bcast(s + 1)
        emit_wo(s)
        if s + 1 < NSB:
            wstream(s + 1, "R1T")
        if s == 0:
            tap("y0", ybufs[0], ybufs_res[0])
            tap("Q0", Qfm, res["Qfm"])
            tap("cum0", cumS, res["cumS"])

    P.add("sync", lambda e: e.nop(), reads=[out_res], tag="final_wait")

    P.finalize()

    with (nc.semaphore("sem_t") as s_t, nc.semaphore("sem_v") as s_v,
          nc.semaphore("sem_a") as s_a, nc.semaphore("sem_s") as s_s,
          nc.semaphore("sem_g") as s_g, nc.semaphore("sem_d") as s_d,
          nc.Block() as block):
        sems = {"tensor": s_t, "vector": s_v, "scalar": s_a,
                "sync": s_s, "gpsimd": s_g, "sdma": s_d}

        @block.tensor
        def _(eng):
            P.emit_engine("tensor", eng, sems)

        @block.vector
        def _(eng):
            P.emit_engine("vector", eng, sems)

        @block.scalar
        def _(eng):
            P.emit_engine("scalar", eng, sems)

        @block.sync
        def _(eng):
            P.emit_engine("sync", eng, sems)

        @block.gpsimd
        def _(eng):
            P.emit_engine("gpsimd", eng, sems)

    return nc, tap_drams


def kernel(**inputs):
    xs, w = _host_prep(inputs)
    taps = tuple(_DEBUG_TAPS)
    nc, tap_drams = build_program(taps)
    in_maps = []
    for b in range(8):
        m = {"xT": xs[b]}
        m.update(w)
        in_maps.append(m)
    r = run_bass_kernel_spmd(nc, in_maps, core_ids=list(range(8)))
    outs = []
    for b in range(8):
        o = np.asarray(r.results[b]["outT"], dtype=np.float32)
        outs.append(o.reshape(128, NCH, T).transpose(1, 0, 2).reshape(D, T).T)
    if taps:
        kernel.last_taps = [{k: np.asarray(r.results[b]["tap_" + k]) for k in taps
                             if ("tap_" + k) in r.results[b]} for b in range(8)]
    return np.ascontiguousarray(np.stack(outs, axis=0))


# version 52
# speedup vs baseline: 1.1834x; 1.0047x over previous
import numpy as np

import concourse.bass as bass
import concourse.mybir as mybir
import concourse.bass_utils as _bu
from concourse.bass_utils import run_bass_kernel_spmd

if not getattr(_bu, "_ldw_patch", False):
    _orig_run_command = _bu.run_command

    def _run_command_ldw(argv, **kw):
        if isinstance(argv, list):
            argv = ["--enable-ldw-opt=true" if a == "--enable-ldw-opt=false" else a
                    for a in argv]
        return _orig_run_command(argv, **kw)

    _bu.run_command = _run_command_ldw
    _bu._ldw_patch = True

F32 = mybir.dt.float32
BF16 = mybir.dt.bfloat16
AF = mybir.ActivationFunctionType
ALU = mybir.AluOpType
AX = mybir.AxisListType

D = 1024
T = 4096
NCH = 8
HEADS = 8
KEXP = 8
R = 64
KR = 512
NKRCH = 4
TILE = 512
NT = T // TILE
SB = 512
NSB = T // SB
EPS_LN = 1e-5
EPS_RMS = 1e-6

_ENGINES = ("tensor", "vector", "scalar", "sync", "gpsimd")
_DMA_ENGINES = ("sync", "gpsimd")

_DEBUG_TAPS = []


class _Step:
    __slots__ = ("eng", "emit", "deps", "inc_needed", "sem_val", "tag", "semkey")

    def __init__(self, eng, emit, tag="", semkey=None):
        self.eng = eng
        self.emit = emit
        self.deps = []
        self.inc_needed = eng in _DMA_ENGINES or semkey == "sdma"
        self.sem_val = None
        self.tag = tag
        self.semkey = semkey or eng


class _Res:
    __slots__ = ("writers", "readers", "rsw", "name")

    def __init__(self, name=""):
        self.writers = []
        self.readers = []
        self.rsw = False
        self.name = name


class _Plan:
    def __init__(self):
        self.steps = []

    def add(self, eng, emit, reads=(), writes=(), tag="", extra_deps=(), semkey=None):
        st = _Step(eng, emit, tag, semkey)
        deps = []
        for r in reads:
            deps.extend(r.writers)
        for w in writes:
            deps.extend(w.writers)
            deps.extend(w.readers)
        deps.extend(extra_deps)
        seen = set()
        for d in deps:
            if d is st or id(d) in seen:
                continue
            seen.add(id(d))
            st.deps.append(d)
        for r in reads:
            r.readers.append(st)
            r.rsw = True
        for w in writes:
            if w.rsw:
                w.writers = [st]
                w.readers = []
                w.rsw = False
            else:
                w.writers.append(st)
        self.steps.append(st)
        return st

    def finalize(self):
        for st in self.steps:
            for d in st.deps:
                d.inc_needed = True
        counters = {e: 0 for e in _ENGINES}
        counters["sdma"] = 0
        for st in self.steps:
            if st.inc_needed:
                dma = st.eng in _DMA_ENGINES or st.semkey == "sdma"
                counters[st.semkey] += 16 if dma else 1
                st.sem_val = counters[st.semkey]

    def emit_engine(self, name, eng, sems):
        watermark = {}
        for st in self.steps:
            if st.eng != name:
                continue
            for d in st.deps:
                if watermark.get(d.semkey, 0) >= d.sem_val:
                    continue
                eng.wait_ge(sems[d.semkey], d.sem_val)
                watermark[d.semkey] = d.sem_val
            insts = st.emit(eng)
            if st.inc_needed:
                if not isinstance(insts, (list, tuple)):
                    insts = [insts]
                if name in _DMA_ENGINES:
                    for i in insts:
                        i.then_inc(sems[name], 16)
                    eng.wait_ge(sems[name], st.sem_val)
                    watermark[name] = st.sem_val
                elif st.semkey == "sdma":
                    for i in insts:
                        i.then_inc(sems["sdma"], 16)
                else:
                    insts[-1].then_inc(sems[name], 1)


def _pack_cm(m):
    r, c = m.shape
    return np.ascontiguousarray(
        m.reshape(r // 128, 128, c).transpose(1, 0, 2).reshape(128, -1))


def _host_prep(inp):
    g = {k: np.asarray(v, dtype=np.float32) for k, v in inp.items()}
    w = {}
    x = g["L0_attn_x"]
    xs = [_pack_cm(np.ascontiguousarray(x[b].T)) for b in range(8)]
    w["wqkvT"] = _pack_cm(np.ascontiguousarray(g["L0_attn_w_qkv"].T))
    w["woutT"] = _pack_cm(np.ascontiguousarray(g["L0_attn_w_out"].T))
    w["WQT"] = _pack_cm(np.ascontiguousarray(g["L0_ffn_W_Q"].T))
    w["WKT"] = _pack_cm(np.ascontiguousarray(g["L0_ffn_W_K"].T))
    w["WIN"] = _pack_cm(np.ascontiguousarray(g["L0_ffn_W_inv"]))
    w["R1T"] = _pack_cm(np.ascontiguousarray(g["L0_ffn_router_w1"].T))
    w["WOT"] = _pack_cm(np.ascontiguousarray(g["L0_ffn_W_O"].T))
    w["R2T"] = _pack_cm(np.ascontiguousarray(g["L0_ffn_router_w2"].T))
    w["Vf"] = _pack_cm(g["L0_ffn_V_fwd"].transpose(1, 0, 2).reshape(D, KR))
    w["Wf"] = _pack_cm(g["L0_ffn_W_fwd"].transpose(1, 0, 2).reshape(D, KR))
    w["Vix"] = _pack_cm(g["L0_ffn_W_inv_exp"].transpose(1, 0, 2).reshape(D, KR))
    w["Wiy"] = _pack_cm(g["L0_ffn_V_inv"].transpose(1, 0, 2).reshape(D, KR))
    w["Uf"] = _pack_cm(g["L0_ffn_U_fwd"].transpose(0, 2, 1).reshape(KR, D))
    w["Ui"] = _pack_cm(g["L0_ffn_U_inv"].transpose(0, 2, 1).reshape(KR, D))
    w["bfw"] = np.ascontiguousarray(g["L0_ffn_b_fwd"])
    w["biw"] = np.ascontiguousarray(g["L0_ffn_b_inv"])
    w["gamma"] = np.ascontiguousarray(g["L0_attn_norm_weight"].reshape(NCH, 128).T)
    w["r1b"] = np.ascontiguousarray(g["L0_ffn_router_b1"].reshape(NCH, 128).T)
    w["temp"] = np.ascontiguousarray(g["L0_attn_temperature"].reshape(8, 1))
    w["r2b"] = np.ascontiguousarray(g["L0_ffn_router_b2"].reshape(8, 1))
    w["eb"] = np.ascontiguousarray(g["L0_ffn_expert_bias"].reshape(8, 1))
    w["alpha"] = np.ascontiguousarray(g["L0_ffn_alpha_bi"].reshape(1, 1))
    w["ident"] = np.eye(128, dtype=np.float32)
    mask = np.full((128, 128), -30000.0, dtype=np.float32)
    for h in range(HEADS):
        mask[16 * h:16 * h + 16, 16 * h:16 * h + 16] = 0.0
    w["mask"] = mask
    nv = np.zeros((128, 32), dtype=np.float32)
    for blk in range(32):
        nv[:, blk] = 1.0 / (blk * 128 + np.arange(128) + 1.0)
    w["nvinv"] = nv
    esel = np.zeros((8, 512), dtype=np.float32)
    for c in range(4):
        for p in range(128):
            esel[2 * c + p // 64, c * 128 + p] = 1.0
    w["Esel"] = esel
    hsel = np.zeros((8, 128), dtype=np.float32)
    for p in range(128):
        hsel[p // 16, p] = 1.0
    w["hsel"] = hsel
    return xs, w


def build_program(taps=()):
    nc = bass.Bass("TRN2", target_bir_lowering=False)
    P = _Plan()

    dx = nc.dram_tensor("xT", [128, NCH * T], F32, kind="ExternalInput").ap()
    dout = nc.dram_tensor("outT", [128, NCH * T], F32, kind="ExternalOutput").ap()
    dw = {}
    for name, shape in [
        ("wqkvT", [128, NCH * 384]), ("woutT", [128, 1024]),
        ("WQT", [128, NCH * D]), ("WKT", [128, NCH * D]), ("WIN", [128, NCH * D]),
        ("R1T", [128, NCH * D]), ("WOT", [128, NCH * D]), ("R2T", [128, 64]),
        ("Vf", [128, NCH * KR]), ("Wf", [128, NCH * KR]),
        ("Vix", [128, NCH * KR]), ("Wiy", [128, NCH * KR]),
        ("Uf", [128, NKRCH * D]), ("Ui", [128, NKRCH * D]),
        ("bfw", [8, 1024]), ("biw", [8, 1024]),
        ("gamma", [128, 8]), ("r1b", [128, 8]),
        ("temp", [8, 1]), ("r2b", [8, 1]), ("eb", [8, 1]), ("alpha", [1, 1]),
        ("ident", [128, 128]), ("mask", [128, 128]), ("nvinv", [128, 32]),
        ("Esel", [8, 512]), ("hsel", [8, 128]),
    ]:
        dw[name] = nc.dram_tensor(name, shape, F32, kind="ExternalInput").ap()

    tap_drams = {}
    res = {}
    STREAM_W = ("WQT", "WKT", "R1T", "R1TI", "UFUI", "WOT")
    dscr = {}
    for nm in STREAM_W:
        dscr[nm] = nc.dram_tensor("scr_" + nm, [128, NCH * D], BF16).ap()
        res["scr_" + nm] = _Res("scr_" + nm)
    for nm in ("VIX2", "WIY2"):
        dscr[nm] = nc.dram_tensor("scr_" + nm, [128, NCH * KR], BF16).ap()
        res["scr_" + nm] = _Res("scr_" + nm)

    def sb(name, shape, dtype):
        h = nc.alloc_sbuf_tensor(name, shape, dtype)
        res[name] = _Res(name)
        return h.ap()

    WATb = sb("WATb", [128, 1024], BF16)
    vfm = sb("vfm", [128, T], BF16)
    wstr = [sb("wsA", [128, NCH * D], BF16), sb("wsB", [128, NCH * D], BF16),
            sb("wsC", [128, NCH * D], BF16)]
    wstr_res = [res["wsA"], res["wsB"], res["wsC"]]
    R2Tb = sb("R2Tb", [128, 64], BF16)
    Vfb = sb("Vfb", [128, NCH * KR], BF16)
    Wfb = sb("Wfb", [128, NCH * KR], BF16)
    Vixb = sb("Vixb", [128, NCH * KR], BF16)
    Wiyb = sb("Wiyb", [128, NCH * KR], BF16)
    bfb = sb("bfb", [8, 1024], BF16)
    bib = sb("bib", [8, 1024], BF16)
    identb = sb("identb", [128, 128], BF16)
    nvinvS = sb("nvinvS", [128, 32], F32)
    EselB = sb("EselB", [8, 512], BF16)
    gammaS = sb("gammaS", [128, 8], F32)
    r1bS = sb("r1bS", [128, 8], F32)
    tempS = sb("tempS", [8, 1], F32)
    r2bS = sb("r2bS", [8, 1], F32)
    ebS = sb("ebS", [8, 1], F32)
    alphaS = sb("alphaS", [1, 1], F32)
    onesB = sb("onesB", [128, 1], BF16)
    negidB = sb("negidB", [128, 1], BF16)
    invdB = sb("invdB", [128, 1], BF16)
    ones1F = sb("ones1F", [1, 128], F32)
    onesColF = sb("onesColF", [128, 1], F32)
    gSqkvB = sb("gSqkvB", [1, 384], BF16)
    bias2 = sb("bias2", [8, 1], F32)
    ethb = sb("ethb", [128, 1], F32)
    alphab = sb("alphab", [128, 1], F32)
    etmpb = sb("etmpb", [8, 1], BF16)
    negmB = sb("negmB", [1, TILE], BF16)
    negmB1 = sb("negmB1", [1, TILE], BF16)
    maskS = sb("maskS", [128, 128], F32)
    hselB = sb("hselB", [8, 128], BF16)
    rkS = sb("rkS", [128, 1], F32)
    st_a = sb("st_a", [1, TILE], F32)
    st_b = sb("st_b", [1, TILE], F32)
    st_c = sb("st_c", [1, TILE], F32)
    sqA = sb("sqA", [128, 1], F32)
    nrmA = sb("nrmA", [128, 1], F32)
    sclA = sb("sclA", [128, 1], F32)
    carryF = sb("carryF", [128, 4], F32)
    carryI = sb("carryI", [128, 4], F32)
    smx1 = sb("smx1", [128, 1], F32)
    smx2 = sb("smx2", [128, 1], F32)
    smxM = sb("smxM", [128, 4], F32)
    sumM = sb("sumM", [128, 4], F32)
    smxQ = sb("smxQ", [128, 4], F32)
    wtsBF = sb("wtsBF", [128, 32], BF16)
    wtsBI = sb("wtsBI", [128, 32], BF16)
    out_res = _Res("OUT")
    res["OUT"] = out_res

    SBUF_TOP = 229344
    arena_base = nc._sbuf_addr_for_side("left")
    arena_base = (arena_base + 31) // 32 * 32

    class _Arena:
        def __init__(self, base):
            self.off = base

        def alloc(self, name, shape, dtype):
            per_part = int(np.prod(shape[1:])) * mybir.dt.size(dtype)
            off = (self.off + 31) // 32 * 32
            assert off + per_part <= SBUF_TOP, (
                f"arena overflow: {name} at {off}+{per_part} > {SBUF_TOP}")
            h = nc.alloc_sbuf_tensor_at(name, shape, dtype, offset=off)
            self.off = off + per_part
            res[name] = _Res(name)
            return h.ap()

    banks = []
    for i in range(8):
        h = nc.alloc_psum_tensor(f"pb{i}", [128, 512], F32)
        banks.append({"f32": h.ap(), "b16": h.ap().bitcast(BF16),
                      "res": _Res(f"bank{i}"), "i": i})
    bank_rr = [0]
    pinned = set()

    def take_bank(pin=False):
        for _ in range(8):
            b = banks[bank_rr[0] % 8]
            bank_rr[0] += 1
            if b["i"] not in pinned:
                if pin:
                    pinned.add(b["i"])
                return b
        raise RuntimeError("all psum banks pinned")

    def unpin(b):
        pinned.discard(b["i"])

    def dma(queue, out_ap, in_ap, reads=(), writes=(), tag=""):
        return P.add(queue, lambda e, o=out_ap, i=in_ap: e.dma_start(out=o, in_=i),
                     reads=reads, writes=writes, tag=tag,
                     semkey="sdma" if queue == "scalar" else None)

    def tap(name, ap_, rd):
        if name not in taps:
            return
        d = nc.dram_tensor("tap_" + name, list(ap_.shape), ap_.dtype,
                           kind="ExternalOutput").ap()
        tap_drams[name] = d
        dma("gpsimd", d, ap_, reads=[rd], tag="tap_" + name)

    def mm_group(ops, bank, reads, tag=""):
        n = len(ops)

        def em(eng, ops=ops, n=n):
            insts = []
            for i, (o, l, r) in enumerate(ops):
                insts.append(eng.matmul(o, l, r, start=(i == 0), stop=(i == n - 1)))
            return insts
        return P.add("tensor", em, reads=reads, writes=[bank["res"]], tag=tag)

    def ve(emit, reads=(), writes=(), tag=""):
        return P.add("vector", emit, reads=reads, writes=writes, tag=tag)

    def ae(emit, reads=(), writes=(), tag=""):
        return P.add("scalar", emit, reads=reads, writes=writes, tag=tag)

    def te(emit, reads=(), writes=(), tag=""):
        return P.add("tensor", emit, reads=reads, writes=writes, tag=tag)

    def ge(emit, reads=(), writes=(), tag=""):
        return P.add("gpsimd", emit, reads=reads, writes=writes, tag=tag)

    def rch(ap_, c, w=SB):
        return ap_.rearrange("p (c t) -> p c t", c=ap_.shape[1] // w)[:, c, :]

    arena = _Arena(arena_base)
    xt0 = arena.alloc("xt0", [128, NCH * TILE], F32)
    xt1 = arena.alloc("xt1", [128, NCH * TILE], F32)
    qfm = arena.alloc("qfm", [128, T], BF16)
    kfm = arena.alloc("kfm", [128, T], BF16)
    xb16 = arena.alloc("xb16", [128, NCH * TILE], BF16)
    xb16b = arena.alloc("xb16b", [128, NCH * TILE], BF16)
    xb16c = arena.alloc("xb16c", [128, NCH * TILE], BF16)
    xsq16 = arena.alloc("xsq16", [128, NCH * TILE], BF16)
    ssbA = arena.alloc("ssbA", [128, TILE], F32)
    ssbA1 = arena.alloc("ssbA1", [128, TILE], F32)
    qtm = arena.alloc("qtm", [128, 128], BF16)
    ktm = arena.alloc("ktm", [128, 128], BF16)
    qtm2 = arena.alloc("qtm2", [128, 128], BF16)
    ktm2 = arena.alloc("ktm2", [128, 128], BF16)
    simsb = arena.alloc("simsb", [128, 128], F32)
    Ab = arena.alloc("Ab", [128, 128], BF16)
    identf = arena.alloc("identf", [128, 128], F32)
    nsq = arena.alloc("nsq", [128, TILE], BF16)
    sqQa = arena.alloc("sqQa", [128, 1], F32)
    sqKa = arena.alloc("sqKa", [128, 1], F32)
    nred = arena.alloc("nred", [128, 1], F32)
    st_a2 = arena.alloc("st_a2", [1, TILE], F32)
    st_b2 = arena.alloc("st_b2", [1, TILE], F32)
    st_c2 = arena.alloc("st_c2", [1, TILE], F32)
    wqkvTb = arena.alloc("wqkvTb", [128, NCH * 384], BF16)
    woutTb = arena.alloc("woutTb", [128, 1024], BF16)


    def wload(dst_ap, dname, rkey):
        dma("gpsimd", dst_ap, dw[dname], writes=[res[rkey]], tag="ld_" + dname)

    dma("gpsimd", wstr[0], dw["WIN"], writes=[wstr_res[0]], tag="prep_ld_WIN")
    dma("gpsimd", qfm, dw["Vix"], writes=[res["qfm"]], tag="prep_ld_Vix")
    dma("gpsimd", wstr[1], dw["R1T"], writes=[wstr_res[1]], tag="prep_ld_R1T")
    wload(wqkvTb, "wqkvT", "wqkvTb")
    wload(identb, "ident", "identb")
    wload(hselB, "hsel", "hselB")
    wload(woutTb, "woutT", "woutTb")
    dma("gpsimd", maskS, dw["mask"], writes=[res["maskS"]])
    dma("gpsimd", identf, dw["ident"], writes=[res["identf"]])
    dma("gpsimd", gammaS, dw["gamma"], writes=[res["gammaS"]])
    dma("gpsimd", tempS, dw["temp"], writes=[res["tempS"]])

    for ap_, val, rk in ((sqQa, 0.0, "sqQa"), (sqKa, 0.0, "sqKa"),
                         (onesB, 1.0, "onesB"), (negidB, -1.0 / 1024.0, "negidB"),
                         (invdB, 1.0 / 1024.0, "invdB"), (ones1F, 1.0, "ones1F"),
                         (onesColF, 1.0, "onesColF"), (carryF, 0.0, "carryF"),
                         (carryI, 0.0, "carryI")):
        ve(lambda e, ap_=ap_, val=val: e.memset(ap_, val),
           writes=[res[rk]], tag="memset_" + rk)

    xts = [xt0, xt1]
    xts_res = [res["xt0"], res["xt1"]]
    xb16s = [xb16, xb16b, xb16c]
    xb16s_res = [res["xb16"], res["xb16b"], res["xb16c"]]
    ssbAs = [ssbA, ssbA1]
    ssbAs_res = [res["ssbA"], res["ssbA1"]]
    negmBs = [negmB, negmB1]
    negmBs_res = [res["negmB"], res["negmB1"]]
    qkv_dst = [(qfm, res["qfm"]), (kfm, res["kfm"]), (vfm, res["vfm"])]

    A_banks = {}

    xbh_res = [[_Res(f"xbh{i}_{h}") for h in range(2)] for i in range(3)]
    xsqh_res = [_Res(f"xsqh{h}") for h in range(2)]
    HL = NCH // 2 * TILE

    def emit_A_head_dma(t):
        xt = xts[t % 2]
        xt_res = xts_res[t % 2]
        xb = xb16s[t % 3]
        xb_res = xb16s_res[t % 3]
        dma("sync", xt.rearrange("p (c t) -> p c t", c=NCH),
            dx.rearrange("p (c t) -> p c t", c=NCH)[:, :, t * TILE:(t + 1) * TILE],
            writes=[xt_res], tag=f"A_x{t}")
        for h in range(2):
            ae(lambda e, xt=xt, xb=xb, h=h:
               e.copy(xb[:, h * HL:(h + 1) * HL], xt[:, h * HL:(h + 1) * HL]),
               reads=[xt_res], writes=[xbh_res[t % 3][h], xb_res],
               tag=f"A_cast{t}_{h}")

    def emit_A_head_mm(t):
        xb = xb16s[t % 3]
        bnm = take_bank(pin=True)
        for h in range(2):
            ae(lambda e, xb=xb, h=h:
               e.activation(xsq16[:, h * HL:(h + 1) * HL],
                            xb[:, h * HL:(h + 1) * HL], AF.Square),
               reads=[xbh_res[t % 3][h]], writes=[xsqh_res[h], res["xsq16"]],
               tag=f"A_sq{t}_{h}")
        for h in range(2):
            te(lambda e, b=bnm, h=h, xb=xb:
               [e.matmul(b["f32"][0:1, 0:TILE], negidB, rch(xb, c, TILE),
                         start=(c == 0), stop=(c == NCH - 1))
                for c in range(h * NCH // 2, (h + 1) * NCH // 2)],
               reads=[xbh_res[t % 3][h], res["negidB"]], writes=[bnm["res"]],
               tag=f"A_negm{t}_{h}")
        for h in range(2):
            te(lambda e, b=bnm, h=h:
               [e.matmul(b["f32"][32:33, 0:TILE], invdB, rch(xsq16, c, TILE),
                         start=(c == 0), stop=(c == NCH - 1))
                for c in range(h * NCH // 2, (h + 1) * NCH // 2)],
               reads=[xsqh_res[h], res["invdB"]], writes=[bnm["res"]],
               tag=f"A_msq{t}_{h}")
        A_banks[t] = bnm

    def emit_A_head(t):
        emit_A_head_dma(t)
        emit_A_head_mm(t)


    emit_A_head(0)
    emit_A_head(1)

    def prep_vw(nm, src_ap, src_res, stage_ap, stage_res):
        for a in range(NCH):
            bk_ = take_bank()
            mm_group([(bk_["f32"][:, 0:KR],
                       wstr[0][:, c * D + a * 128:c * D + (a + 1) * 128],
                       src_ap.rearrange("p (c t) -> p c t", c=NCH)[:, c, :])
                      for c in range(NCH)], bk_,
                     reads=[wstr_res[0], src_res], tag=f"prep_{nm}_{a}")
            ae(lambda e, b=bk_, a=a, stage_ap=stage_ap:
               e.copy(stage_ap[:, a * KR:(a + 1) * KR], b["f32"][:, 0:KR]),
               reads=[bk_["res"]], writes=[stage_res], tag=f"prep_{nm}ev_{a}")

    dma("gpsimd", xb16c, dw["Wiy"], writes=[res["xb16c"]], tag="prep_ld_Wiy")
    prep_vw("VIX2", qfm, res["qfm"], kfm, res["kfm"])
    xt1b = xt1.bitcast(BF16)
    for a in range(NCH):
        for hf in range(2):
            bk_ = take_bank()
            mm_group([(bk_["f32"][:, 0:512],
                       wstr[0][:, c * D + a * 128:c * D + (a + 1) * 128],
                       wstr[1][:, c * D + hf * 512:c * D + (hf + 1) * 512])
                      for c in range(NCH)], bk_,
                     reads=[wstr_res[0], wstr_res[1]], tag=f"prep_R1TI_{a}_{hf}")
            ae(lambda e, b=bk_, a=a, hf=hf:
               e.copy(xt1b[:, a * D + hf * 512:a * D + (hf + 1) * 512],
                      b["f32"][:, 0:512]),
               reads=[bk_["res"]], writes=[res["xt1"]], tag=f"prep_R1TIev_{a}_{hf}")
    prep_vw("WIY2", xb16c, res["xb16c"], qfm, res["qfm"])
    dma("gpsimd", dscr["R1T"], wstr[1], reads=[wstr_res[1]],
        writes=[res["scr_R1T"]], tag="precast_st_R1T")
    dma("gpsimd", dscr["R1TI"], xt1b, reads=[res["xt1"]],
        writes=[res["scr_R1TI"]], tag="prep_st_R1TI")
    dma("gpsimd", dscr["VIX2"], kfm, reads=[res["kfm"]], writes=[res["scr_VIX2"]],
        tag="prep_st_VIX2")
    dma("gpsimd", dscr["WIY2"], qfm, reads=[res["qfm"]], writes=[res["scr_WIY2"]],
        tag="prep_st_WIY2")

    for c in range(NCH):
        ve(lambda e, c=c: e.tensor_scalar_mul(wqkvTb[:, c * 384:(c + 1) * 384],
                                              wqkvTb[:, c * 384:(c + 1) * 384],
                                              gammaS[:, c:c + 1]),
           reads=[res["gammaS"]], writes=[res["wqkvTb"]], tag=f"gt_scale{c}")

    bk = take_bank()
    mm_group([(bk["f32"][0:1, 0:384], onesB, wqkvTb[:, c * 384:(c + 1) * 384])
              for c in range(NCH)], bk,
             reads=[res["wqkvTb"], res["onesB"]], tag="gSqkv_mm")
    ae(lambda e, b=bk: e.copy(gSqkvB, b["f32"][0:1, 0:384]),
       reads=[bk["res"]], writes=[res["gSqkvB"]], tag="gSqkv_cp")

    ae(lambda e: e.activation(tempS, tempS, AF.Exp), writes=[res["tempS"]], tag="exp_temp")
    ae(lambda e: e.copy(etmpb, tempS), reads=[res["tempS"]], writes=[res["etmpb"]])
    bk = take_bank()
    mm_group([(bk["f32"][0:128, 0:1], hselB, etmpb)], bk,
             reads=[res["hselB"], res["etmpb"]], tag="ethb_mm")
    ve(lambda e, b=bk: e.tensor_copy(ethb, b["f32"][0:128, 0:1]),
       reads=[bk["res"]], writes=[res["ethb"]])


    def emit_A_tail(t):
        bnm = A_banks.pop(t)
        bms = bnm
        sA, sB, sC = ((st_a, st_b, st_c) if t % 2 == 0 else (st_a2, st_b2, st_c2))
        rA, rB, rC = (("st_a", "st_b", "st_c") if t % 2 == 0
                      else ("st_a2", "st_b2", "st_c2"))
        ae(lambda e, b=bnm, sA=sA: e.activation(sA, b["f32"][0:1, 0:TILE], AF.Square),
           reads=[bnm["res"]], writes=[res[rA]], tag=f"A_m2{t}")
        ve(lambda e, b=bms, sA=sA, sB=sB: e.tensor_sub(sB, b["f32"][32:33, 0:TILE], sA),
           reads=[bms["res"], res[rA]], writes=[res[rB]], tag=f"A_var{t}")
        ve(lambda e, sA=sA, sB=sB: e.tensor_scalar(sA, sB, 1.0 + EPS_RMS,
                                                   EPS_LN * EPS_RMS,
                                                   ALU.mult, ALU.add),
           reads=[res[rB]], writes=[res[rA]])
        ae(lambda e, sA=sA: e.activation(sA, sA, AF.Ln), writes=[res[rA]])
        ae(lambda e, sA=sA, sC=sC: e.activation(sC, sA, AF.Exp, scale=-0.5),
           reads=[res[rA]], writes=[res[rC]], tag=f"A_s{t}")
        ae(lambda e, b=bnm, t=t: e.copy(negmBs[t % 2], b["f32"][0:1, 0:TILE]),
           reads=[bnm["res"]], writes=[negmBs_res[t % 2]], tag=f"A_negmB{t}")
        bsc = take_bank()
        mm_group([(bsc["f32"][:, 0:TILE], ones1F, sC)], bsc,
                 reads=[res[rC], res["ones1F"]], tag=f"A_sbc{t}")
        ve(lambda e, b=bsc, t=t: e.tensor_copy(ssbAs[t % 2], b["f32"][:, 0:TILE]),
           reads=[bsc["res"]], writes=[ssbAs_res[t % 2]], tag=f"A_ssb{t}")
        unpin(bnm)

    bG = take_bank(pin=True)
    qtms = [qtm, qtm2]
    ktms = [ktm, ktm2]
    qtms_res = [res["qtm"], res["qtm2"]]
    ktms_res = [res["ktm"], res["ktm2"]]

    def emit_A_gram(t):
        for b_ in range(TILE // 128):
            i = t * (TILE // 128) + b_
            bt1 = take_bank()
            te(lambda e, b=bt1, i=i: e.transpose(b["b16"][:, 0:128],
                                                 qfm[:, i * 128:(i + 1) * 128], identb),
               reads=[res["qfm"], res["identb"]], writes=[bt1["res"]], tag=f"G_tq{i}")
            ve(lambda e, b=bt1, i=i: e.tensor_copy(qtms[i % 2], b["b16"][:, 0:128]),
               reads=[bt1["res"]], writes=[qtms_res[i % 2]])
            bt2 = take_bank()
            te(lambda e, b=bt2, i=i: e.transpose(b["b16"][:, 0:128],
                                                 kfm[:, i * 128:(i + 1) * 128], identb),
               reads=[res["kfm"], res["identb"]], writes=[bt2["res"]], tag=f"G_tk{i}")
            ve(lambda e, b=bt2, i=i: e.tensor_copy(ktms[i % 2], b["b16"][:, 0:128]),
               reads=[bt2["res"]], writes=[ktms_res[i % 2]])
            te(lambda e, b=bG, i=i: e.matmul(b["f32"][:, 0:128], qtms[i % 2], ktms[i % 2],
                                             start=(i == 0), stop=(i == 31)),
               reads=[qtms_res[i % 2], ktms_res[i % 2]], writes=[bG["res"]],
               tag=f"G_mm{i}")

    def emit_A_qkv(t):
        xb = xb16s[t % 3]
        xb_res = xb16s_res[t % 3]
        for oc in range(3):
            bq = take_bank()
            ops = [(bq["f32"][:, 0:TILE],
                    wqkvTb[:, c * 384 + oc * 128: c * 384 + (oc + 1) * 128],
                    rch(xb, c, TILE)) for c in range(NCH)]
            ops.append((bq["f32"][:, 0:TILE], gSqkvB[:, oc * 128:(oc + 1) * 128],
                        negmBs[t % 2]))
            mm_group(ops, bq, reads=[res["wqkvTb"], xb_res, res["gSqkvB"],
                                     negmBs_res[t % 2]], tag=f"A_qkv{t}_{oc}")
            dst, dres = qkv_dst[oc]
            ve(lambda e, b=bq, dst=dst, t=t:
               e.tensor_mul(dst[:, t * TILE:(t + 1) * TILE], b["f32"][:, 0:TILE],
                            ssbAs[t % 2]),
               reads=[bq["res"], ssbAs_res[t % 2]], writes=[dres], tag=f"A_ev{t}_{oc}")
        for (fm2, fres2, acc, accr) in ((qfm, res["qfm"], sqQa, "sqQa"),
                                        (kfm, res["kfm"], sqKa, "sqKa")):
            ae(lambda e, fm2=fm2, t=t: e.activation(nsq, fm2[:, t * TILE:(t + 1) * TILE],
                                                    AF.Square),
               reads=[fres2], writes=[res["nsq"]], tag=f"A_nsq{t}")
            ve(lambda e: e.reduce_sum(nred, nsq, AX.X),
               reads=[res["nsq"]], writes=[res["nred"]], tag=f"A_nred{t}")
            ve(lambda e, acc=acc: e.tensor_add(acc, acc, nred),
               reads=[res["nred"]], writes=[res[accr]], tag=f"A_nacc{t}")

    emit_A_tail(0)
    for t in range(NT):
        if t + 2 < NT:
            emit_A_head(t + 2)
        if t + 1 < NT:
            emit_A_tail(t + 1)
        emit_A_qkv(t)
        emit_A_gram(t)

    tap("qfm", qfm, res["qfm"])
    tap("kfm", kfm, res["kfm"])
    tap("vfm", vfm, res["vfm"])

    for (acc, accr, which) in ((sqQa, "sqQa", "q"), (sqKa, "sqKa", "k")):
        ae(lambda e, acc=acc: e.activation(nrmA, acc, AF.Sqrt),
           reads=[res[accr]], writes=[res["nrmA"]])
        ve(lambda e: e.tensor_scalar_max(nrmA, nrmA, 1e-12), writes=[res["nrmA"]])
        ve(lambda e: e.reciprocal(nrmA, nrmA), writes=[res["nrmA"]])
        if which == "q":
            ve(lambda e: e.tensor_mul(sclA, nrmA, ethb),
               reads=[res["nrmA"], res["ethb"]], writes=[res["sclA"]], tag="N_qscale")
        else:
            ve(lambda e: e.tensor_copy(rkS, nrmA),
               reads=[res["nrmA"]], writes=[res["rkS"]], tag="N_kscale")

    brt = take_bank()
    te(lambda e, b=brt: e.transpose(b["f32"][0:1, 0:128], rkS, identf),
       reads=[res["rkS"], res["identf"]], writes=[brt["res"]], tag="S_rkT")
    ae(lambda e, b=brt: e.copy(st_a[:, 0:128], b["f32"][0:1, 0:128]),
       reads=[brt["res"]], writes=[res["st_a"]], tag="S_rkrow")
    bcb = take_bank()
    mm_group([(bcb["f32"][:, 0:128], ones1F, st_a[:, 0:128])], bcb,
             reads=[res["st_a"], res["ones1F"]], tag="S_colmm")
    ae(lambda e, b=bcb: e.copy(ssbA[:, 0:128], b["f32"][:, 0:128]),
       reads=[bcb["res"]], writes=[res["ssbA"]], tag="S_colcp")
    ve(lambda e, b=bG: e.scalar_tensor_tensor(simsb, b["f32"][:, 0:128], sclA,
                                              ssbA[:, 0:128],
                                              ALU.mult, ALU.mult),
       reads=[bG["res"], res["sclA"], res["ssbA"]], writes=[res["simsb"]],
       tag="S_scale")
    unpin(bG)
    ve(lambda e: e.tensor_add(simsb, simsb, maskS),
       reads=[res["maskS"]], writes=[res["simsb"]], tag="S_mask")
    ve(lambda e: e.reduce_max(smx1, simsb, AX.X, negate=True),
       reads=[res["simsb"]], writes=[res["smx1"]])
    ae(lambda e: e.activation(simsb, simsb, AF.Exp, bias=smx1),
       reads=[res["smx1"]], writes=[res["simsb"]], tag="S_exp")
    ve(lambda e: e.reduce_sum(smx2, simsb, AX.X),
       reads=[res["simsb"]], writes=[res["smx2"]])
    ve(lambda e: e.reciprocal(smx2, smx2), writes=[res["smx2"]])
    ve(lambda e: e.tensor_scalar_mul(Ab, simsb, smx2),
       reads=[res["simsb"], res["smx2"]], writes=[res["Ab"]], tag="S_A")
    tap("Ab", Ab, res["Ab"])
    for half in range(2):
        bw = take_bank()
        mm_group([(bw["f32"][:, 0:512], Ab, woutTb[:, half * 512:(half + 1) * 512])],
                 bw, reads=[res["Ab"], res["woutTb"]], tag=f"WAT{half}")
        ve(lambda e, b=bw, half=half:
           e.tensor_copy(WATb[:, half * 512:(half + 1) * 512], b["f32"][:, 0:512]),
           reads=[bw["res"]], writes=[res["WATb"]], tag=f"WATev{half}")
    tap("WATb", WATb, res["WATb"])

    for nm, dst, rkey in [("R2T", R2Tb, "R2Tb"),
                          ("Vf", Vfb, "Vfb"), ("Wf", Wfb, "Wfb"),
                          ("bfw", bfb, "bfb"), ("biw", bib, "bib"),
                          ("Esel", EselB, "EselB")]:
        wload(dst, nm, rkey)
    dma("gpsimd", Vixb, dscr["VIX2"], reads=[res["scr_VIX2"]],
        writes=[res["Vixb"]], tag="ld_VIX2")
    dma("gpsimd", Wiyb, dscr["WIY2"], reads=[res["scr_WIY2"]],
        writes=[res["Wiyb"]], tag="ld_WIY2")
    dma("gpsimd", r1bS, dw["r1b"], writes=[res["r1bS"]])
    dma("gpsimd", r2bS, dw["r2b"], writes=[res["r2bS"]])
    dma("gpsimd", ebS, dw["eb"], writes=[res["ebS"]])
    dma("gpsimd", alphaS, dw["alpha"], writes=[res["alphaS"]])
    dma("gpsimd", nvinvS, dw["nvinv"], writes=[res["nvinvS"]])

    ve(lambda e: e.tensor_add(bias2, r2bS, ebS),
       reads=[res["r2bS"], res["ebS"]], writes=[res["bias2"]], tag="bias2")
    bk = take_bank()
    mm_group([(bk["f32"][0:128, 0:1], ones1F, alphaS)], bk,
             reads=[res["ones1F"], res["alphaS"]], tag="alpha_mm")
    ve(lambda e, b=bk: e.tensor_copy(alphab, b["f32"][0:128, 0:1]),
       reads=[bk["res"]], writes=[res["alphab"]])
    ve(lambda e: e.tensor_scalar_mul(bib, bib, alphab[0:8, 0:1]),
       reads=[res["alphab"]], writes=[res["bib"]], tag="abi")

    for nm in ("WQT", "WKT", "WOT"):
        buf = wstr[0] if nm != "WKT" else wstr[1]
        bres = wstr_res[0] if nm != "WKT" else wstr_res[1]
        dma("gpsimd", buf, dw[nm], writes=[bres], tag=f"precast_ld_{nm}")
        dma("gpsimd", dscr[nm], buf, reads=[bres], writes=[res["scr_" + nm]],
            tag=f"precast_st_{nm}")
    buf, bres = wstr[1], wstr_res[1]
    dma("gpsimd", buf[:, 0:NKRCH * D], dw["Uf"], writes=[bres], tag="precast_ld_Uf")
    dma("gpsimd", buf[:, NKRCH * D:2 * NKRCH * D], dw["Ui"], writes=[bres],
        tag="precast_ld_Ui")
    ve(lambda e, buf=buf: e.tensor_scalar_mul(
        buf[:, NKRCH * D:2 * NKRCH * D], buf[:, NKRCH * D:2 * NKRCH * D], alphab),
       reads=[res["alphab"]], writes=[bres], tag="precast_aUi")
    dma("gpsimd", dscr["UFUI"], buf, reads=[bres], writes=[res["scr_UFUI"]],
        tag="precast_st_UFUI")

    def _steps_of(*names):
        out, seen = [], set()
        for nm_ in names:
            r_ = res[nm_]
            for st_ in r_.writers + r_.readers:
                if id(st_) not in seen:
                    seen.add(id(st_))
                    out.append(st_)
        return out

    fence0 = _steps_of("xt0")
    fence1 = _steps_of("xt1")

    arena = _Arena(arena_base)
    ybufs = [arena.alloc("ybuf0", [128, NCH * SB], F32),
             arena.alloc("ybuf1", [128, NCH * SB], F32)]
    ybufs_res = [res["ybuf0"], res["ybuf1"]]
    yb16 = arena.alloc("yb16", [128, NCH * SB], BF16)
    lntmp = arena.alloc("lntmp", [128, NCH * SB], BF16)
    lntmpb = lntmp
    Qfm = arena.alloc("Qfm", [128, NCH * SB], BF16)
    Kfm = arena.alloc("Kfm", [128, NCH * SB], BF16)
    hsb = [arena.alloc("hsb0", [128, SB], BF16), arena.alloc("hsb1", [128, SB], BF16)]
    hsb_res = [res["hsb0"], res["hsb1"]]
    lgsbF = arena.alloc("lgsbF", [8, SB], BF16)
    lgsbI = arena.alloc("lgsbI", [8, SB], BF16)
    wfmF = arena.alloc("wfmF", [8, SB], BF16)
    wfmI = arena.alloc("wfmI", [8, SB], BF16)
    cumS = arena.alloc("cumS", [128, NKRCH * SB], F32)
    P2F = arena.alloc("P2F", [128, NKRCH * SB], BF16)
    P2I = arena.alloc("P2I", [128, NKRCH * SB], BF16)
    ff16 = arena.alloc("ff16", [128, NCH * SB], BF16)

    def wuse(s, nm):
        idx = list(STREAM_W).index(nm)
        return wstr[idx % 3], wstr_res[idx % 3]

    def wstream(s, nm):
        buf, bres = wuse(s, nm)
        dma("gpsimd", buf, dscr[nm], reads=[res["scr_" + nm]], writes=[bres],
            tag=f"B_wld{s}_{nm}")

    def emit_load(s, extra=()):
        yb = ybufs[s % 2]
        P.add("sync",
              lambda e, yb=yb, s=s: e.dma_start(
                  out=yb.rearrange("p (c t) -> p c t", c=NCH),
                  in_=dx.rearrange("p (c t) -> p c t", c=NCH)[:, :, s * SB:(s + 1) * SB]),
              writes=[ybufs_res[s % 2]], extra_deps=extra, tag=f"B_x{s}")

    def emit_proj(s):
        yb = ybufs[s % 2]
        yb_res = ybufs_res[s % 2]
        for oc in range(NCH):
            bp = take_bank()
            mm_group([(bp["f32"][:, 0:SB], WATb[:, oc * 128:(oc + 1) * 128],
                       vfm[:, s * SB:(s + 1) * SB])], bp,
                     reads=[res["WATb"], res["vfm"]], tag=f"B_proj{s}_{oc}")
            ve(lambda e, b=bp, oc=oc, yb=yb:
               e.tensor_add(yb[:, oc * SB:(oc + 1) * SB],
                            yb[:, oc * SB:(oc + 1) * SB], b["f32"][:, 0:SB]),
               reads=[bp["res"]], writes=[yb_res], tag=f"B_res{s}_{oc}")

    B_banks = {}

    def emit_stats_head(s):
        yb = ybufs[s % 2]
        yb_res = ybufs_res[s % 2]
        ae(lambda e, yb=yb: e.copy(yb16, yb),
           reads=[yb_res], writes=[res["yb16"]], tag=f"B_cast{s}")
        bnm = take_bank(pin=True)
        mm_group([(bnm["f32"][0:1, 0:SB], negidB, rch(yb16, c)) for c in range(NCH)],
                 bnm, reads=[res["yb16"], res["negidB"]], tag=f"B_negm{s}")
        ae(lambda e: e.activation(lntmp, yb16, AF.Square),
           reads=[res["yb16"]], writes=[res["lntmp"]], tag=f"B_sq{s}")
        bms = take_bank(pin=True)
        mm_group([(bms["f32"][0:1, 0:SB], invdB, rch(lntmp, c)) for c in range(NCH)],
                 bms, reads=[res["lntmp"], res["invdB"]], tag=f"B_msq{s}")
        B_banks[s] = (bnm, bms)

    def emit_stats_tail(s):
        yb = ybufs[s % 2]
        yb_res = ybufs_res[s % 2]
        bnm, bms = B_banks.pop(s)
        ae(lambda e, b=bnm: e.activation(st_a, b["f32"][0:1, 0:SB], AF.Square),
           reads=[bnm["res"]], writes=[res["st_a"]], tag=f"B_m2{s}")
        ve(lambda e, b=bms: e.tensor_sub(st_b, b["f32"][0:1, 0:SB], st_a),
           reads=[bms["res"], res["st_a"]], writes=[res["st_b"]])
        ve(lambda e: e.tensor_scalar_add(st_a, st_b, EPS_LN),
           reads=[res["st_b"]], writes=[res["st_a"]])
        ae(lambda e: e.activation(st_a, st_a, AF.Ln), writes=[res["st_a"]])
        ae(lambda e: e.activation(st_c, st_a, AF.Exp, scale=-0.5),
           reads=[res["st_a"]], writes=[res["st_c"]], tag=f"B_s{s}")
        ae(lambda e, b=bnm: e.copy(st_b, b["f32"][0:1, 0:SB]),
           reads=[bnm["res"]], writes=[res["st_b"]], tag=f"B_negm_sb{s}")
        unpin(bnm)
        unpin(bms)

    def emit_stats_bcast(s):
        yb = ybufs[s % 2]
        yb_res = ybufs_res[s % 2]
        bnb = take_bank()
        mm_group([(bnb["f32"][:, 0:SB], ones1F, st_b)], bnb,
                 reads=[res["st_b"], res["ones1F"]], tag=f"B_nmb{s}")
        bsb = take_bank()
        mm_group([(bsb["f32"][:, 0:SB], ones1F, st_c)], bsb,
                 reads=[res["st_c"], res["ones1F"]], tag=f"B_sbc{s}")
        ve(lambda e, b=bnb, yb=yb:
           e.tensor_add(lntmp.rearrange("p (c t) -> p c t", c=NCH),
                        yb.rearrange("p (c t) -> p c t", c=NCH),
                        b["f32"][:, 0:SB].unsqueeze(1).broadcast_to([128, NCH, SB]),
                        ),
           reads=[bnb["res"], yb_res], writes=[res["lntmp"]], tag=f"B_ln1_{s}")
        ve(lambda e, b=bsb:
           e.tensor_mul(yb16.rearrange("p (c t) -> p c t", c=NCH),
                        lntmp.rearrange("p (c t) -> p c t", c=NCH),
                        b["f32"][:, 0:SB].unsqueeze(1).broadcast_to([128, NCH, SB])),
           reads=[bsb["res"], res["lntmp"]], writes=[res["yb16"]], tag=f"B_ln2_{s}")

    def emit_qk(s):
        for (nm, dst, dkey) in (("WQT", Qfm, "Qfm"), ("WKT", Kfm, "Kfm")):
            wt, wres = wuse(s, nm)
            for oc in range(NCH):
                bq = take_bank()
                mm_group([(bq["f32"][:, 0:SB],
                           wt[:, c * D + oc * 128: c * D + (oc + 1) * 128],
                           rch(yb16, c)) for c in range(NCH)], bq,
                         reads=[wres, res["yb16"]], tag=f"B_{nm}{s}_{oc}")
                ae(lambda e, b=bq, dst=dst, oc=oc:
                   e.copy(dst[:, oc * SB:(oc + 1) * SB], b["f32"][:, 0:SB]),
                   reads=[bq["res"]], writes=[res[dkey]], tag=f"B_{nm}ev{s}_{oc}")

    def emit_h(s, bs, wname, lgsb, lgkey, finish_prev=None):
        wt, wres = wuse(s, wname)
        blg = take_bank(pin=True)

        def lg_mm(jc, hb, hbr):
            te(lambda e, b=blg, hb=hb, jc=jc:
               e.matmul(b["f32"][0:8, 0:SB], R2Tb[:, jc * 8:(jc + 1) * 8], hb,
                        start=(jc == 0), stop=(jc == NCH - 1)),
               reads=[hbr, res["R2Tb"]], writes=[blg["res"]], tag=f"B{bs}_lg{s}_{jc}")

        pend = None
        for jc in range(NCH):
            bh = take_bank()
            mm_group([(bh["f32"][:, 0:SB], wt[:, c * D + jc * 128: c * D + (jc + 1) * 128],
                       rch(Qfm, c)) for c in range(NCH)], bh,
                     reads=[wres, res["Qfm"]], tag=f"B{bs}_h{s}_{jc}")
            hb = hsb[jc % 2]
            hbr = hsb_res[jc % 2]
            ae(lambda e, b=bh, hb=hb, jc=jc:
               e.activation(hb, b["f32"][:, 0:SB], AF.Gelu, bias=r1bS[:, jc:jc + 1]),
               reads=[bh["res"], res["r1bS"]], writes=[hbr], tag=f"B{bs}_hev{s}_{jc}")
            if jc == 0 and finish_prev is not None:
                finish_prev()
            if pend is not None:
                lg_mm(*pend)
            pend = (jc, hb, hbr)

        def finish():
            lg_mm(*pend)
            ve(lambda e, b=blg, lgsb=lgsb:
               e.tensor_scalar_add(lgsb, b["f32"][0:8, 0:SB], bias2),
               reads=[blg["res"], res["bias2"]], writes=[res[lgkey]],
               tag=f"B{bs}_lgev{s}")
            unpin(blg)
        return finish

    def emit_smx_pre(s, bs, lgsb, lgkey, wts, wtsk):
        btA = take_bank(pin=True)
        for bi in range(SB // 128):
            te(lambda e, b=btA, bi=bi, lgsb=lgsb:
               e.transpose(b["b16"][:, bi * 8:(bi + 1) * 8],
                           lgsb[:, bi * 128:(bi + 1) * 128], identb[0:8, 0:8]),
               reads=[res[lgkey], res["identb"]], writes=[btA["res"]],
               tag=f"B{bs}_lt{s}_{bi}")
        ve(lambda e, b=btA:
           e.reduce_max(smxM, b["b16"][:, 0:32].rearrange("p (c e) -> p c e", e=8),
                        AX.X, negate=True),
           reads=[btA["res"]], writes=[res["smxM"]], tag=f"B{bs}_smax{s}")
        ve(lambda e, b=btA, wts=wts:
           e.tensor_add(wts.rearrange("p (c e) -> p c e", e=8),
                        b["b16"][:, 0:32].rearrange("p (c e) -> p c e", e=8),
                        smxM.unsqueeze(2).broadcast_to([128, 4, 8])),
           reads=[btA["res"], res["smxM"]], writes=[res[wtsk]],
           tag=f"B{bs}_ssub{s}")
        unpin(btA)

    def emit_smx_mid(s, bs, wts, wtsk):
        ae(lambda e, wts=wts: e.activation(wts, wts, AF.Exp, bias=0.0),
           writes=[res[wtsk]], tag=f"B{bs}_wexp{s}")
        for c in range(SB // 128):
            ve(lambda e, c=c, wts=wts: e.reduce_sum(sumM[:, c:c + 1],
                                                    wts[:, c * 8:(c + 1) * 8], AX.X),
               reads=[res[wtsk]], writes=[res["sumM"]], tag=f"B{bs}_ssum{s}_{c}")
        ve(lambda e: e.reciprocal(sumM, sumM), writes=[res["sumM"]],
           tag=f"B{bs}_srec{s}")
        ve(lambda e, s=s: e.tensor_mul(smxQ, nvinvS[:, s * 4:(s + 1) * 4], sumM),
           reads=[res["nvinvS"], res["sumM"]], writes=[res["smxQ"]],
           tag=f"B{bs}_snv{s}")
        ve(lambda e, wts=wts: e.tensor_mul(wts.rearrange("p (c e) -> p c e", e=8),
                                           wts.rearrange("p (c e) -> p c e", e=8),
                                           smxQ.unsqueeze(2).broadcast_to([128, 4, 8])),
           reads=[res["smxQ"]], writes=[res[wtsk]], tag=f"B{bs}_wts{s}")

    def emit_smx_back(s, bs, wfm, wfmk, wts, wtsk):
        btB = take_bank()
        for bi in range(SB // 128):
            te(lambda e, b=btB, bi=bi, wts=wts:
               e.transpose(b["b16"][0:8, bi * 128:(bi + 1) * 128],
                           wts[:, bi * 8:(bi + 1) * 8], identb),
               reads=[res[wtsk], res["identb"]], writes=[btB["res"]],
               tag=f"B{bs}_wt{s}_{bi}")
        ae(lambda e, b=btB, wfm=wfm: e.copy(wfm, b["b16"][0:8, 0:SB]),
           reads=[btB["res"]], writes=[res[wfmk]], tag=f"B{bs}_wfm{s}")

    def emit_ywxv(s, bs, Vp, Vpk, Wp, Wpk, carry, carryk, hook=None):
        P2, P2k = (P2F, "P2F") if bs == "F" else (P2I, "P2I")

        def wrep_p2(kc):
            bwr = take_bank()
            mm_group([(bwr["f32"][:, 0:SB], EselB[:, kc * 128:(kc + 1) * 128],
                       wfmF if bs == "F" else wfmI)],
                     bwr, reads=[res["EselB"], res["wfmF" if bs == "F" else "wfmI"]],
                     tag=f"B{bs}_wrep{s}_{kc}")
            ve(lambda e, b=bwr, kc=kc:
               e.tensor_mul(rch(P2, kc), rch(cumS, kc), b["f32"][:, 0:SB]),
               reads=[res["cumS"], bwr["res"]], writes=[res[P2k]],
               tag=f"B{bs}_p2{s}_{kc}")

        pend_kc = None
        for kc in range(NKRCH):
            byw = take_bank()
            mm_group([(byw["f32"][:, 0:SB],
                       Wp[:, c * KR + kc * 128: c * KR + (kc + 1) * 128],
                       rch(Kfm, c)) for c in range(NCH)], byw,
                     reads=[res[Wpk], res["Kfm"]], tag=f"B{bs}_yw{s}_{kc}")
            ve(lambda e, b=byw, kc=kc, carry=carry, s=s:
               e.tensor_tensor_scan(rch(cumS, kc), onesColF.broadcast_to([128, SB]),
                                    b["f32"][:, 0:SB],
                                    carry[:, kc:kc + 1] if s > 0 else 0.0,
                                    ALU.mult, ALU.add),
               reads=[byw["res"], res["onesColF"]] + ([res[carryk]] if s > 0 else []),
               writes=[res["cumS"]], tag=f"B{bs}_scan{s}_{kc}")
            ve(lambda e, kc=kc, carry=carry:
               e.tensor_copy(carry[:, kc:kc + 1], rch(cumS, kc)[:, SB - 1:SB]),
               reads=[res["cumS"]], writes=[res[carryk]], tag=f"B{bs}_carry{s}_{kc}")
            bxv = take_bank()
            mm_group([(bxv["f32"][:, 0:SB],
                       Vp[:, c * KR + kc * 128: c * KR + (kc + 1) * 128],
                       rch(Qfm, c)) for c in range(NCH)], bxv,
                     reads=[res[Vpk], res["Qfm"]], tag=f"B{bs}_xv{s}_{kc}")
            if kc == 0 and hook is not None:
                hook()
            ve(lambda e, b=bxv, kc=kc: e.tensor_tensor(rch(cumS, kc), rch(cumS, kc),
                                                       b["f32"][:, 0:SB], ALU.mult),
               reads=[bxv["res"]], writes=[res["cumS"]], tag=f"B{bs}_p1{s}_{kc}")
            if pend_kc is not None:
                wrep_p2(pend_kc)
            pend_kc = kc
        wrep_p2(pend_kc)

    def emit_out(s):
        wt, wres = wuse(s, "UFUI")
        ufo = 0
        uio = NKRCH * D
        for dc in range(NCH):
            bo = take_bank()
            ops = []
            out = bo["f32"][:, 0:SB]
            for kc in range(NKRCH):
                ops.append((out, wt[:, ufo + kc * D + dc * 128: ufo + kc * D + (dc + 1) * 128],
                            rch(P2F, kc)))
            ops.append((out, bfb[0:8, dc * 128:(dc + 1) * 128], wfmF))
            for kc in range(NKRCH):
                ops.append((out, wt[:, uio + kc * D + dc * 128: uio + kc * D + (dc + 1) * 128],
                            rch(P2I, kc)))
            ops.append((out, bib[0:8, dc * 128:(dc + 1) * 128], wfmI))
            mm_group(ops, bo,
                     reads=[wres, res["bfb"], res["bib"],
                            res["P2F"], res["P2I"], res["wfmF"], res["wfmI"]],
                     tag=f"B_out{s}_{dc}")
            ae(lambda e, b=bo, dc=dc: e.copy(ff16[:, dc * SB:(dc + 1) * SB],
                                             b["f32"][:, 0:SB]),
               reads=[bo["res"]], writes=[res["ff16"]], tag=f"B_outev{s}_{dc}")

    def emit_wo(s):
        yb = ybufs[s % 2]
        yb_res = ybufs_res[s % 2]
        wt, wres = wuse(s, "WOT")
        for oc in range(NCH):
            bw = take_bank()
            mm_group([(bw["f32"][:, 0:SB], wt[:, c * D + oc * 128: c * D + (oc + 1) * 128],
                       rch(ff16, c)) for c in range(NCH)], bw,
                     reads=[wres, res["ff16"]], tag=f"B_wo{s}_{oc}")
            ve(lambda e, b=bw, oc=oc, yb=yb:
               e.tensor_add(yb[:, oc * SB:(oc + 1) * SB],
                            yb[:, oc * SB:(oc + 1) * SB], b["f32"][:, 0:SB]),
               reads=[bw["res"]], writes=[yb_res], tag=f"B_wores{s}_{oc}")
        dma("sync", dout.rearrange("p (c t) -> p c t", c=NCH)[:, :, s * SB:(s + 1) * SB],
            yb.rearrange("p (c t) -> p c t", c=NCH),
            reads=[yb_res], writes=[out_res], tag=f"B_store{s}")

    emit_load(0, extra=fence0)
    wstream(0, "WQT")
    wstream(0, "WKT")
    wstream(0, "R1T")
    emit_proj(0)
    emit_stats_head(0)
    emit_stats_tail(0)
    emit_stats_bcast(0)
    for s in range(NSB):
        emit_qk(s)
        wstream(s, "R1TI")
        if s + 1 < NSB:
            emit_load(s + 1, extra=(fence1 if s + 1 == 1 else ()))
            emit_proj(s + 1)
        finF = emit_h(s, "F", "R1T", lgsbF, "lgsbF")
        wstream(s, "UFUI")
        wstream(s, "WOT")
        finI = emit_h(s, "I", "R1TI", lgsbI, "lgsbI",
                      finish_prev=lambda s=s, f=finF:
                      (f(), emit_smx_pre(s, "F", lgsbF, "lgsbF",
                                         wtsBF, "wtsBF")))
        emit_smx_mid(s, "F", wtsBF, "wtsBF")
        if s + 1 < NSB:
            wstream(s + 1, "WQT")
        emit_ywxv(s, "F", Vfb, "Vfb", Wfb, "Wfb", carryF, "carryF",
                  hook=lambda s=s, f=finI:
                  (f(),
                   emit_smx_pre(s, "I", lgsbI, "lgsbI", wtsBI, "wtsBI"),
                   emit_smx_mid(s, "I", wtsBI, "wtsBI"),
                   emit_smx_back(s, "F", wfmF, "wfmF", wtsBF, "wtsBF")))
        emit_ywxv(s, "I", Vixb, "Vixb", Wiyb, "Wiyb", carryI, "carryI",
                  hook=lambda s=s: emit_smx_back(s, "I", wfmI, "wfmI",
                                                 wtsBI, "wtsBI"))
        if s + 1 < NSB:
            emit_stats_head(s + 1)
            emit_stats_tail(s + 1)
        emit_out(s)
        if s + 1 < NSB:
            wstream(s + 1, "WKT")
            emit_stats_bcast(s + 1)
        emit_wo(s)
        if s + 1 < NSB:
            wstream(s + 1, "R1T")
        if s == 0:
            tap("y0", ybufs[0], ybufs_res[0])
            tap("Q0", Qfm, res["Qfm"])
            tap("cum0", cumS, res["cumS"])

    P.add("sync", lambda e: e.nop(), reads=[out_res], tag="final_wait")

    P.finalize()

    with (nc.semaphore("sem_t") as s_t, nc.semaphore("sem_v") as s_v,
          nc.semaphore("sem_a") as s_a, nc.semaphore("sem_s") as s_s,
          nc.semaphore("sem_g") as s_g, nc.semaphore("sem_d") as s_d,
          nc.Block() as block):
        sems = {"tensor": s_t, "vector": s_v, "scalar": s_a,
                "sync": s_s, "gpsimd": s_g, "sdma": s_d}

        @block.tensor
        def _(eng):
            P.emit_engine("tensor", eng, sems)

        @block.vector
        def _(eng):
            P.emit_engine("vector", eng, sems)

        @block.scalar
        def _(eng):
            P.emit_engine("scalar", eng, sems)

        @block.sync
        def _(eng):
            P.emit_engine("sync", eng, sems)

        @block.gpsimd
        def _(eng):
            P.emit_engine("gpsimd", eng, sems)

    return nc, tap_drams


def kernel(**inputs):
    xs, w = _host_prep(inputs)
    taps = tuple(_DEBUG_TAPS)
    nc, tap_drams = build_program(taps)
    in_maps = []
    for b in range(8):
        m = {"xT": xs[b]}
        m.update(w)
        in_maps.append(m)
    r = run_bass_kernel_spmd(nc, in_maps, core_ids=list(range(8)))
    outs = []
    for b in range(8):
        o = np.asarray(r.results[b]["outT"], dtype=np.float32)
        outs.append(o.reshape(128, NCH, T).transpose(1, 0, 2).reshape(D, T).T)
    if taps:
        kernel.last_taps = [{k: np.asarray(r.results[b]["tap_" + k]) for k in taps
                             if ("tap_" + k) in r.results[b]} for b in range(8)]
    return np.ascontiguousarray(np.stack(outs, axis=0))
